# version 12
# baseline (speedup 1.0000x reference)
import numpy as np
import ml_dtypes
import concourse.bass as bass
import concourse.mybir as mybir
from concourse.bass_utils import run_bass_kernel_spmd
from contextlib import ExitStack

F32 = mybir.dt.float32
BF16 = mybir.dt.bfloat16
AF = mybir.ActivationFunctionType
ALU = mybir.AluOpType
AX = mybir.AxisListType

T = 2048
C = 1024
H = 16
HID = 256
FF = 4096
SLAB = 512
EPS_LN = 1.0e-5
EPS_RMS = float(np.finfo(np.float32).eps)
KB = 1024

_STOP = None
_MSK = 3


def build_nc():
    nc = bass.Bass("TRN2", target_bir_lowering=False)

    def din(name, shape, dt=F32):
        return nc.dram_tensor(name, shape, dt, kind="ExternalInput").ap()

    xT = din("xT", [C, T])
    wqT = din("wqT", [C, C])
    wkT = din("wkT", [C, C])
    wvT = din("wvT", [C, C])
    wpjT = din("wpjT", [C, C])
    wm1T = din("wm1T", [C, HID])
    wm2T = din("wm2T", [HID, H])
    bm1c = din("bm1c", [128, 2])
    bm2c = din("bm2c", [16, 1])
    wfcT = din("wfcT", [C, FF])
    wcpT = din("wcpT", [FF, C])
    w1T = din("w1T", [C, FF])
    w2T = din("w2T", [FF, C])
    cwc = din("cwc", [128, 96])
    cbc = din("cbc", [128, 32])
    c_col = din("c_col", [128, 8])
    wmod = din("wmod", [C, 6 * C])

    out = nc.dram_tensor("out", [C, T], F32, kind="ExternalOutput").ap()
    w1q_scr = nc.dram_tensor("w1q_scr", [128, 8 * 4096], BF16, kind="Internal").ap()
    w2q_scr = nc.dram_tensor("w2q_scr", [128, 8 * 4096], BF16, kind="Internal").ap()
    wfc_scr = nc.dram_tensor("wfc_scr", [128, 32 * 1024], BF16, kind="Internal").ap()
    wcp_scr = nc.dram_tensor("wcp_scr", [128, 64 * 512], BF16, kind="Internal").ap()

    es = ExitStack()
    with es:
        R1 = 0
        R2 = 64 * KB
        R3 = 128 * KB
        W0 = R3 + 33 * KB
        ARENA = W0 + 32 * KB

        back = es.enter_context(nc.sbuf_tensor("ARENA", [128, ARENA], mybir.dt.uint8))
        base_addr = nc.lookup_mloc(back).addr

        def place(name, shape, dtype, off, base_partition=0):
            h = nc.virtual_sbuf_tensor(name, shape, dtype)
            if base_partition:
                h.base_partition = base_partition
            nc._tensor(h.name, shape, dtype, type="SB")
            mloc = nc.lookup_mloc(h)
            mloc.allocated = True
            mloc.addr = base_addr + off
            return h

        wq_sb = place("wq_sb", [128, 8 * C], BF16, R1 + 0)
        wk_sb = place("wk_sb", [128, 8 * C], BF16, R1 + 16 * KB)
        wv_sb = place("wv_sb", [128, 8 * C], BF16, R1 + 32 * KB)
        wm1_sb = place("wm1_sb", [128, 2048], BF16, R1 + 48 * KB)
        wm2_sb = place("wm2_sb", [128, 32], BF16, R1 + 52 * KB)
        ELp_sb = place("ELp_sb", [128, 8 * 512], BF16, R1 + 53 * KB)
        wpj_sb = place("wpj_sb", [128, 8 * C], BF16, R1 + 0)
        x2_sb = place("x2_sb", [128, 8 * T], F32, R1 + 0)

        wmod_sb = place("wmod_sb", [128, 2 * 6144], BF16, R2 + 0)
        x_sl = place("x_sl", [128, 8 * 512], F32, R2 + 0)
        h1_sb = place("h1_sb", [128, 8 * 512], BF16, R2 + 16 * KB)
        qf_sb = place("qf_sb", [128, 8 * 512], BF16, R2 + 24 * KB)
        xb_sb = place("xb_sb", [128, 8 * 512], BF16, R2 + 24 * KB)
        kf_sb = place("kf_sb", [128, 8 * 512], BF16, R2 + 32 * KB)
        xsq_sb = place("xsq_sb", [128, 8 * 512], BF16, R2 + 32 * KB)
        ktok_sb = place("ktok_sb", [128, 4 * C], BF16, R2 + 40 * KB)
        vtok_sb = place("vtok_sb", [128, 4 * C], BF16, R2 + 48 * KB)
        m1b_sb = place("m1b_sb", [128, 1024], BF16, R2 + 56 * KB)
        st2A = place("st2A", [128, 512], F32, R2 + 58 * KB)
        st2B = place("st2B", [128, 512], F32, R2 + 60 * KB)
        st2C = place("st2C", [128, 512], F32, R2 + 62 * KB)
        ob_sb = place("ob_sb", [128, 8 * T], BF16, R2 + 0)
        h2_sb = place("h2_sb", [128, 8 * T], F32, R2 + 0)
        silu_sb = place("silu_sb", [128, 32 * 1024], BF16, R2 + 0)

        y_sb = place("y_sb", [128, 8 * T], BF16, R3 + 0)
        relu2_sb = place("relu2_sb", [128, 32 * 512], BF16, R3 + 0)
        xq1_sb = place("xq1_sb", [128, 8 * 2050], BF16, R3 + 0)

        qfn_sb = place("qfn_sb", [64, 8 * 512], BF16, W0 + 0)
        NOH = place("NOH", [16, 16 * 128], F32, W0 + 8 * KB)
        PSel = place("PSel", [16, 8 * 128], F32, W0 + 16 * KB)
        e1x = place("e1x", [16, 512], F32, W0 + 20 * KB)
        sp1 = place("sp1", [16, 512], F32, W0 + 22 * KB)
        sp2 = place("sp2", [16, 512], F32, W0 + 24 * KB)
        P16 = place("P16", [16, 512], F32, W0 + 26 * KB)
        mw16 = place("mw16", [16, 512], F32, W0 + 28 * KB)
        RST16 = place("RST16", [16, 512], F32, W0 + 30 * KB)
        yn_sb = place("yn_sb", [128, 8 * 512], BF16, W0 + 0)
        wfc_t = place("wfc_t", [128, 2 * 1024], BF16, W0 + 8 * KB)
        x_sl3 = place("x_sl3", [128, 2 * 512], F32, W0 + 12 * KB)
        t1_sc = place("t1_sc", [128, 2 * 512], F32, W0 + 16 * KB)
        wcpt = place("wcpt", [128, 2 * 512], BF16, W0 + 20 * KB)
        st3A = place("st3A", [128, 512], F32, W0 + 22 * KB)
        st3B = place("st3B", [128, 512], F32, W0 + 24 * KB)
        st3C = place("st3C", [128, 512], F32, W0 + 26 * KB)
        st3D = place("st3D", [128, 512], F32, W0 + 28 * KB)
        xb4_sb = place("xb4_sb", [128, 8 * 512], BF16, W0 + 0)
        xsq4_sb = place("xsq4_sb", [128, 8 * 512], BF16, W0 + 8 * KB)
        wfT_sb = place("wfT_sb", [128, 512], F32, R2 + 58 * KB)
        wqt_sb = place("wqt_sb", [128, 2 * 512], BF16, R2 + 60 * KB)
        stW_sb = place("stW_sb", [128, 512], BF16, R2 + 62 * KB)
        pq4 = place("pq4", [128, 2048], BF16, W0 + 0)
        aq4 = place("aq4", [128, 2048], BF16, W0 + 4 * KB)
        rq4 = place("rq4", [128, 2048], BF16, W0 + 8 * KB)
        uq4 = place("uq4", [128, 2048], BF16, W0 + 12 * KB)
        OHT = place("OHT", [16, 16 * 128], F32, W0 + 10 * KB, 0)
        s1rep = place("s1rep", [128, T], F32, W0 + 18 * KB)
        sc1rep = place("sc1rep", [128, T], BF16, W0 + 26 * KB)
        outb = place("outb", [128, 2 * 512], F32, W0 + 0)
        u_sb = place("u_sb", [128, 2 * 1026], BF16, W0 + 4 * KB + 256)
        w1t_sb = place("w1t_sb", [128, 2 * 1024], BF16, W0 + 8 * KB + 512)
        dia_sb = place("dia_sb", [128, 2 * 384], BF16, W0 + 12 * KB + 768)
        run2q = place("run2q", [128, 2048], BF16, W0 + 19 * KB)
        run2f = place("run2f", [128, 1024], F32, W0 + 10 * KB)
        s2rep = place("s2rep", [128, 1024], BF16, W0 + 23 * KB)
        sc2rep = place("sc2rep", [128, 1024], BF16, W0 + 25 * KB)
        pq5 = place("pq5", [128, 1024], BF16, W0 + 10 * KB)
        aq5 = place("aq5", [128, 1024], BF16, W0 + 12 * KB)
        rq5 = place("rq5", [128, 1024], BF16, W0 + 14 * KB)
        uq5 = place("uq5", [128, 1024], BF16, W0 + 16 * KB)
        s2d = place("s2d", [128, 1024], BF16, W0 + 27 * KB)
        sc2d = place("sc2d", [128, 1024], BF16, W0 + 29 * KB)
        xq2_sc = place("xq2_sc", [128, 4 * 512], BF16, W0 + 19 * KB)
        w2t_sb = place("w2t_sb", [128, 2 * 1024], BF16, W0 + 4 * KB + 256)

        def sbt(name, shape, dtype):
            return es.enter_context(nc.sbuf_tensor(name, shape, dtype))

        consts = sbt("consts", [128, 8], F32)
        onesb = sbt("onesb", [128, 128], BF16)
        onesf = sbt("onesf", [1, 128], F32)
        diagf = sbt("diagf", [128, 128], F32)
        U01 = sbt("U01", [128, 128], F32)
        IdNeg = sbt("IdNeg", [128, 128], F32)
        statesF = sbt("statesF", [128, 8 * 64], F32)
        statesB = sbt("statesB", [128, 8 * 64], BF16)
        PPall = sbt("PPall", [128, 32], F32)
        modtab = sbt("modtab", [128, 48], F32)
        modfin = sbt("modfin", [128, 48], F32)
        modsc = sbt("modsc", [128, 2], F32)
        ccol_sb = sbt("ccol_sb", [128, 8], F32)
        csb = sbt("csb", [128, 8], BF16)
        bm1_sb = sbt("bm1_sb", [128, 2], F32)
        bm2_sb = sbt("bm2_sb", [16, 1], F32)
        nbm2_sb = sbt("nbm2_sb", [16, 1], F32)
        cwc_sb = sbt("cwc_sb", [128, 96], F32)
        cbc_sb = sbt("cbc_sb", [128, 32], F32)
        stD = sbt("stD", [128, 512], F32)
        mask_sb = sbt("mask_sb", [128, 512], F32)
        ms_sb = sbt("ms_sb", [128, 512], BF16)
        qp_sb = sbt("qp_sb", [128, 256], BF16)
        qpn_sb = sbt("qpn_sb", [64, 256], BF16)
        kt2_sb = sbt("kt2_sb", [128, 256], BF16)
        rmax1 = sbt("rmax1", [128, 16], F32)
        rmax2 = sbt("rmax2", [128, 8], F32)
        sv_a = sbt("sv_a", [128, 16], F32)
        sv_b = sbt("sv_b", [128, 16], F32)
        s1v = sbt("s1v", [128, 16], F32)
        sc1v = sbt("sc1v", [128, 16], F32)
        tposed = sbt("tposed", [16, 256], F32)
        acc_w = sbt("acc_w", [128, 1], F32)
        tmpc = sbt("tmpc", [128, 1], F32)
        onecf = sbt("onecf", [128, 1], F32)
        msc_sc = sbt("msc_sc", [128, 4], F32)
        thn_sc = sbt("thn_sc", [128, 2], F32)
        sc11 = sbt("sc11", [1, 2], F32)

        psA = es.enter_context(nc.psum_tensor("psA", [128, 512], F32))
        psB = es.enter_context(nc.psum_tensor("psB", [128, 512], F32))
        psY = es.enter_context(nc.psum_tensor("psY", [128, 512], F32))
        psS = es.enter_context(nc.psum_tensor("psS", [128, 512], F32))
        psG0 = es.enter_context(nc.psum_tensor("psG0", [128, 512], F32))
        psG1 = es.enter_context(nc.psum_tensor("psG1", [128, 512], F32))
        psC = es.enter_context(nc.psum_tensor("psC", [128, 512], F32))
        psD = es.enter_context(nc.psum_tensor("psD", [128, 512], F32))
        psG = [psG0, psG1]
        psBANKS = [psA, psB, psY, psS, psG0, psG1, psC, psD]

        sem_names = ["hS", "hX", "h3", "hW0", "hW1", "oW0", "oW1",
                     "h50", "h51", "oO0", "oO1", "gW", "gM0", "gM1", "gF0", "gF1",
                     "gP0", "gP1", "cv", "t", "v", "s", "g"]
        SEM = {n: es.enter_context(nc.semaphore("sem_" + n)) for n in sem_names}
        CNT = {n: 0 for n in sem_names}
        WAITED = {}

        block = es.enter_context(nc.Block())

        def wt(eng, ename, *keys):
            for k in keys:
                tgt = CNT[k]
                if tgt and WAITED.get((ename, k), 0) < tgt:
                    eng.wait_ge(SEM[k], tgt)
                    WAITED[(ename, k)] = tgt

        def wtv(eng, ename, k, tgt):
            if tgt and WAITED.get((ename, k), 0) < tgt:
                eng.wait_ge(SEM[k], tgt)
                WAITED[(ename, k)] = tgt

        def bump(instr, key, n=1):
            instr.then_inc(SEM[key], n)
            CNT[key] += n

        stopped = [False]

        def sec(ename, fn):
            if stopped[0]:
                return
            getattr(block, ename)(fn)

        def vTT(v, o, a, b, op, inc=None):
            i = v.tensor_tensor(o, a, b, op)
            if inc: bump(i, inc)
            v.drain()

        def vTS(v, o, a, s1, s2, op0, op1=None, inc=None):
            if op1 is None:
                i = v.tensor_scalar(o, a, s1, None, op0)
            else:
                i = v.tensor_scalar(o, a, s1, s2, op0, op1)
            if inc: bump(i, inc)
            v.drain()

        def vSTT(v, o, a, s, b, op0, op1, inc=None):
            i = v.scalar_tensor_tensor(o, a, s, b, op0, op1)
            if inc: bump(i, inc)
            v.drain()

        def vCP(v, o, a, inc=None):
            i = v.tensor_copy(o, a)
            if inc: bump(i, inc)
            v.drain()

        def vMS(v, o, val, inc=None):
            i = v.memset(o, val)
            if inc: bump(i, inc)
            v.drain()

        def vRED(v, o, a, op, absval=None, inc=None):
            i = v.tensor_reduce(o, a, axis=AX.X, op=op, apply_absolute_value=absval)
            if inc: bump(i, inc)
            v.drain()

        def vRCP(v, o, a, inc=None):
            i = v.reciprocal(o, a)
            if inc: bump(i, inc)
            v.drain()

        def vSCAN(v, o, d0, d1, init, op0, op1, inc=None):
            i = v.tensor_tensor_scan(o, d0, d1, init, op0, op1)
            if inc: bump(i, inc)
            v.drain()

        def sACT(s, o, a, func, bias, scale, inc=None):
            i = s.activation(o, a, func, bias=bias, scale=scale)
            if inc: bump(i, inc)
            s.drain()

        def pTS(g, o, a, s1, s2, op0, op1=None, inc=None):
            if op1 is None:
                i = g.tensor_scalar(o, a, s1, None, op0)
            else:
                i = g.tensor_scalar(o, a, s1, s2, op0, op1)
            if inc: bump(i, inc)
            g.drain()

        def ladder_round(v, pq, aq, rq, uq, src_a, src_b, dest, scAP, inc=None):
            vTT(v, pq[:], src_a, src_b, ALU.mult)
            vTS(v, rq[:], pq[:], -1.0, None, ALU.mult)
            vTT(v, aq[:], rq[:], pq[:], ALU.max)
            first = True
            for kbit in (64, 32, 16, 8, 4, 2, 1):
                src = aq if first else rq
                vTS(v, uq[:], src[:], kbit - 0.5, float(-kbit), ALU.is_ge, ALU.mult)
                vTT(v, rq[:], src[:], uq[:], ALU.add)
                first = False
            vTT(v, aq[:], aq[:], rq[:], ALU.subtract)
            vTS(v, uq[:], pq[:], 0.0, 2.0, ALU.is_ge, ALU.mult)
            vSTT(v, aq[:], uq[:], -1.0, aq[:], ALU.add, ALU.mult)
            vTT(v, dest, aq[:], scAP, ALU.mult, inc=inc)

        def ccol(idx):
            return consts[:, idx:idx + 1]
        C_ONE, C_EPSLN, C_EPSRMS, C_MHALF, C_MONE, C_INV1024, C_ZERO = 0, 1, 2, 3, 4, 5, 6

        def c16(idx):
            return consts[0:16, idx:idx + 1]

        def g_p0(g):
            g.iota(U01[:], [[-1, 128]], base=0, channel_multiplier=1,
                   allow_small_or_imprecise_dtypes=True)
            g.iota(NOH[:], [[-1, 16], [0, 128]], base=0, channel_multiplier=1,
                   allow_small_or_imprecise_dtypes=True)
            i = g.iota(PSel[:], [[-1, 16], [0, 64]], base=0, channel_multiplier=1,
                       allow_small_or_imprecise_dtypes=True)
            bump(i, "g")
            for (dst, src) in [(wq_sb, wqT), (wk_sb, wkT), (wv_sb, wvT)]:
                for hh in range(2):
                    i = g.dma_start(
                        out=dst[:, hh * 4096:(hh + 1) * 4096],
                        in_=src[hh * 512:(hh + 1) * 512, :].rearrange(
                            "(i p) o -> p i o", p=128))
                    bump(i, "gW", 16)
            i = g.dma_start(out=wm1_sb[:], in_=wm1T.rearrange("(i p) o -> p i o", p=128))
            bump(i, "gW", 16)
            i = g.dma_start(out=wm2_sb[:], in_=wm2T.rearrange("(i p) o -> p i o", p=128))
            bump(i, "gW", 16)
            i = g.dma_start(out=wmod_sb[:, 0:6144], in_=wmod[0:128, :])
            bump(i, "gM0", 16)
        sec("gpsimd", g_p0)

        def hw_p0(sy):
            for (dst, src) in [(ccol_sb, c_col), (bm1_sb, bm1c), (bm2_sb, bm2c),
                               (cwc_sb, cwc), (cbc_sb, cbc)]:
                i = sy.dma_start(out=dst[:], in_=src)
                bump(i, "hS", 16)
        sec("sync", hw_p0)

        def v_p0(v):
            vMS(v, consts[:, 0:1], 1.0)
            vMS(v, consts[:, 1:2], EPS_LN)
            vMS(v, consts[:, 7:8], 3.0)
            vRCP(v, consts[:, 2:3], consts[:, 7:8])
            vTS(v, consts[:, 2:3], consts[:, 2:3], 4.0, -1.0, ALU.mult, ALU.add)
            vTS(v, consts[:, 2:3], consts[:, 2:3], 3.0, -1.0, ALU.mult, ALU.add)
            vTS(v, consts[:, 7:8], consts[:, 2:3], -1.0, None, ALU.mult)
            vTT(v, consts[:, 2:3], consts[:, 2:3], consts[:, 7:8], ALU.max)
            vMS(v, consts[:, 3:4], -0.5)
            vMS(v, consts[:, 4:5], -1.0)
            vMS(v, consts[:, 5:6], 1.0 / 1024.0)
            vMS(v, consts[:, 6:7], 0.0)
            vMS(v, onesb[:], 1.0)
            vMS(v, onesf[:], 1.0)
            vMS(v, onecf[:], 1.0)
            vMS(v, statesF[:], 0.0)
            vMS(v, statesB[:], 0.0)
            vMS(v, RST16[:], 1.0)
            for cst in range(0, 512, 128):
                vMS(v, RST16[:, cst:cst + 1], 0.0)
            wt(v, "vector", "g", "hS")
            vTS(v, NOH[:], NOH[:], 0.0, -1.0, ALU.is_equal, ALU.mult)
            vTS(v, PSel[:], PSel[:], 0.0, -1.0, ALU.is_equal, ALU.mult)
            vTS(v, nbm2_sb[:], bm2_sb[:], -1.0, None, ALU.mult)
            vTS(v, diagf[:], U01[:], 0.0, None, ALU.is_equal)
            vTS(v, U01[:], U01[:], 0.0, None, ALU.is_gt)
            vTS(v, IdNeg[:], diagf[:], -1.0e9, None, ALU.mult, inc="v")
        sec("vector", v_p0)
        p0_v = CNT["v"]

        def s_p0(s):
            wt(s, "scalar", "hS")
            wtv(s, "scalar", "v", p0_v)
            sACT(s, csb[:], ccol_sb[:], AF.Silu, bias=ccol(C_ZERO), scale=ccol(C_ONE), inc="s")
        sec("scalar", s_p0)

        for ic in range(8):
            par = ic % 2
            if ic + 1 < 8:
                def g_ld(g, ic=ic):
                    par2 = (ic + 1) % 2
                    if ic + 1 >= 2:
                        wt(g, "gpsimd", "t")
                    i = g.dma_start(out=wmod_sb[:, par2 * 6144:(par2 + 1) * 6144],
                                    in_=wmod[(ic + 1) * 128:(ic + 2) * 128, :])
                    bump(i, "gM0" if par2 == 0 else "gM1", 16)
                sec("gpsimd", g_ld)

            def t_mod(t, ic=ic, par=par):
                wt(t, "tensor", "gM0" if par == 0 else "gM1", "s")
                last = None
                for jc in range(48):
                    last = t.matmul(
                        psC[:, jc * 8 + ic: jc * 8 + ic + 1],
                        wmod_sb[:, par * 6144 + jc * 128: par * 6144 + (jc + 1) * 128],
                        csb[:, ic:ic + 1], start=True, stop=True)
                bump(last, "t")
            sec("tensor", t_mod)

        def v_mod(v):
            wt(v, "vector", "t")
            i = v.tensor_reduce(modtab[:],
                                psC[:, 0:384].rearrange("p (j i) -> p j i", i=8),
                                axis=AX.X, op=ALU.add)
            v.drain()
            vCP(v, modfin[:, 0:8], modtab[:, 0:8])
            vTS(v, modfin[:, 8:16], modtab[:, 8:16], 1.0, None, ALU.add)
            vCP(v, modfin[:, 24:32], modtab[:, 24:32])
            vTS(v, modfin[:, 32:40], modtab[:, 32:40], 1.0, None, ALU.add)
            for g0 in (16, 40):
                for kk in range(8):
                    vTS(v, modsc[:, 0:1], modtab[:, g0 + kk:g0 + kk + 1], 0.0, None, ALU.is_ge)
                    vTS(v, modsc[:, 0:1], modsc[:, 0:1], 2.0, -1.0, ALU.mult, ALU.add)
                    vTS(v, modsc[:, 1:2], modtab[:, g0 + kk:g0 + kk + 1], -1.0, None, ALU.mult)
                    vTT(v, modsc[:, 1:2], modsc[:, 1:2], modtab[:, g0 + kk:g0 + kk + 1], ALU.max)
                    vTS(v, modsc[:, 1:2], modsc[:, 1:2], 0.25, None, ALU.max)
                    vTT(v, modfin[:, g0 + kk:g0 + kk + 1], modsc[:, 0:1], modsc[:, 1:2], ALU.mult)
            i = v.drain()
            bump(i, "v")
        sec("vector", v_mod)
        modfin_v = CNT["v"]
        mod_t = CNT["t"]

        def col_sa(cc): return modfin[:, 0 + cc:1 + cc]
        def col_ca1(cc): return modfin[:, 8 + cc:9 + cc]
        def col_ga(cc): return modfin[:, 16 + cc:17 + cc]
        def col_sb(cc): return modfin[:, 24 + cc:25 + cc]
        def col_cb1(cc): return modfin[:, 32 + cc:33 + cc]
        def col_gb(cc): return modfin[:, 40 + cc:41 + cc]

        def stage(name):
            if _STOP == name:
                def hw_stop(sy):
                    for k in ("t", "v", "s", "g"):
                        wt(sy, "sync", k)
                    i = sy.dma_start(out=out[0:128, 0:8], in_=consts[:, 0:8])
                    bump(i, "oO0", 16)
                    sy.wait_ge(SEM["oO0"], CNT["oO0"])
                sec("sync", hw_stop)
                stopped[0] = True

        stage("p0")

        def g_conv(g):
            wfi = wfcT.rearrange("(i p) (c f) -> p c i f", p=128, f=128)
            wci = wcpT.rearrange("(c p) (h x) -> p c h x", p=128, x=512)
            for c in range(32):
                i = g.dma_start(out=wfc_scr[:, c * 1024:(c + 1) * 1024]
                                .rearrange("p (i f) -> p i f", f=128),
                                in_=wfi[:, c])
                bump(i, "cv", 16)
                i = g.dma_start(out=wcp_scr[:, c * 1024:(c + 1) * 1024]
                                .rearrange("p (h x) -> p h x", x=512),
                                in_=wci[:, c])
                bump(i, "cv", 16)
        sec("gpsimd", g_conv)
        CV_TOT = CNT["cv"]

        glob = 0
        NIT = 16 * 8
        SN_mask = [0] * NIT
        SN_exp = [0] * NIT
        SN_ms = [0] * NIT
        SN_grp = [0] * NIT
        SN_new = [0] * NIT
        SN_yev = [0] * NIT

        w1_tiles = [("w1", a, b) for a in range(8) for b in range(8)]
        w2_tiles = [("w2", a, b) for a in range(32) for b in range(2)]

        def wtile_src(kind, a, b):
            if kind == "w1":
                return w1T[a * 128:(a + 1) * 128, b * 512:(b + 1) * 512]
            return w2T[a * 128:(a + 1) * 128, b * 512:(b + 1) * 512]

        def wtile_dst(kind, a, b):
            if kind == "w1":
                rr = w1q_scr.rearrange("p (fc i f) -> p fc i f", i=8, f=128)
                return rr[:, b * 4:(b + 1) * 4, a, :]
            return w2q_scr[:, a * 1024 + b * 512: a * 1024 + (b + 1) * 512]

        p4b_sched = [[] for _ in range(NIT)]
        mth_cell = [0, 0]
        for wi, tiles in enumerate((w1_tiles, w2_tiles)):
            g0 = wi * 64

            def v_acc0(v):
                vMS(v, acc_w[:], 0.0, inc="v")
            p4b_sched[g0].append(("vector", v_acc0))

            for ti, (kk, a, b) in enumerate(tiles):
                g = g0 + ti // 2

                def hw_w(sy, kk=kk, a=a, b=b):
                    wt(sy, "sync", "v")
                    i = sy.dma_start(out=wfT_sb[:], in_=wtile_src(kk, a, b))
                    bump(i, "hW0", 16)
                p4b_sched[g].append(("sync", hw_w))

                def v_abs(v):
                    wt(v, "vector", "hW0")
                    vRED(v, tmpc[:], wfT_sb[:], ALU.add, absval=True)
                    vTT(v, acc_w[:], acc_w[:], tmpc[:], ALU.add, inc="v")
                p4b_sched[g].append(("vector", v_abs))

            gm = g0 + 32

            def t_tot(t):
                wt(t, "tensor", "v")
                i = t.matmul(psD[0:1, 0:1], acc_w[:], onecf[:], start=True, stop=True)
                bump(i, "t")
            p4b_sched[gm].append(("tensor", t_tot))

            def v_mth(v):
                wt(v, "vector", "t")
                vTS(v, sc11[0:1, 0:1], psD[0:1, 0:1], 1.0 / float(FF * C), None, ALU.mult)
                vTS(v, sc11[0:1, 1:2], sc11[0:1, 0:1], 1.0e-5, None, ALU.max)
                vTS(v, sc11[0:1, 0:1], sc11[0:1, 1:2], 0.5, None, ALU.mult, inc="v")
            p4b_sched[gm].append(("vector", v_mth))

            def t_rep(t):
                wt(t, "tensor", "v")
                i = t.matmul(psD[:, 0:2], onesf[:], sc11[0:1, 0:2], start=True, stop=True)
                bump(i, "t")
            p4b_sched[gm].append(("tensor", t_rep))

            def v_rep(v, wi=wi):
                wt(v, "vector", "t")
                vCP(v, msc_sc[:, wi * 2:(wi + 1) * 2], psD[:, 0:2])
                vTS(v, thn_sc[:, wi:wi + 1], msc_sc[:, wi * 2:wi * 2 + 1], -1.0, None,
                    ALU.mult, inc="v")
                mth_cell[wi] = CNT["v"]
            p4b_sched[gm].append(("vector", v_rep))

            for ti, (kk, a, b) in enumerate(tiles):
                g = g0 + 32 + ti // 2
                par = ti % 2

                def hw_w2(sy, kk=kk, a=a, b=b):
                    wt(sy, "sync", "v")
                    i = sy.dma_start(out=wfT_sb[:], in_=wtile_src(kk, a, b))
                    bump(i, "hW0", 16)
                p4b_sched[g].append(("sync", hw_w2))

                def v_tern(v, par=par, wi=wi):
                    wt(v, "vector", "hW0")
                    wtv(v, "vector", "v", mth_cell[wi])
                    wt(v, "vector", "oW0" if par == 0 else "oW1")
                    vTS(v, wqt_sb[:, par * 512:(par + 1) * 512], wfT_sb[:],
                        msc_sc[:, wi * 2:wi * 2 + 1], None, ALU.is_ge)
                    vTS(v, stW_sb[:], wfT_sb[:], thn_sc[:, wi:wi + 1], None, ALU.is_le)
                    vTT(v, wqt_sb[:, par * 512:(par + 1) * 512],
                        wqt_sb[:, par * 512:(par + 1) * 512], stW_sb[:],
                        ALU.subtract, inc="v")
                p4b_sched[g].append(("vector", v_tern))

                def hw_st(sy, kk=kk, a=a, b=b, par=par):
                    wt(sy, "sync", "v")
                    i = sy.dma_start(out=wtile_dst(kk, a, b),
                                     in_=wqt_sb[:, par * 512:(par + 1) * 512])
                    bump(i, "oW0" if par == 0 else "oW1", 16)
                p4b_sched[g].append(("sync", hw_st))

        for s_i in range(4):
            t0 = s_i * SLAB

            def hw_xs(sy, s_i=s_i, t0=t0):
                wtv(sy, "sync", "t", mod_t)
                if s_i > 0:
                    wt(sy, "sync", "v")
                for cc in range(8):
                    i = sy.dma_start(out=x_sl[:, cc * 512:(cc + 1) * 512],
                                     in_=xT[cc * 128:(cc + 1) * 128, t0:t0 + 512])
                    bump(i, "hX", 16)
            sec("sync", hw_xs)

            def v_cast(v, s_i=s_i):
                wt(v, "vector", "hX")
                if s_i > 0:
                    wt(v, "vector", "t")
                for cc in range(8):
                    vCP(v, xb_sb[:, cc * 512:(cc + 1) * 512], x_sl[:, cc * 512:(cc + 1) * 512])
                for cc in range(8):
                    vTT(v, xsq_sb[:, cc * 512:(cc + 1) * 512],
                        xb_sb[:, cc * 512:(cc + 1) * 512],
                        xb_sb[:, cc * 512:(cc + 1) * 512], ALU.mult)
                i = v.drain()
                bump(i, "v")
            sec("vector", v_cast)

            def t_stats(t):
                wt(t, "tensor", "v")
                for cc in range(8):
                    t.matmul(psC[:, 0:512], onesb[:], xb_sb[:, cc * 512:(cc + 1) * 512],
                             start=(cc == 0), stop=(cc == 7))
                i = None
                for cc in range(8):
                    i = t.matmul(psD[:, 0:512], onesb[:], xsq_sb[:, cc * 512:(cc + 1) * 512],
                                 start=(cc == 0), stop=(cc == 7))
                bump(i, "t")
            sec("tensor", t_stats)

            def v_stat2(v):
                wt(v, "vector", "t")
                wt(v, "vector", "oW0", "oW1")
                vTS(v, st2A[:], psC[:, 0:512], 1.0 / 1024.0, None, ALU.mult)
                vTS(v, st2B[:], psD[:, 0:512], 1.0 / 1024.0, None, ALU.mult)
                vTT(v, st2C[:], st2A[:], st2A[:], ALU.mult)
                vSTT(v, stD[:], st2C[:], -1.0, st2B[:], ALU.mult, ALU.add, inc="v")
            sec("vector", v_stat2)

            def s_stat(s):
                wt(s, "scalar", "v")
                sACT(s, st2C[:], stD[:], AF.Ln, bias=ccol(C_EPSLN), scale=ccol(C_ONE))
                sACT(s, st2B[:], st2C[:], AF.Exp, bias=ccol(C_ZERO), scale=ccol(C_MHALF), inc="s")
            sec("scalar", s_stat)

            def v_h1(v):
                wt(v, "vector", "s")
                wtv(v, "vector", "v", modfin_v)
                for cc in range(8):
                    sl = slice(cc * 512, (cc + 1) * 512)
                    vTT(v, stD[:], x_sl[:, sl], st2A[:], ALU.subtract)
                    vTT(v, st2C[:], stD[:], st2B[:], ALU.mult)
                    vTS(v, h1_sb[:, sl], st2C[:], col_ca1(cc), col_sa(cc), ALU.mult, ALU.add)
                i = v.drain()
                bump(i, "v")
            sec("vector", v_h1)
            h1_v = CNT["v"]
            if s_i == 0:
                stage("p2_h1")

            qkv_specs = ([("qf", hp) for hp in range(8)]
                         + [("kf", hp) for hp in range(8)]
                         + [("kt", tt * 2 + oh) for tt in range(4) for oh in range(2)]
                         + [("vt", tt * 2 + oh) for tt in range(4) for oh in range(2)]
                         + [("m1", hb) for hb in range(2)])

            for si, (kind, idx) in enumerate(qkv_specs):
                par = si % 2

                def t_qkv(t, kind=kind, idx=idx, par=par, si=si, h1_v=h1_v):
                    wtv(t, "tensor", "v", h1_v)
                    wt(t, "tensor", "gW")
                    if si >= 2:
                        wt(t, "tensor", "v")
                        wt(t, "tensor", "s")
                    i = None
                    if kind == "qf":
                        for ic in range(8):
                            i = t.matmul(psG[par][:, 0:512],
                                         wq_sb[:, ic * 1024 + idx * 128: ic * 1024 + (idx + 1) * 128],
                                         h1_sb[:, ic * 512:(ic + 1) * 512],
                                         start=(ic == 0), stop=(ic == 7))
                    elif kind == "kf":
                        for ic in range(8):
                            i = t.matmul(psG[par][:, 0:512],
                                         wk_sb[:, ic * 1024 + idx * 128: ic * 1024 + (idx + 1) * 128],
                                         h1_sb[:, ic * 512:(ic + 1) * 512],
                                         start=(ic == 0), stop=(ic == 7))
                    elif kind in ("kt", "vt"):
                        tt, oh = idx // 2, idx % 2
                        wsb = wk_sb if kind == "kt" else wv_sb
                        for ic in range(8):
                            i = t.matmul(psG[par][:, 0:512],
                                         h1_sb[:, ic * 512 + tt * 128: ic * 512 + (tt + 1) * 128],
                                         wsb[:, ic * 1024 + oh * 512: ic * 1024 + (oh + 1) * 512],
                                         start=(ic == 0), stop=(ic == 7))
                    else:
                        hb = idx
                        for ic in range(8):
                            i = t.matmul(psG[par][:, 0:512],
                                         wm1_sb[:, ic * 256 + hb * 128: ic * 256 + (hb + 1) * 128],
                                         h1_sb[:, ic * 512:(ic + 1) * 512],
                                         start=(ic == 0), stop=(ic == 7))
                    bump(i, "t")
                sec("tensor", t_qkv)

                if kind == "m1":
                    def s_m1(s, idx=idx, par=par):
                        wt(s, "scalar", "t")
                        sACT(s, m1b_sb[:, idx * 512:(idx + 1) * 512], psG[par][:, 0:512],
                             AF.Relu, bias=bm1_sb[:, idx:idx + 1], scale=ccol(C_ONE), inc="s")
                    sec("scalar", s_m1)
                else:
                    def v_ev(v, kind=kind, idx=idx, par=par):
                        wt(v, "vector", "t")
                        if kind == "qf":
                            dst = qf_sb[:, idx * 512:(idx + 1) * 512]
                        elif kind == "kf":
                            dst = kf_sb[:, idx * 512:(idx + 1) * 512]
                        else:
                            tt, oh = idx // 2, idx % 2
                            buf = ktok_sb if kind == "kt" else vtok_sb
                            dst = buf[:, tt * 1024 + oh * 512: tt * 1024 + (oh + 1) * 512]
                        vCP(v, dst, psG[par][:, 0:512])
                        if kind == "qf":
                            vTS(v, qfn_sb[:, idx * 512:(idx + 1) * 512],
                                psG[par][0:64, 0:512], -1.0, None, ALU.mult)
                        i = v.drain()
                        bump(i, "v")
                    sec("vector", v_ev)

            if s_i == 0:
                stage("p2_qkv")

            def t_z(t):
                wt(t, "tensor", "s")
                i = None
                for hb in range(2):
                    i = t.matmul(psC[0:16, 0:512],
                                 wm2_sb[:, hb * 16:(hb + 1) * 16],
                                 m1b_sb[:, hb * 512:(hb + 1) * 512],
                                 start=(hb == 0), stop=(hb == 1))
                bump(i, "t")
            sec("tensor", t_z)

            def s_z(s):
                wt(s, "scalar", "t", "v")
                sACT(s, e1x[:], psC[0:16, 0:512], AF.Exp, bias=bm2_sb[:], scale=c16(C_ONE))
                sACT(s, sp1[:], e1x[:], AF.Ln, bias=c16(C_ONE), scale=c16(C_ONE))
                sACT(s, e1x[:], psC[0:16, 0:512], AF.Exp, bias=nbm2_sb[:],
                     scale=consts[0:16, 4:5])
                sACT(s, sp2[:], e1x[:], AF.Ln, bias=c16(C_ONE), scale=c16(C_ONE), inc="s")
            sec("scalar", s_z)

            def v_dec(v, s_i=s_i):
                wt(v, "vector", "s")
                if s_i > 0:
                    wtv(v, "vector", "t", SN_mask[s_i * 32 - 1])
                vSCAN(v, P16[:], RST16[:], sp1[:], 0.0, ALU.mult, ALU.add)
                vTT(v, mw16[:], sp2[:], P16[:], ALU.subtract, inc="v")
            sec("vector", v_dec)
            dec_v = CNT["v"]
            if s_i == 0:
                stage("p2_z")

            for hp in range(8):
                par = hp % 2

                def t_el(t, hp=hp, par=par, dec_v=dec_v):
                    wtv(t, "tensor", "v", dec_v)
                    if hp >= 2:
                        wt(t, "tensor", "s")
                    i = t.matmul(psG[par][:, 0:512], PSel[:, hp * 128:(hp + 1) * 128],
                                 P16[:], start=True, stop=True)
                    bump(i, "t")
                sec("tensor", t_el)

                def s_el(s, hp=hp, par=par):
                    wt(s, "scalar", "t", "v")
                    sACT(s, ELp_sb[:, hp * 512:(hp + 1) * 512], psG[par][:, 0:512],
                         AF.Exp, bias=ccol(C_ZERO), scale=ccol(C_ONE))
                    sACT(s, PPall[:, hp * 4:(hp + 1) * 4], psG[par][:, 127:512:128],
                         AF.Exp, bias=ccol(C_ZERO), scale=ccol(C_ONE), inc="s")
                sec("scalar", s_el)
            elp_s = CNT["s"]
            if s_i == 0:
                stage("p2_el")

            for cin in range(4):
                gc = s_i * 4 + cin
                for hp in range(8):
                    i_g = glob
                    par = i_g % 2
                    pc = par * 256

                    def t_msk(t, i_g=i_g, hp=hp, cin=cin, pc=pc, dec_v=dec_v):
                        wtv(t, "tensor", "v", dec_v)
                        if i_g >= 2:
                            wtv(t, "tensor", "s", SN_exp[i_g - 2])
                            wtv(t, "tensor", "v", SN_ms[i_g - 2])
                        cs = slice(cin * 128, (cin + 1) * 128)
                        last = None
                        for h in range(2):
                            hm = 2 * hp + h
                            last = t.matmul(psA[:, pc + h * 128: pc + (h + 1) * 128],
                                     mw16[:, cs], NOH[:, hm * 128:(hm + 1) * 128],
                                     start=True, stop=(_MSK == 1))
                            if _MSK >= 2:
                                t.matmul(psA[:, pc + h * 128: pc + (h + 1) * 128],
                                         NOH[:, hm * 128:(hm + 1) * 128], P16[:, cs],
                                         start=False, stop=False)
                                last = t.matmul(psA[:, pc + h * 128: pc + (h + 1) * 128],
                                         IdNeg[:], U01[:], start=False, stop=True)
                        if _MSK >= 3:
                            sl = slice(hp * 512 + cin * 128, hp * 512 + (cin + 1) * 128)
                            t.matmul(psB[:, pc: pc + 128],
                                     kf_sb[0:64, sl], qf_sb[0:64, sl],
                                     start=True, stop=True)
                            t.matmul(psB[:, pc + 128: pc + 256],
                                     kf_sb[:, sl], qf_sb[:, sl],
                                     start=True, stop=False)
                            last = t.matmul(psB[:, pc + 128: pc + 256],
                                            kf_sb[0:64, sl], qfn_sb[0:64, sl],
                                            start=False, stop=True)
                        bump(last, "t")
                        SN_mask[i_g] = CNT["t"]
                    sec("tensor", t_msk)
                    if i_g == 0:
                        stage("c0_msk")

                    def s_exp(s, i_g=i_g, pc=pc):
                        wtv(s, "scalar", "t", SN_mask[i_g])
                        if i_g >= 2:
                            wtv(s, "scalar", "v", SN_ms[i_g - 2])
                        sACT(s, mask_sb[:, pc:pc + 256], psA[:, pc:pc + 256],
                             AF.Exp, bias=ccol(C_ZERO), scale=ccol(C_ONE), inc="s")
                        SN_exp[i_g] = CNT["s"]
                    sec("scalar", s_exp)
                    if i_g == 0:
                        stage("c0_exp")

                    def v_msqk(v, i_g=i_g, hp=hp, cin=cin, par=par, pc=pc, elp_s=elp_s):
                        wtv(v, "vector", "s", SN_exp[i_g])
                        wtv(v, "vector", "s", elp_s)
                        if i_g >= 2:
                            wtv(v, "vector", "t", SN_grp[i_g - 2])
                        for h in range(2):
                            vTT(v, ms_sb[:, pc + h * 128: pc + (h + 1) * 128],
                                psB[:, pc + h * 128: pc + (h + 1) * 128],
                                mask_sb[:, pc + h * 128: pc + (h + 1) * 128], ALU.mult)
                        sl = slice(hp * 512 + cin * 128, hp * 512 + (cin + 1) * 128)
                        vTT(v, qp_sb[:, par * 128:(par + 1) * 128],
                            qf_sb[:, sl], ELp_sb[:, sl], ALU.mult)
                        vTT(v, qpn_sb[:, par * 128:(par + 1) * 128],
                            qfn_sb[0:64, sl], ELp_sb[0:64, sl], ALU.mult)
                        for h in range(2):
                            hm = 2 * hp + h
                            vTS(v, kt2_sb[:, par * 128 + h * 64: par * 128 + (h + 1) * 64],
                                ktok_sb[:, cin * 1024 + hm * 64: cin * 1024 + (hm + 1) * 64],
                                mask_sb[:, pc + h * 128 + 127: pc + h * 128 + 128],
                                None, ALU.mult)
                        i = v.drain()
                        bump(i, "v")
                        SN_ms[i_g] = CNT["v"]
                    sec("vector", v_msqk)
                    if i_g == 0:
                        stage("c0_ms")

                    def t_grp(t, i_g=i_g, hp=hp, cin=cin, par=par, pc=pc):
                        wtv(t, "tensor", "v", SN_ms[i_g])
                        if i_g >= 8:
                            wtv(t, "tensor", "v", SN_new[i_g - 8])
                        if i_g >= 2:
                            wtv(t, "tensor", "s", SN_yev[i_g - 2])
                            wtv(t, "tensor", "v", SN_new[i_g - 2])
                        t.matmul(psY[0:64, par * 128:(par + 1) * 128],
                                 vtok_sb[:, cin * 1024 + 2 * hp * 64: cin * 1024 + (2 * hp + 1) * 64],
                                 ms_sb[:, pc: pc + 128],
                                 start=True, stop=False)
                        t.matmul(psY[0:64, par * 128:(par + 1) * 128],
                                 statesB[0:64, hp * 64:(hp + 1) * 64],
                                 qp_sb[0:64, par * 128:(par + 1) * 128],
                                 start=False, stop=True)
                        t.matmul(psY[64:128, par * 128:(par + 1) * 128],
                                 vtok_sb[:, cin * 1024 + (2 * hp + 1) * 64: cin * 1024 + (2 * hp + 2) * 64],
                                 ms_sb[:, pc + 128: pc + 256],
                                 start=True, stop=False, tile_position=(0, 64))
                        t.matmul(psY[64:128, par * 128:(par + 1) * 128],
                                 statesB[:, hp * 64:(hp + 1) * 64],
                                 qp_sb[:, par * 128:(par + 1) * 128],
                                 start=False, stop=False, tile_position=(0, 64))
                        t.matmul(psY[64:128, par * 128:(par + 1) * 128],
                                 statesB[0:64, hp * 64:(hp + 1) * 64],
                                 qpn_sb[0:64, par * 128:(par + 1) * 128],
                                 start=False, stop=True, tile_position=(0, 64))
                        last = None
                        for h in range(2):
                            hm = 2 * hp + h
                            tp = None if h == 0 else (0, 64)
                            last = t.matmul(
                                psS[h * 64:(h + 1) * 64, par * 128: par * 128 + 64],
                                kt2_sb[:, par * 128 + h * 64: par * 128 + (h + 1) * 64],
                                vtok_sb[:, cin * 1024 + hm * 64: cin * 1024 + (hm + 1) * 64],
                                start=True, stop=True, tile_position=tp)
                        bump(last, "t")
                        SN_grp[i_g] = CNT["t"]
                    sec("tensor", t_grp)
                    if i_g == 0:
                        stage("c0_grp")

                    def v_new(v, i_g=i_g, hp=hp, cin=cin, par=par):
                        wtv(v, "vector", "t", SN_grp[i_g])
                        vSTT(v, statesF[:, hp * 64:(hp + 1) * 64],
                             statesF[:, hp * 64:(hp + 1) * 64],
                             PPall[:, hp * 4 + cin: hp * 4 + cin + 1],
                             psS[:, par * 128: par * 128 + 64],
                             ALU.mult, ALU.add)
                        vCP(v, statesB[:, hp * 64:(hp + 1) * 64],
                            statesF[:, hp * 64:(hp + 1) * 64], inc="v")
                        SN_new[i_g] = CNT["v"]
                    sec("vector", v_new)

                    def s_yev(s, i_g=i_g, hp=hp, gc=gc, par=par):
                        wtv(s, "scalar", "t", SN_grp[i_g])
                        sACT(s, y_sb[:, hp * 2048 + gc * 128: hp * 2048 + (gc + 1) * 128],
                             psY[:, par * 128:(par + 1) * 128],
                             AF.Identity, bias=ccol(C_ZERO), scale=ccol(C_ONE), inc="s")
                        SN_yev[i_g] = CNT["s"]
                    sec("scalar", s_yev)

                    for engx, fnx in p4b_sched[i_g]:
                        sec(engx, fnx)

                    glob += 1
                    if glob == 8:
                        stage("p2_c0")

        stage("p2")
        wq_o0 = CNT["oW0"]
        wq_o1 = CNT["oW1"]
        def g_wpj(g):
            wt(g, "gpsimd", "t")
            for hh in range(2):
                i = g.dma_start(
                    out=wpj_sb[:, hh * 4096:(hh + 1) * 4096],
                    in_=wpjT[hh * 512:(hh + 1) * 512, :].rearrange("(i p) o -> p i o", p=128))
                bump(i, "gW", 16)
        sec("gpsimd", g_wpj)

        for cc in range(8):
            for ts4 in range(4):
                si = cc * 4 + ts4
                par = si % 2

                def t_prj(t, cc=cc, ts4=ts4, par=par, si=si):
                    wt(t, "tensor", "gW")
                    wtv(t, "tensor", "s", SN_yev[NIT - 1])
                    if si >= 2:
                        wt(t, "tensor", "v")
                    i = None
                    for ic in range(8):
                        i = t.matmul(psG[par][:, 0:512],
                                     wpj_sb[:, ic * 1024 + cc * 128: ic * 1024 + (cc + 1) * 128],
                                     y_sb[:, ic * 2048 + ts4 * 512: ic * 2048 + (ts4 + 1) * 512],
                                     start=(ic == 0), stop=(ic == 7))
                    bump(i, "t")
                sec("tensor", t_prj)

                def v_prj(v, cc=cc, ts4=ts4, par=par):
                    wt(v, "vector", "t")
                    vCP(v, ob_sb[:, cc * 2048 + ts4 * 512: cc * 2048 + (ts4 + 1) * 512],
                        psG[par][:, 0:512], inc="v")
                sec("vector", v_prj)
        prj_v = CNT["v"]

        stage("p2b")
        THID_SNAP = []
        TMLP_SNAP = []
        for ts in range(4):
            def v_osq(v, ts=ts, prj_v=prj_v):
                wtv(v, "vector", "v", prj_v)
                if ts > 0:
                    wt(v, "vector", "t")
                for cc in range(8):
                    sl = slice(cc * 512, (cc + 1) * 512)
                    ol = slice(cc * 2048 + ts * 512, cc * 2048 + (ts + 1) * 512)
                    vTT(v, yn_sb[:, sl], ob_sb[:, ol], ob_sb[:, ol], ALU.mult)
                i = v.drain()
                bump(i, "v")
            sec("vector", v_osq)

            def t_rstat(t):
                wt(t, "tensor", "v")
                i = None
                for cc in range(8):
                    i = t.matmul(psC[:, 0:512], onesb[:], yn_sb[:, cc * 512:(cc + 1) * 512],
                                 start=(cc == 0), stop=(cc == 7))
                bump(i, "t")
            sec("tensor", t_rstat)

            def v_rst2(v):
                wt(v, "vector", "t")
                vTS(v, st3A[:], psC[:, 0:512], 1.0 / 1024.0, None, ALU.mult, inc="v")
            sec("vector", v_rst2)

            def s_rst(s):
                wt(s, "scalar", "v")
                sACT(s, st3C[:], st3A[:], AF.Ln, bias=ccol(C_EPSRMS), scale=ccol(C_ONE))
                sACT(s, st3B[:], st3C[:], AF.Exp, bias=ccol(C_ZERO), scale=ccol(C_MHALF), inc="s")
            sec("scalar", s_rst)

            def v_yn(v, ts=ts):
                wt(v, "vector", "s")
                for cc in range(8):
                    sl = slice(cc * 512, (cc + 1) * 512)
                    ol = slice(cc * 2048 + ts * 512, cc * 2048 + (ts + 1) * 512)
                    vTT(v, yn_sb[:, sl], ob_sb[:, ol], st3B[:], ALU.mult)
                i = v.drain()
                bump(i, "v")
            sec("vector", v_yn)
            yn_v = CNT["v"]

            for fc in range(32):
                par = fc % 2

                def g_wfc(eng, ts=ts, fc=fc, par=par):
                    q = "gpsimd" if par == 0 else "scalar"
                    if ts == 0 and fc < 2:
                        wtv(eng, q, "cv", CV_TOT)
                    gi = ts * 32 + fc
                    if gi >= 2:
                        wtv(eng, q, "t", THID_SNAP[gi - 2])
                    i = eng.dma_start(
                        out=wfc_t[:, par * 1024:(par + 1) * 1024],
                        in_=wfc_scr[:, fc * 1024:(fc + 1) * 1024])
                    bump(i, "gF0" if par == 0 else "gF1", 16)
                sec("gpsimd" if par == 0 else "scalar", g_wfc)

                def t_hid(t, fc=fc, par=par, yn_v=yn_v):
                    wt(t, "tensor", "gF0" if par == 0 else "gF1")
                    wtv(t, "tensor", "v", yn_v)
                    if fc >= 2:
                        wt(t, "tensor", "v", "s")
                    i = None
                    for ic in range(8):
                        i = t.matmul(psG[par][:, 0:512],
                                     wfc_t[:, par * 1024 + ic * 128: par * 1024 + (ic + 1) * 128],
                                     yn_sb[:, ic * 512:(ic + 1) * 512],
                                     start=(ic == 0), stop=(ic == 7))
                    bump(i, "t")
                    THID_SNAP.append(CNT["t"])
                sec("tensor", t_hid)

                def s_r2(s, fc=fc, par=par):
                    wt(s, "scalar", "t")
                    sACT(s, t1_sc[:, par * 512:(par + 1) * 512], psG[par][:, 0:512],
                         AF.Relu, bias=ccol(C_ZERO), scale=ccol(C_ONE), inc="s")
                sec("scalar", s_r2)

                def v_r2(v, fc=fc, par=par):
                    wt(v, "vector", "s")
                    vTT(v, relu2_sb[:, fc * 512:(fc + 1) * 512],
                        t1_sc[:, par * 512:(par + 1) * 512],
                        t1_sc[:, par * 512:(par + 1) * 512], ALU.mult, inc="v")
                sec("vector", v_r2)
            r2_v = CNT["v"]

            for ch in range(2):
                for fc in range(32):
                    parw = fc % 2

                    def g_wcp(eng, ts=ts, ch=ch, fc=fc, parw=parw):
                        q = "sync" if parw == 0 else "gpsimd"
                        if ts == 0 and ch == 0 and fc < 2:
                            wtv(eng, q, "cv", CV_TOT)
                        gi = (ts * 2 + ch) * 32 + fc
                        if gi >= 2:
                            wtv(eng, q, "t", TMLP_SNAP[gi - 2])
                        i = eng.dma_start(out=wcpt[:, parw * 512:(parw + 1) * 512],
                                          in_=wcp_scr[:, fc * 1024 + ch * 512:
                                                      fc * 1024 + (ch + 1) * 512])
                        bump(i, "gP0" if parw == 0 else "gP1", 16)
                    sec("sync" if parw == 0 else "gpsimd", g_wcp)

                    def t_mlp(t, ch=ch, fc=fc, parw=parw, r2_v=r2_v):
                        wtv(t, "tensor", "v", r2_v)
                        wt(t, "tensor", "gP0" if parw == 0 else "gP1")
                        if fc == 0:
                            wt(t, "tensor", "v")
                        i = None
                        for j in range(4):
                            i = t.matmul(psBANKS[j][:, 0:512],
                                         wcpt[:, parw * 512 + j * 128: parw * 512 + (j + 1) * 128],
                                         relu2_sb[:, fc * 512:(fc + 1) * 512],
                                         start=(fc == 0), stop=(fc == 31))
                        bump(i, "t")
                        TMLP_SNAP.append(CNT["t"])
                    sec("tensor", t_mlp)

                for j in range(4):
                    cc = ch * 4 + j
                    parx = (ts * 8 + cc) % 2

                    def hw_x3(sy, cc=cc, ts=ts, parx=parx):
                        wt(sy, "sync", "v")
                        i = sy.dma_start(out=x_sl3[:, parx * 512:(parx + 1) * 512],
                                         in_=xT[cc * 128:(cc + 1) * 128, ts * 512:(ts + 1) * 512])
                        bump(i, "h3", 16)
                    sec("sync", hw_x3)

                    def v_x2(v, j=j, cc=cc, ts=ts, parx=parx):
                        wt(v, "vector", "t", "h3")
                        ol = slice(cc * 2048 + ts * 512, cc * 2048 + (ts + 1) * 512)
                        vTT(v, t1_sc[:, parx * 512:(parx + 1) * 512],
                            psBANKS[j][:, 0:512], ob_sb[:, ol], ALU.add)
                        vSTT(v, x2_sb[:, ol],
                             t1_sc[:, parx * 512:(parx + 1) * 512], col_ga(cc),
                             x_sl3[:, parx * 512:(parx + 1) * 512],
                             ALU.mult, ALU.add, inc="v")
                    sec("vector", v_x2)
        x2_v = CNT["v"]

        stage("p3")
        for ts in range(4):
            def v_x2c(v, ts=ts, x2_v=x2_v):
                wtv(v, "vector", "v", x2_v)
                if ts > 0:
                    wt(v, "vector", "t")
                for cc in range(8):
                    sl = slice(cc * 512, (cc + 1) * 512)
                    ol = slice(cc * 2048 + ts * 512, cc * 2048 + (ts + 1) * 512)
                    vCP(v, xb4_sb[:, sl], x2_sb[:, ol])
                for cc in range(8):
                    sl = slice(cc * 512, (cc + 1) * 512)
                    vTT(v, xsq4_sb[:, sl], xb4_sb[:, sl], xb4_sb[:, sl], ALU.mult)
                i = v.drain()
                bump(i, "v")
            sec("vector", v_x2c)

            def t_st4(t):
                wt(t, "tensor", "v")
                for cc in range(8):
                    t.matmul(psC[:, 0:512], onesb[:], xb4_sb[:, cc * 512:(cc + 1) * 512],
                             start=(cc == 0), stop=(cc == 7))
                i = None
                for cc in range(8):
                    i = t.matmul(psD[:, 0:512], onesb[:], xsq4_sb[:, cc * 512:(cc + 1) * 512],
                                 start=(cc == 0), stop=(cc == 7))
                bump(i, "t")
            sec("tensor", t_st4)

            def v_st4(v):
                wt(v, "vector", "t")
                vTS(v, st3A[:], psC[:, 0:512], 1.0 / 1024.0, None, ALU.mult)
                vTS(v, st3B[:], psD[:, 0:512], 1.0 / 1024.0, None, ALU.mult)
                vTT(v, st3C[:], st3A[:], st3A[:], ALU.mult)
                vSTT(v, st3D[:], st3C[:], -1.0, st3B[:], ALU.mult, ALU.add, inc="v")
            sec("vector", v_st4)

            def s_st4(s):
                wt(s, "scalar", "v")
                sACT(s, st3C[:], st3D[:], AF.Ln, bias=ccol(C_EPSLN), scale=ccol(C_ONE))
                sACT(s, st3B[:], st3C[:], AF.Exp, bias=ccol(C_ZERO), scale=ccol(C_MHALF), inc="s")
            sec("scalar", s_st4)

            def v_h2(v, ts=ts):
                wt(v, "vector", "s")
                wt(v, "vector", "oW0", "oW1")
                for cc in range(8):
                    ol = slice(cc * 2048 + ts * 512, cc * 2048 + (ts + 1) * 512)
                    vTT(v, st3D[:], x2_sb[:, ol], st3A[:], ALU.subtract)
                    vTT(v, st3C[:], st3D[:], st3B[:], ALU.mult)
                    vTS(v, h2_sb[:, ol], st3C[:], col_cb1(cc), col_sb(cc), ALU.mult, ALU.add)
                o0 = ts * 512
                vMS(v, st3D[:], 0.0)
                for cc in range(8):
                    hsl = slice(cc * 2048 + o0, cc * 2048 + o0 + 512)
                    vTT(v, st3D[:], st3D[:], h2_sb[:, hsl], ALU.max)
                    vSTT(v, st3D[:], h2_sb[:, hsl], -1.0, st3D[:], ALU.mult, ALU.max)
                i = v.drain()
                bump(i, "v")
            sec("vector", v_h2)

            def t_tp4(t):
                wt(t, "tensor", "v")
                i = None
                for tt in range(4):
                    i = t.transpose(psD[:, tt * 128:(tt + 1) * 128],
                                    st3D[:, tt * 128:(tt + 1) * 128], diagf[:])
                bump(i, "t")
            sec("tensor", t_tp4)

            def v_rm1(v, ts=ts):
                wt(v, "vector", "t")
                for tt in range(4):
                    vRED(v, rmax1[:, ts * 4 + tt: ts * 4 + tt + 1],
                         psD[:, tt * 128:(tt + 1) * 128], ALU.max)
                i = v.drain()
                bump(i, "v")
            sec("vector", v_rm1)
        p4a_v = CNT["v"]

        stage("p4a")
        def v_s1(v, p4a_v=p4a_v):
            wtv(v, "vector", "v", p4a_v)
            wtv(v, "vector", "oW0", wq_o0)
            wtv(v, "vector", "oW1", wq_o1)
            vTS(v, sv_a[:], rmax1[:], 1.0e-5, None, ALU.max)
            vRCP(v, sv_b[:], sv_a[:])
            vTS(v, s1v[:], sv_b[:], 127.0, None, ALU.mult)
            vTS(v, sc1v[:], sv_a[:], msc_sc[:, 1:2], 1.0 / 127.0, ALU.mult, ALU.mult,
                inc="v")
        sec("vector", v_s1)

        def t_tps1(t):
            wt(t, "tensor", "v")
            t.transpose(psD[0:16, 0:128], s1v[:], diagf[:])
            i = t.transpose(psD[0:16, 128:256], sc1v[:], diagf[:])
            bump(i, "t")
        sec("tensor", t_tps1)

        def v_tps1(v):
            wt(v, "vector", "t")
            vCP(v, tposed[:], psD[0:16, 0:256], inc="v")
        sec("vector", v_tps1)

        def g_oht1(g):
            wt(g, "gpsimd", "v")
            i = g.iota(OHT[:], [[-1, 16], [0, 128]], base=0, channel_multiplier=1,
                       allow_small_or_imprecise_dtypes=True)
            bump(i, "g")
            g.drain()
        sec("gpsimd", g_oht1)

        def v_oht1(v):
            wt(v, "vector", "g")
            vTS(v, OHT[:], OHT[:], 0.0, None, ALU.is_equal, inc="v")
        sec("vector", v_oht1)

        for grp in range(4):
            def t_rp1(t, grp=grp):
                wt(t, "tensor", "v")
                for k in range(4):
                    tt = grp * 4 + k
                    t.matmul(psC[:, k * 128:(k + 1) * 128],
                             OHT[:, tt * 128:(tt + 1) * 128], tposed[:, 0:128],
                             start=True, stop=True)
                i = None
                for k in range(4):
                    tt = grp * 4 + k
                    i = t.matmul(psD[:, k * 128:(k + 1) * 128],
                                 OHT[:, tt * 128:(tt + 1) * 128], tposed[:, 128:256],
                                 start=True, stop=True)
                bump(i, "t")
            sec("tensor", t_rp1)

            def v_rp1(v, grp=grp):
                wt(v, "vector", "t")
                vCP(v, s1rep[:, grp * 512:(grp + 1) * 512], psC[:, 0:512])
                vCP(v, sc1rep[:, grp * 512:(grp + 1) * 512], psD[:, 0:512], inc="v")
            sec("vector", v_rp1)
        rep1_v = CNT["v"]

        def v_pad(v):
            wt(v, "vector", "t")
            for ic in range(8):
                vMS(v, xq1_sb[:, ic * 2050:ic * 2050 + 1], 0.0)
                vMS(v, xq1_sb[:, ic * 2050 + 2049:ic * 2050 + 2050], 0.0)
            i = v.drain()
            bump(i, "v")
        sec("vector", v_pad)

        for cc in range(8):

            def v_xq1(v, cc=cc, rep1_v=rep1_v):
                wtv(v, "vector", "v", rep1_v)
                ol = slice(cc * 2048, (cc + 1) * 2048)
                ladder_round(v, pq4, aq4, rq4, uq4,
                             h2_sb[:, ol], s1rep[:],
                             xq1_sb[:, cc * 2050 + 1: cc * 2050 + 2049],
                             sc1rep[:], inc="v")
            sec("vector", v_xq1)
        xq1_v = CNT["v"]

        stage("p4c")
        TU_SNAP = []
        UEV_SNAP = []
        TCV_SNAP = []
        for hf in range(2):
            def v_rz(v, hf=hf, xq1_v=xq1_v):
                wtv(v, "vector", "v", xq1_v)
                if hf > 0:
                    wt(v, "vector", "t")
                vMS(v, run2q[:], 0.0, inc="v")
            sec("vector", v_rz)

            for fc in range(32):
                par = fc % 2

                def hw_w1s(sy, hf=hf, fc=fc, par=par):
                    gi = hf * 32 + fc
                    wtv(sy, "sync", "oW0", wq_o0)
                    wtv(sy, "sync", "oW1", wq_o1)
                    if gi >= 2:
                        wtv(sy, "sync", "t", TU_SNAP[gi - 2])
                    else:
                        wt(sy, "sync", "t", "v")
                    i = sy.dma_start(out=w1t_sb[:, par * 1024:(par + 1) * 1024],
                                     in_=w1q_scr[:, fc * 1024:(fc + 1) * 1024])
                    bump(i, "h50" if par == 0 else "h51", 16)
                sec("sync", hw_w1s)

                def g_dia(g, hf=hf, fc=fc, par=par):
                    gi = hf * 32 + fc
                    if fc < 2:
                        if hf == 0:
                            wtv(g, "gpsimd", "v", xq1_v)
                        else:
                            wt(g, "gpsimd", "v")
                    if gi >= 2:
                        wtv(g, "gpsimd", "t", TCV_SNAP[gi - 2])
                    for j in range(3):
                        pTS(g, dia_sb[:, par * 384 + j * 128: par * 384 + (j + 1) * 128],
                            diagf[:], cwc_sb[:, fc * 3 + j: fc * 3 + j + 1], None, ALU.mult)
                    i = g.nop()
                    bump(i, "g")
                sec("gpsimd", g_dia)

                uG0 = psG0 if par == 0 else psY
                uG1 = psG1 if par == 0 else psS
                uC = psC if par == 0 else psD

                def t_u(t, hf=hf, fc=fc, par=par, uG0=uG0, uG1=uG1, uC=uC):
                    gi = hf * 32 + fc
                    wt(t, "tensor", "h50" if par == 0 else "h51")
                    if gi >= 2:
                        wtv(t, "tensor", "v", UEV_SNAP[gi - 2])
                    else:
                        wtv(t, "tensor", "v", xq1_v)
                        wt(t, "tensor", "v")
                    for ic in range(8):
                        base = ic * 2050 + hf * 1024
                        t.matmul(uG0[:, 0:512],
                                 w1t_sb[:, par * 1024 + ic * 128: par * 1024 + (ic + 1) * 128],
                                 xq1_sb[:, base: base + 512],
                                 start=(ic == 0), stop=(ic == 7))
                        t.matmul(uG1[:, 0:512],
                                 w1t_sb[:, par * 1024 + ic * 128: par * 1024 + (ic + 1) * 128],
                                 xq1_sb[:, base + 512: base + 1024],
                                 start=(ic == 0), stop=(ic == 7))
                    i = None
                    for ic in range(8):
                        base = ic * 2050 + hf * 1024
                        i = t.matmul(uC[:, 0:2],
                                     w1t_sb[:, par * 1024 + ic * 128: par * 1024 + (ic + 1) * 128],
                                     xq1_sb[:, base + 1024: base + 1026],
                                     start=(ic == 0), stop=(ic == 7))
                    bump(i, "t")
                    TU_SNAP.append(CNT["t"])
                sec("tensor", t_u)

                def v_uev(v, par=par, uG0=uG0, uG1=uG1, uC=uC):
                    wt(v, "vector", "t")
                    wtv(v, "vector", "oW0", wq_o0)
                    wtv(v, "vector", "oW1", wq_o1)
                    vCP(v, u_sb[:, par * 1026 + 0: par * 1026 + 512], uG0[:, 0:512])
                    vCP(v, u_sb[:, par * 1026 + 512: par * 1026 + 1024], uG1[:, 0:512])
                    vCP(v, u_sb[:, par * 1026 + 1024: par * 1026 + 1026], uC[:, 0:2],
                        inc="v")
                    UEV_SNAP.append(CNT["v"])
                sec("vector", v_uev)

                def t_cv(t, fc=fc, par=par):
                    wt(t, "tensor", "v", "s", "g")
                    i = None
                    for oh in range(2):
                        pb = psA if oh == 0 else psB
                        for j in range(3):
                            i = t.matmul(pb[:, 0:512],
                                         dia_sb[:, par * 384 + j * 128: par * 384 + (j + 1) * 128],
                                         u_sb[:, par * 1026 + oh * 512 + j:
                                              par * 1026 + oh * 512 + j + 512],
                                         start=(j == 0), stop=(j == 2))
                    bump(i, "t")
                    TCV_SNAP.append(CNT["t"])
                sec("tensor", t_cv)

                def s_si(s, fc=fc):
                    wt(s, "scalar", "t")
                    sACT(s, silu_sb[:, fc * 512: (fc + 1) * 512], psA[:, 0:512],
                         AF.Silu, bias=cbc_sb[:, fc:fc + 1], scale=ccol(C_ONE))
                    sACT(s, silu_sb[:, 16384 + fc * 512: 16384 + (fc + 1) * 512],
                         psB[:, 0:512],
                         AF.Silu, bias=cbc_sb[:, fc:fc + 1], scale=ccol(C_ONE), inc="s")
                sec("scalar", s_si)

                if fc % 2 == 1:
                    def v_cm(v, fc=fc):
                        wt(v, "vector", "s")
                        for qq in range(2):
                            seg = slice(qq * 16384 + (fc - 1) * 512,
                                        qq * 16384 + (fc + 1) * 512)
                            acc = slice(qq * 1024, (qq + 1) * 1024)
                            vTT(v, run2q[:, acc], run2q[:, acc], silu_sb[:, seg],
                                ALU.max)
                            vSTT(v, run2q[:, acc], silu_sb[:, seg], -1.0,
                                 run2q[:, acc], ALU.mult, ALU.max)
                        i = v.drain()
                        bump(i, "v")
                    sec("vector", v_cm)
            cm_v = CNT["v"]

            def v_r2f(v, cm_v=cm_v):
                wtv(v, "vector", "v", cm_v)
                for qq in range(2):
                    vTT(v, run2f[:, qq * 512:(qq + 1) * 512],
                        run2q[:, qq * 1024: qq * 1024 + 512],
                        run2q[:, qq * 1024 + 512: (qq + 1) * 1024], ALU.max)
                i = v.drain()
                bump(i, "v")
            sec("vector", v_r2f)

            for wv_i in range(2):
                def t_tp5(t, wv_i=wv_i):
                    wt(t, "tensor", "v")
                    i = None
                    for k in range(4):
                        i = t.transpose(psD[:, k * 128:(k + 1) * 128],
                                        run2f[:, (wv_i * 4 + k) * 128:(wv_i * 4 + k + 1) * 128],
                                        diagf[:])
                    bump(i, "t")
                sec("tensor", t_tp5)

                def v_rm2(v, wv_i=wv_i):
                    wt(v, "vector", "t")
                    for k in range(4):
                        vRED(v, rmax2[:, wv_i * 4 + k: wv_i * 4 + k + 1],
                             psD[:, k * 128:(k + 1) * 128], ALU.max)
                    i = v.drain()
                    bump(i, "v")
                sec("vector", v_rm2)

            def v_s2(v):
                wt(v, "vector", "t")
                vTS(v, sv_a[:, 0:8], rmax2[:], 1.0e-5, None, ALU.max)
                vRCP(v, sv_b[:, 0:8], sv_a[:, 0:8])
                vTS(v, s1v[:, 0:8], sv_b[:, 0:8], 127.0, None, ALU.mult)
                vTS(v, sc1v[:, 0:8], sv_a[:, 0:8], msc_sc[:, 3:4], 1.0 / 127.0,
                    ALU.mult, ALU.mult, inc="v")
            sec("vector", v_s2)

            def t_tps2(t):
                wt(t, "tensor", "v")
                t.transpose(psD[0:8, 0:128], s1v[:, 0:8], diagf[:])
                i = t.transpose(psD[0:8, 128:256], sc1v[:, 0:8], diagf[:])
                bump(i, "t")
            sec("tensor", t_tps2)

            def v_tps2(v):
                wt(v, "vector", "t")
                vCP(v, tposed[0:8, 0:256], psD[0:8, 0:256], inc="v")
            sec("vector", v_tps2)

            def g_oht2(g):
                wt(g, "gpsimd", "v", "t")
                i = g.iota(OHT[:], [[-1, 16], [0, 128]], base=0, channel_multiplier=1,
                           allow_small_or_imprecise_dtypes=True)
                bump(i, "g")
                g.drain()
            sec("gpsimd", g_oht2)

            def v_oht2(v):
                wt(v, "vector", "g")
                vTS(v, OHT[:], OHT[:], 0.0, None, ALU.is_equal, inc="v")
            sec("vector", v_oht2)

            for grp in range(2):
                def t_rp2(t, grp=grp):
                    wt(t, "tensor", "v")
                    for k in range(4):
                        tt = grp * 4 + k
                        t.matmul(psC[:, k * 128:(k + 1) * 128],
                                 OHT[:, tt * 128:(tt + 1) * 128], tposed[0:16, 0:128],
                                 start=True, stop=True)
                    i = None
                    for k in range(4):
                        tt = grp * 4 + k
                        i = t.matmul(psD[:, k * 128:(k + 1) * 128],
                                     OHT[:, tt * 128:(tt + 1) * 128], tposed[0:16, 128:256],
                                     start=True, stop=True)
                    bump(i, "t")
                sec("tensor", t_rp2)

                def v_rp2(v, grp=grp):
                    wt(v, "vector", "t")
                    vCP(v, s2rep[:, grp * 512:(grp + 1) * 512], psC[:, 0:512])
                    vCP(v, sc2rep[:, grp * 512:(grp + 1) * 512], psD[:, 0:512], inc="v")
                sec("vector", v_rp2)
            rep2_v = CNT["v"]

            for q in range(2):
                ts_g = hf * 2 + q

                def v_s2d(v, q=q, rep2_v=rep2_v):
                    wtv(v, "vector", "v", rep2_v)
                    qsl = slice(q * 512, (q + 1) * 512)
                    vCP(v, s2d[:, 0:512], s2rep[:, qsl])
                    vCP(v, s2d[:, 512:1024], s2rep[:, qsl])
                    vCP(v, sc2d[:, 0:512], sc2rep[:, qsl])
                    vCP(v, sc2d[:, 512:1024], sc2rep[:, qsl], inc="v")
                sec("vector", v_s2d)

                TSNAP = []
                for fcp in range(16):

                    def v_xq2(v, q=q, fcp=fcp, TSNAP=TSNAP):
                        if fcp < 2:
                            wt(v, "vector", "t")
                        else:
                            wtv(v, "vector", "t", TSNAP[2 * fcp - 3])
                        base = (fcp % 2) * 1024
                        ladder_round(v, pq5, aq5, rq5, uq5,
                                     silu_sb[:, q * 16384 + fcp * 1024:
                                             q * 16384 + (fcp + 1) * 1024],
                                     s2d[:], xq2_sc[:, base: base + 1024],
                                     sc2d[:], inc="v")
                    sec("vector", v_xq2)

                    for hx in range(2):
                        fc = fcp * 2 + hx
                        par5 = fc % 2

                        def hw_w2s(sy, fc=fc, par5=par5, TSNAP=TSNAP):
                            if fc >= 2:
                                wtv(sy, "sync", "t", TSNAP[fc - 2])
                            else:
                                wt(sy, "sync", "t")
                            i = sy.dma_start(out=w2t_sb[:, par5 * 1024:(par5 + 1) * 1024],
                                             in_=w2q_scr[:, fc * 1024:(fc + 1) * 1024])
                            bump(i, "h50" if par5 == 0 else "h51", 16)
                        sec("sync", hw_w2s)

                        def t_w2(t, fc=fc, fcp=fcp, par5=par5, TSNAP=TSNAP):
                            wt(t, "tensor", "v", "h50" if par5 == 0 else "h51")
                            xbase = (fcp % 2) * 1024 + (fc % 2) * 512
                            i = None
                            for ccx in range(8):
                                i = t.matmul(psBANKS[ccx][:, 0:512],
                                             w2t_sb[:, par5 * 1024 + ccx * 128:
                                                    par5 * 1024 + (ccx + 1) * 128],
                                             xq2_sc[:, xbase: xbase + 512],
                                             start=(fc == 0), stop=(fc == 31))
                            bump(i, "t")
                            TSNAP.append(CNT["t"])
                        sec("tensor", t_w2)

                for ccx in range(8):
                    parO = ccx % 2

                    def v_out(v, ccx=ccx, ts_g=ts_g, parO=parO):
                        wt(v, "vector", "t")
                        if ccx >= 2 or ts_g > 0:
                            wt(v, "vector", "oO0" if parO == 0 else "oO1")
                        ol = slice(ccx * 2048 + ts_g * 512, ccx * 2048 + (ts_g + 1) * 512)
                        vSTT(v, outb[:, parO * 512:(parO + 1) * 512],
                             psBANKS[ccx][:, 0:512], col_gb(ccx), x2_sb[:, ol],
                             ALU.mult, ALU.add, inc="v")
                    sec("vector", v_out)

                    def hw_out(sy, ccx=ccx, ts_g=ts_g, parO=parO):
                        wt(sy, "sync", "v")
                        i = sy.dma_start(out=out[ccx * 128:(ccx + 1) * 128,
                                                 ts_g * 512:(ts_g + 1) * 512],
                                         in_=outb[:, parO * 512:(parO + 1) * 512])
                        bump(i, "oO0" if parO == 0 else "oO1", 16)
                    sec("sync", hw_out)

        def hw_fin(sy):
            sy.wait_ge(SEM["oO0"], CNT["oO0"])
            sy.wait_ge(SEM["oO1"], CNT["oO1"])
        sec("sync", hw_fin)

    return nc


_CACHE = {}


def _build_quiet():
    import os, sys
    sys.stdout.flush(); sys.stderr.flush()
    saved = (os.dup(1), os.dup(2))
    devnull = os.open(os.devnull, os.O_WRONLY)
    try:
        os.dup2(devnull, 1)
        os.dup2(devnull, 2)
        return build_nc()
    finally:
        os.dup2(saved[0], 1)
        os.dup2(saved[1], 2)
        os.close(saved[0]); os.close(saved[1]); os.close(devnull)


def kernel(**inputs):
    key = "nc"
    if key not in _CACHE:
        _CACHE[key] = _build_quiet()
    nc = _CACHE[key]

    f32 = np.float32

    def ct(a):
        return np.ascontiguousarray(a, dtype=f32)

    x = ct(inputs["L0_attn_x"])
    Wq = ct(inputs["L0_attn_Wq"])
    Wk = ct(inputs["L0_attn_Wk"])
    Wv = ct(inputs["L0_attn_Wv"])
    Wm1 = ct(inputs["L0_attn_Wm1"])
    bm1 = ct(inputs["L0_attn_bm1"])
    Wm2 = ct(inputs["L0_attn_Wm2"])
    bm2 = ct(inputs["L0_attn_bm2"])
    Wproj = ct(inputs["L0_attn_Wproj"])
    Wfc = ct(inputs["L0_attn_Wfc"])
    Wcp = ct(inputs["L0_attn_Wcp"])
    w1 = ct(inputs["L0_ffn_w1"])
    conv_w = ct(inputs["L0_ffn_conv_w"])
    conv_b = ct(inputs["L0_ffn_conv_b"])
    w2 = ct(inputs["L0_ffn_w2"])
    cvec = ct(inputs["L0_c"])
    wmod = ct(inputs["L0_wmod"])

    com = {
        "wqT": np.ascontiguousarray(Wq.T),
        "wkT": np.ascontiguousarray(Wk.T),
        "wvT": np.ascontiguousarray(Wv.T),
        "wpjT": np.ascontiguousarray(Wproj.T),
        "wm1T": np.ascontiguousarray(Wm1.T),
        "wm2T": np.ascontiguousarray(Wm2.T),
        "bm1c": np.ascontiguousarray(bm1.reshape(2, 128).T),
        "bm2c": np.ascontiguousarray(bm2.reshape(16, 1)),
        "wfcT": np.ascontiguousarray(Wfc.T),
        "wcpT": np.ascontiguousarray(Wcp.T),
        "w1T": np.ascontiguousarray(w1.T),
        "w2T": np.ascontiguousarray(w2.T),
        "cwc": np.ascontiguousarray(
            conv_w.reshape(32, 128, 3).transpose(1, 0, 2).reshape(128, 96)),
        "cbc": np.ascontiguousarray(conv_b.reshape(32, 128).T),
        "wmod": wmod,
    }
    in_maps = []
    for b in range(8):
        m = dict(com)
        m["xT"] = np.ascontiguousarray(x[b].T)
        m["c_col"] = np.ascontiguousarray(cvec[b].reshape(8, 128).T)
        in_maps.append(m)

    res = run_bass_kernel_spmd(nc, in_maps, core_ids=list(range(8)))
    outs = [np.ascontiguousarray(np.asarray(r["out"]).T) for r in res.results]
    return np.stack(outs, axis=0)


# version 13
# speedup vs baseline: 1.1608x; 1.1608x over previous
import numpy as np
import ml_dtypes
import concourse.bass as bass
import concourse.mybir as mybir
from concourse.bass_utils import run_bass_kernel_spmd
from contextlib import ExitStack

F32 = mybir.dt.float32
BF16 = mybir.dt.bfloat16
AF = mybir.ActivationFunctionType
ALU = mybir.AluOpType
AX = mybir.AxisListType

T = 2048
C = 1024
H = 16
HID = 256
FF = 4096
SLAB = 512
EPS_LN = 1.0e-5
EPS_RMS = float(np.finfo(np.float32).eps)
KB = 1024

_STOP = None
_MSK = 3


def build_nc():
    nc = bass.Bass("TRN2", target_bir_lowering=False)

    def din(name, shape, dt=F32):
        return nc.dram_tensor(name, shape, dt, kind="ExternalInput").ap()

    xT = din("xT", [C, T])
    wqT = din("wqT", [C, C])
    wkT = din("wkT", [C, C])
    wvT = din("wvT", [C, C])
    wpjT = din("wpjT", [C, C])
    wm1T = din("wm1T", [C, HID])
    wm2T = din("wm2T", [HID, H])
    bm1c = din("bm1c", [128, 2])
    bm2c = din("bm2c", [16, 1])
    wfcT = din("wfcT", [C, FF])
    wcpT = din("wcpT", [FF, C])
    w1T = din("w1T", [C, FF])
    w2T = din("w2T", [FF, C])
    cwc = din("cwc", [128, 96])
    cbc = din("cbc", [128, 32])
    c_col = din("c_col", [128, 8])
    wmod = din("wmod", [C, 6 * C])

    out = nc.dram_tensor("out", [C, T], F32, kind="ExternalOutput").ap()
    w1q_scr = nc.dram_tensor("w1q_scr", [128, 8 * 4096], BF16, kind="Internal").ap()
    w2q_scr = nc.dram_tensor("w2q_scr", [128, 8 * 4096], BF16, kind="Internal").ap()
    wfc_scr = nc.dram_tensor("wfc_scr", [128, 32 * 1024], BF16, kind="Internal").ap()
    wcp_scr = nc.dram_tensor("wcp_scr", [128, 64 * 512], BF16, kind="Internal").ap()

    es = ExitStack()
    with es:
        R1 = 0
        R2 = 64 * KB
        R3 = 128 * KB
        W0 = R3 + 33 * KB
        ARENA = W0 + 32 * KB

        back = es.enter_context(nc.sbuf_tensor("ARENA", [128, ARENA], mybir.dt.uint8))
        base_addr = nc.lookup_mloc(back).addr

        def place(name, shape, dtype, off, base_partition=0):
            h = nc.virtual_sbuf_tensor(name, shape, dtype)
            if base_partition:
                h.base_partition = base_partition
            nc._tensor(h.name, shape, dtype, type="SB")
            mloc = nc.lookup_mloc(h)
            mloc.allocated = True
            mloc.addr = base_addr + off
            return h

        wq_sb = place("wq_sb", [128, 8 * C], BF16, R1 + 0)
        wk_sb = place("wk_sb", [128, 8 * C], BF16, R1 + 16 * KB)
        wv_sb = place("wv_sb", [128, 8 * C], BF16, R1 + 32 * KB)
        wm1_sb = place("wm1_sb", [128, 2048], BF16, R1 + 48 * KB)
        wm2_sb = place("wm2_sb", [128, 32], BF16, R1 + 52 * KB)
        ELp_sb = place("ELp_sb", [128, 8 * 512], BF16, R1 + 53 * KB)
        wpj_sb = place("wpj_sb", [128, 8 * C], BF16, R1 + 0)
        x2_sb = place("x2_sb", [128, 8 * T], F32, R1 + 0)

        wmod_sb = place("wmod_sb", [128, 2 * 6144], BF16, R2 + 0)
        x_sl = place("x_sl", [128, 8 * 512], F32, R2 + 0)
        h1_sb = place("h1_sb", [128, 8 * 512], BF16, R2 + 16 * KB)
        qf_sb = place("qf_sb", [128, 8 * 512], BF16, R2 + 24 * KB)
        xb_sb = place("xb_sb", [128, 8 * 512], BF16, R2 + 24 * KB)
        kf_sb = place("kf_sb", [128, 8 * 512], BF16, R2 + 32 * KB)
        xsq_sb = place("xsq_sb", [128, 8 * 512], BF16, R2 + 32 * KB)
        ktok_sb = place("ktok_sb", [128, 4 * C], BF16, R2 + 40 * KB)
        vtok_sb = place("vtok_sb", [128, 4 * C], BF16, R2 + 48 * KB)
        m1b_sb = place("m1b_sb", [128, 1024], BF16, R2 + 56 * KB)
        st2A = place("st2A", [128, 512], F32, R2 + 58 * KB)
        st2B = place("st2B", [128, 512], F32, R2 + 60 * KB)
        st2C = place("st2C", [128, 512], F32, R2 + 62 * KB)
        ob_sb = place("ob_sb", [128, 8 * T], BF16, R2 + 0)
        h2_sb = place("h2_sb", [128, 8 * T], F32, R2 + 0)
        silu_sb = place("silu_sb", [128, 32 * 1024], BF16, R2 + 0)

        y_sb = place("y_sb", [128, 8 * T], BF16, R3 + 0)
        relu2_sb = place("relu2_sb", [128, 32 * 512], BF16, R3 + 0)
        xq1_sb = place("xq1_sb", [128, 8 * 2050], BF16, R3 + 0)

        qfn_sb = place("qfn_sb", [64, 8 * 512], BF16, W0 + 0)
        NOH = place("NOH", [16, 16 * 128], F32, W0 + 8 * KB)
        PSel = place("PSel", [16, 8 * 128], F32, W0 + 16 * KB)
        e1x = place("e1x", [16, 512], F32, W0 + 20 * KB)
        sp1 = place("sp1", [16, 512], F32, W0 + 22 * KB)
        sp2 = place("sp2", [16, 512], F32, W0 + 24 * KB)
        P16 = place("P16", [16, 512], F32, W0 + 26 * KB)
        mw16 = place("mw16", [16, 512], F32, W0 + 28 * KB)
        RST16 = place("RST16", [16, 512], F32, W0 + 30 * KB)
        yn_sb = place("yn_sb", [128, 8 * 512], BF16, W0 + 0)
        wfc_t = place("wfc_t", [128, 2 * 1024], BF16, W0 + 8 * KB)
        x_sl3 = place("x_sl3", [128, 2 * 512], F32, W0 + 12 * KB)
        t1_sc = place("t1_sc", [128, 2 * 512], F32, W0 + 16 * KB)
        wcpt = place("wcpt", [128, 2 * 512], BF16, W0 + 20 * KB)
        st3A = place("st3A", [128, 512], F32, W0 + 22 * KB)
        st3B = place("st3B", [128, 512], F32, W0 + 24 * KB)
        st3C = place("st3C", [128, 512], F32, W0 + 26 * KB)
        st3D = place("st3D", [128, 512], F32, W0 + 28 * KB)
        xb4_sb = place("xb4_sb", [128, 8 * 512], BF16, W0 + 0)
        xsq4_sb = place("xsq4_sb", [128, 8 * 512], BF16, W0 + 8 * KB)
        wfT_sb = place("wfT_sb", [128, 512], F32, R2 + 58 * KB)
        wqt_sb = place("wqt_sb", [128, 2 * 512], BF16, R2 + 60 * KB)
        stW_sb = place("stW_sb", [128, 512], BF16, R2 + 62 * KB)
        pq4 = place("pq4", [128, 2048], BF16, W0 + 0)
        aq4 = place("aq4", [128, 2048], BF16, W0 + 4 * KB)
        rq4 = place("rq4", [128, 2048], BF16, W0 + 8 * KB)
        uq4 = place("uq4", [128, 2048], BF16, W0 + 12 * KB)
        OHT = place("OHT", [16, 16 * 128], F32, W0 + 10 * KB, 0)
        s1rep = place("s1rep", [128, T], F32, W0 + 18 * KB)
        sc1rep = place("sc1rep", [128, T], BF16, W0 + 26 * KB)
        outb = place("outb", [128, 2 * 512], F32, W0 + 0)
        u_sb = place("u_sb", [128, 2 * 1026], BF16, W0 + 4 * KB + 256)
        w1t_sb = place("w1t_sb", [128, 2 * 1024], BF16, W0 + 8 * KB + 512)
        dia_sb = place("dia_sb", [128, 2 * 384], BF16, W0 + 12 * KB + 768)
        run2q = place("run2q", [128, 2048], BF16, W0 + 19 * KB)
        run2f = place("run2f", [128, 1024], F32, W0 + 10 * KB)
        s2rep = place("s2rep", [128, 1024], BF16, W0 + 23 * KB)
        sc2rep = place("sc2rep", [128, 1024], BF16, W0 + 25 * KB)
        pq5 = place("pq5", [128, 1024], BF16, W0 + 10 * KB)
        aq5 = place("aq5", [128, 1024], BF16, W0 + 12 * KB)
        rq5 = place("rq5", [128, 1024], BF16, W0 + 14 * KB)
        uq5 = place("uq5", [128, 1024], BF16, W0 + 16 * KB)
        s2d = place("s2d", [128, 1024], BF16, W0 + 27 * KB)
        sc2d = place("sc2d", [128, 1024], BF16, W0 + 29 * KB)
        xq2_sc = place("xq2_sc", [128, 4 * 512], BF16, W0 + 19 * KB)
        w2t_sb = place("w2t_sb", [128, 2 * 1024], BF16, W0 + 4 * KB + 256)

        def sbt(name, shape, dtype):
            return es.enter_context(nc.sbuf_tensor(name, shape, dtype))

        consts = sbt("consts", [128, 8], F32)
        onesb = sbt("onesb", [128, 128], BF16)
        onesf = sbt("onesf", [1, 128], F32)
        diagf = sbt("diagf", [128, 128], F32)
        U01 = sbt("U01", [128, 128], F32)
        IdNeg = sbt("IdNeg", [128, 128], F32)
        statesF = sbt("statesF", [128, 8 * 64], F32)
        statesB = sbt("statesB", [128, 8 * 64], BF16)
        PPall = sbt("PPall", [128, 32], F32)
        modtab = sbt("modtab", [128, 48], F32)
        modfin = sbt("modfin", [128, 48], F32)
        modsc = sbt("modsc", [128, 2], F32)
        ccol_sb = sbt("ccol_sb", [128, 8], F32)
        csb = sbt("csb", [128, 8], BF16)
        bm1_sb = sbt("bm1_sb", [128, 2], F32)
        bm2_sb = sbt("bm2_sb", [16, 1], F32)
        nbm2_sb = sbt("nbm2_sb", [16, 1], F32)
        cwc_sb = sbt("cwc_sb", [128, 96], F32)
        cbc_sb = sbt("cbc_sb", [128, 32], F32)
        stD = sbt("stD", [128, 512], F32)
        mask_sb = sbt("mask_sb", [128, 512], F32)
        ms_sb = sbt("ms_sb", [128, 512], BF16)
        qp_sb = sbt("qp_sb", [128, 256], BF16)
        qpn_sb = sbt("qpn_sb", [64, 256], BF16)
        kt2_sb = sbt("kt2_sb", [128, 256], BF16)
        rmax1 = sbt("rmax1", [128, 16], F32)
        rmax2 = sbt("rmax2", [128, 8], F32)
        sv_a = sbt("sv_a", [128, 16], F32)
        sv_b = sbt("sv_b", [128, 16], F32)
        s1v = sbt("s1v", [128, 16], F32)
        sc1v = sbt("sc1v", [128, 16], F32)
        tposed = sbt("tposed", [16, 256], F32)
        acc_w = sbt("acc_w", [128, 1], F32)
        tmpc = sbt("tmpc", [128, 1], F32)
        onecf = sbt("onecf", [128, 1], F32)
        msc_sc = sbt("msc_sc", [128, 4], F32)
        thn_sc = sbt("thn_sc", [128, 2], F32)
        sc11 = sbt("sc11", [1, 2], F32)

        psA = es.enter_context(nc.psum_tensor("psA", [128, 512], F32))
        psB = es.enter_context(nc.psum_tensor("psB", [128, 512], F32))
        psY = es.enter_context(nc.psum_tensor("psY", [128, 512], F32))
        psS = es.enter_context(nc.psum_tensor("psS", [128, 512], F32))
        psG0 = es.enter_context(nc.psum_tensor("psG0", [128, 512], F32))
        psG1 = es.enter_context(nc.psum_tensor("psG1", [128, 512], F32))
        psC = es.enter_context(nc.psum_tensor("psC", [128, 512], F32))
        psD = es.enter_context(nc.psum_tensor("psD", [128, 512], F32))
        psG = [psG0, psG1]
        psBANKS = [psA, psB, psY, psS, psG0, psG1, psC, psD]

        sem_names = ["hS", "hX", "h3", "hW0", "hW1", "oW0", "oW1",
                     "h50", "h51", "oO0", "oO1", "gW", "gM0", "gM1", "gF0", "gF1",
                     "gP0", "gP1", "cv", "t", "v", "s", "g"]
        SEM = {n: es.enter_context(nc.semaphore("sem_" + n)) for n in sem_names}
        CNT = {n: 0 for n in sem_names}
        WAITED = {}

        block = es.enter_context(nc.Block())

        def wt(eng, ename, *keys):
            for k in keys:
                tgt = CNT[k]
                if tgt and WAITED.get((ename, k), 0) < tgt:
                    eng.wait_ge(SEM[k], tgt)
                    WAITED[(ename, k)] = tgt

        def wtv(eng, ename, k, tgt):
            if tgt and WAITED.get((ename, k), 0) < tgt:
                eng.wait_ge(SEM[k], tgt)
                WAITED[(ename, k)] = tgt

        def bump(instr, key, n=1):
            instr.then_inc(SEM[key], n)
            CNT[key] += n

        stopped = [False]

        def sec(ename, fn):
            if stopped[0]:
                return
            getattr(block, ename)(fn)

        def vTT(v, o, a, b, op, inc=None):
            i = v.tensor_tensor(o, a, b, op)
            if inc: bump(i, inc)
            v.drain()

        def vTS(v, o, a, s1, s2, op0, op1=None, inc=None):
            if op1 is None:
                i = v.tensor_scalar(o, a, s1, None, op0)
            else:
                i = v.tensor_scalar(o, a, s1, s2, op0, op1)
            if inc: bump(i, inc)
            v.drain()

        def vSTT(v, o, a, s, b, op0, op1, inc=None):
            i = v.scalar_tensor_tensor(o, a, s, b, op0, op1)
            if inc: bump(i, inc)
            v.drain()

        def vCP(v, o, a, inc=None):
            i = v.tensor_copy(o, a)
            if inc: bump(i, inc)
            v.drain()

        def vMS(v, o, val, inc=None):
            i = v.memset(o, val)
            if inc: bump(i, inc)
            v.drain()

        def vRED(v, o, a, op, absval=None, inc=None):
            i = v.tensor_reduce(o, a, axis=AX.X, op=op, apply_absolute_value=absval)
            if inc: bump(i, inc)
            v.drain()

        def vRCP(v, o, a, inc=None):
            i = v.reciprocal(o, a)
            if inc: bump(i, inc)
            v.drain()

        def vSCAN(v, o, d0, d1, init, op0, op1, inc=None):
            i = v.tensor_tensor_scan(o, d0, d1, init, op0, op1)
            if inc: bump(i, inc)
            v.drain()

        def sACT(s, o, a, func, bias, scale, inc=None):
            i = s.activation(o, a, func, bias=bias, scale=scale)
            if inc: bump(i, inc)
            s.drain()

        def pTS(g, o, a, s1, s2, op0, op1=None, inc=None):
            if op1 is None:
                i = g.tensor_scalar(o, a, s1, None, op0)
            else:
                i = g.tensor_scalar(o, a, s1, s2, op0, op1)
            if inc: bump(i, inc)
            g.drain()

        def ladder_round(v, pq, aq, rq, uq, src_a, src_b, dest, scAP, inc=None):
            vTT(v, pq[:], src_a, src_b, ALU.mult)
            vTS(v, rq[:], pq[:], -1.0, None, ALU.mult)
            vTT(v, aq[:], rq[:], pq[:], ALU.max)
            first = True
            for kbit in (64, 32, 16, 8, 4, 2, 1):
                src = aq if first else rq
                vTS(v, uq[:], src[:], kbit - 0.5, float(-kbit), ALU.is_ge, ALU.mult)
                vTT(v, rq[:], src[:], uq[:], ALU.add)
                first = False
            vTT(v, aq[:], aq[:], rq[:], ALU.subtract)
            vTS(v, uq[:], pq[:], 0.0, 2.0, ALU.is_ge, ALU.mult)
            vSTT(v, aq[:], uq[:], -1.0, aq[:], ALU.add, ALU.mult)
            vTT(v, dest, aq[:], scAP, ALU.mult, inc=inc)

        def ccol(idx):
            return consts[:, idx:idx + 1]
        C_ONE, C_EPSLN, C_EPSRMS, C_MHALF, C_MONE, C_INV1024, C_ZERO = 0, 1, 2, 3, 4, 5, 6

        def c16(idx):
            return consts[0:16, idx:idx + 1]

        def g_p0(g):
            g.iota(U01[:], [[-1, 128]], base=0, channel_multiplier=1,
                   allow_small_or_imprecise_dtypes=True)
            g.iota(NOH[:], [[-1, 16], [0, 128]], base=0, channel_multiplier=1,
                   allow_small_or_imprecise_dtypes=True)
            i = g.iota(PSel[:], [[-1, 16], [0, 64]], base=0, channel_multiplier=1,
                       allow_small_or_imprecise_dtypes=True)
            bump(i, "g")
            for (dst, src) in [(wq_sb, wqT), (wk_sb, wkT), (wv_sb, wvT)]:
                for hh in range(2):
                    i = g.dma_start(
                        out=dst[:, hh * 4096:(hh + 1) * 4096],
                        in_=src[hh * 512:(hh + 1) * 512, :].rearrange(
                            "(i p) o -> p i o", p=128))
                    bump(i, "gW", 16)
            i = g.dma_start(out=wm1_sb[:], in_=wm1T.rearrange("(i p) o -> p i o", p=128))
            bump(i, "gW", 16)
            i = g.dma_start(out=wm2_sb[:], in_=wm2T.rearrange("(i p) o -> p i o", p=128))
            bump(i, "gW", 16)
            i = g.dma_start(out=wmod_sb[:, 0:6144], in_=wmod[0:128, :])
            bump(i, "gM0", 16)
        sec("gpsimd", g_p0)

        def hw_p0(sy):
            for (dst, src) in [(ccol_sb, c_col), (bm1_sb, bm1c), (bm2_sb, bm2c),
                               (cwc_sb, cwc), (cbc_sb, cbc)]:
                i = sy.dma_start(out=dst[:], in_=src)
                bump(i, "hS", 16)
        sec("sync", hw_p0)

        def v_p0(v):
            vMS(v, consts[:, 0:1], 1.0)
            vMS(v, consts[:, 1:2], EPS_LN)
            vMS(v, consts[:, 7:8], 3.0)
            vRCP(v, consts[:, 2:3], consts[:, 7:8])
            vTS(v, consts[:, 2:3], consts[:, 2:3], 4.0, -1.0, ALU.mult, ALU.add)
            vTS(v, consts[:, 2:3], consts[:, 2:3], 3.0, -1.0, ALU.mult, ALU.add)
            vTS(v, consts[:, 7:8], consts[:, 2:3], -1.0, None, ALU.mult)
            vTT(v, consts[:, 2:3], consts[:, 2:3], consts[:, 7:8], ALU.max)
            vMS(v, consts[:, 3:4], -0.5)
            vMS(v, consts[:, 4:5], -1.0)
            vMS(v, consts[:, 5:6], 1.0 / 1024.0)
            vMS(v, consts[:, 6:7], 0.0)
            vMS(v, onesb[:], 1.0)
            vMS(v, onesf[:], 1.0)
            vMS(v, onecf[:], 1.0)
            vMS(v, statesF[:], 0.0)
            vMS(v, statesB[:], 0.0)
            vMS(v, RST16[:], 1.0)
            for cst in range(0, 512, 128):
                vMS(v, RST16[:, cst:cst + 1], 0.0)
            wt(v, "vector", "g", "hS")
            vTS(v, NOH[:], NOH[:], 0.0, -1.0, ALU.is_equal, ALU.mult)
            vTS(v, PSel[:], PSel[:], 0.0, -1.0, ALU.is_equal, ALU.mult)
            vTS(v, nbm2_sb[:], bm2_sb[:], -1.0, None, ALU.mult)
            vTS(v, diagf[:], U01[:], 0.0, None, ALU.is_equal)
            vTS(v, U01[:], U01[:], 0.0, None, ALU.is_gt)
            vTS(v, IdNeg[:], diagf[:], -1.0e9, None, ALU.mult, inc="v")
        sec("vector", v_p0)
        p0_v = CNT["v"]

        def s_p0(s):
            wt(s, "scalar", "hS")
            wtv(s, "scalar", "v", p0_v)
            sACT(s, csb[:], ccol_sb[:], AF.Silu, bias=ccol(C_ZERO), scale=ccol(C_ONE), inc="s")
        sec("scalar", s_p0)

        for ic in range(8):
            par = ic % 2
            if ic + 1 < 8:
                def g_ld(g, ic=ic):
                    par2 = (ic + 1) % 2
                    if ic + 1 >= 2:
                        wt(g, "gpsimd", "t")
                    i = g.dma_start(out=wmod_sb[:, par2 * 6144:(par2 + 1) * 6144],
                                    in_=wmod[(ic + 1) * 128:(ic + 2) * 128, :])
                    bump(i, "gM0" if par2 == 0 else "gM1", 16)
                sec("gpsimd", g_ld)

            def t_mod(t, ic=ic, par=par):
                wt(t, "tensor", "gM0" if par == 0 else "gM1", "s")
                last = None
                for jc in range(48):
                    last = t.matmul(
                        psC[:, jc * 8 + ic: jc * 8 + ic + 1],
                        wmod_sb[:, par * 6144 + jc * 128: par * 6144 + (jc + 1) * 128],
                        csb[:, ic:ic + 1], start=True, stop=True)
                bump(last, "t")
            sec("tensor", t_mod)

        def v_mod(v):
            wt(v, "vector", "t")
            i = v.tensor_reduce(modtab[:],
                                psC[:, 0:384].rearrange("p (j i) -> p j i", i=8),
                                axis=AX.X, op=ALU.add)
            v.drain()
            vCP(v, modfin[:, 0:8], modtab[:, 0:8])
            vTS(v, modfin[:, 8:16], modtab[:, 8:16], 1.0, None, ALU.add)
            vCP(v, modfin[:, 24:32], modtab[:, 24:32])
            vTS(v, modfin[:, 32:40], modtab[:, 32:40], 1.0, None, ALU.add)
            for g0 in (16, 40):
                for kk in range(8):
                    vTS(v, modsc[:, 0:1], modtab[:, g0 + kk:g0 + kk + 1], 0.0, None, ALU.is_ge)
                    vTS(v, modsc[:, 0:1], modsc[:, 0:1], 2.0, -1.0, ALU.mult, ALU.add)
                    vTS(v, modsc[:, 1:2], modtab[:, g0 + kk:g0 + kk + 1], -1.0, None, ALU.mult)
                    vTT(v, modsc[:, 1:2], modsc[:, 1:2], modtab[:, g0 + kk:g0 + kk + 1], ALU.max)
                    vTS(v, modsc[:, 1:2], modsc[:, 1:2], 0.25, None, ALU.max)
                    vTT(v, modfin[:, g0 + kk:g0 + kk + 1], modsc[:, 0:1], modsc[:, 1:2], ALU.mult)
            i = v.drain()
            bump(i, "v")
        sec("vector", v_mod)
        modfin_v = CNT["v"]
        mod_t = CNT["t"]

        def col_sa(cc): return modfin[:, 0 + cc:1 + cc]
        def col_ca1(cc): return modfin[:, 8 + cc:9 + cc]
        def col_ga(cc): return modfin[:, 16 + cc:17 + cc]
        def col_sb(cc): return modfin[:, 24 + cc:25 + cc]
        def col_cb1(cc): return modfin[:, 32 + cc:33 + cc]
        def col_gb(cc): return modfin[:, 40 + cc:41 + cc]

        def stage(name):
            if _STOP == name:
                def hw_stop(sy):
                    for k in ("t", "v", "s", "g"):
                        wt(sy, "sync", k)
                    i = sy.dma_start(out=out[0:128, 0:8], in_=consts[:, 0:8])
                    bump(i, "oO0", 16)
                    sy.wait_ge(SEM["oO0"], CNT["oO0"])
                sec("sync", hw_stop)
                stopped[0] = True

        stage("p0")

        def g_conv(g):
            wfi = wfcT.rearrange("(i p) (c f) -> p c i f", p=128, f=128)
            wci = wcpT.rearrange("(c p) (h x) -> p c h x", p=128, x=512)
            for c in range(32):
                i = g.dma_start(out=wfc_scr[:, c * 1024:(c + 1) * 1024]
                                .rearrange("p (i f) -> p i f", f=128),
                                in_=wfi[:, c])
                bump(i, "cv", 16)
                i = g.dma_start(out=wcp_scr[:, c * 1024:(c + 1) * 1024]
                                .rearrange("p (h x) -> p h x", x=512),
                                in_=wci[:, c])
                bump(i, "cv", 16)
        sec("gpsimd", g_conv)
        CV_TOT = CNT["cv"]

        glob = 0
        NIT = 16 * 8
        SN_mask = [0] * NIT
        SN_exp = [0] * NIT
        SN_ms = [0] * NIT
        SN_grp = [0] * NIT
        SN_new = [0] * NIT
        SN_yev = [0] * NIT

        w1_tiles = [("w1", a, b) for a in range(8) for b in range(8)]
        w2_tiles = [("w2", a, b) for a in range(32) for b in range(2)]

        def wtile_src(kind, a, b):
            if kind == "w1":
                return w1T[a * 128:(a + 1) * 128, b * 512:(b + 1) * 512]
            return w2T[a * 128:(a + 1) * 128, b * 512:(b + 1) * 512]

        def wtile_dst(kind, a, b):
            if kind == "w1":
                rr = w1q_scr.rearrange("p (fc i f) -> p fc i f", i=8, f=128)
                return rr[:, b * 4:(b + 1) * 4, a, :]
            return w2q_scr[:, a * 1024 + b * 512: a * 1024 + (b + 1) * 512]

        p4b_sched = [[] for _ in range(NIT)]
        mth_cell = [0, 0]
        for wi, tiles in enumerate((w1_tiles, w2_tiles)):
            g0 = wi * 64

            def v_acc0(v):
                vMS(v, acc_w[:], 0.0, inc="v")
            p4b_sched[g0].append(("vector", v_acc0))

            for ti, (kk, a, b) in enumerate(tiles):
                g = g0 + ti // 2

                def hw_w(sy, kk=kk, a=a, b=b):
                    wt(sy, "sync", "v")
                    i = sy.dma_start(out=wfT_sb[:], in_=wtile_src(kk, a, b))
                    bump(i, "hW0", 16)
                p4b_sched[g].append(("sync", hw_w))

                def v_abs(v):
                    wt(v, "vector", "hW0")
                    vRED(v, tmpc[:], wfT_sb[:], ALU.add, absval=True)
                    vTT(v, acc_w[:], acc_w[:], tmpc[:], ALU.add, inc="v")
                p4b_sched[g].append(("vector", v_abs))

            gm = g0 + 32

            def t_tot(t):
                wt(t, "tensor", "v")
                i = t.matmul(psD[0:1, 0:1], acc_w[:], onecf[:], start=True, stop=True)
                bump(i, "t")
            p4b_sched[gm].append(("tensor", t_tot))

            def v_mth(v):
                wt(v, "vector", "t")
                vTS(v, sc11[0:1, 0:1], psD[0:1, 0:1], 1.0 / float(FF * C), None, ALU.mult)
                vTS(v, sc11[0:1, 1:2], sc11[0:1, 0:1], 1.0e-5, None, ALU.max)
                vTS(v, sc11[0:1, 0:1], sc11[0:1, 1:2], 0.5, None, ALU.mult, inc="v")
            p4b_sched[gm].append(("vector", v_mth))

            def t_rep(t):
                wt(t, "tensor", "v")
                i = t.matmul(psD[:, 0:2], onesf[:], sc11[0:1, 0:2], start=True, stop=True)
                bump(i, "t")
            p4b_sched[gm].append(("tensor", t_rep))

            def v_rep(v, wi=wi):
                wt(v, "vector", "t")
                vCP(v, msc_sc[:, wi * 2:(wi + 1) * 2], psD[:, 0:2])
                vTS(v, thn_sc[:, wi:wi + 1], msc_sc[:, wi * 2:wi * 2 + 1], -1.0, None,
                    ALU.mult, inc="v")
                mth_cell[wi] = CNT["v"]
            p4b_sched[gm].append(("vector", v_rep))

            for ti, (kk, a, b) in enumerate(tiles):
                g = g0 + 32 + ti // 2
                par = ti % 2

                def hw_w2(sy, kk=kk, a=a, b=b):
                    wt(sy, "sync", "v")
                    i = sy.dma_start(out=wfT_sb[:], in_=wtile_src(kk, a, b))
                    bump(i, "hW0", 16)
                p4b_sched[g].append(("sync", hw_w2))

                def v_tern(v, par=par, wi=wi):
                    wt(v, "vector", "hW0")
                    wtv(v, "vector", "v", mth_cell[wi])
                    wt(v, "vector", "oW0" if par == 0 else "oW1")
                    vTS(v, wqt_sb[:, par * 512:(par + 1) * 512], wfT_sb[:],
                        msc_sc[:, wi * 2:wi * 2 + 1], None, ALU.is_ge)
                    vTS(v, stW_sb[:], wfT_sb[:], thn_sc[:, wi:wi + 1], None, ALU.is_le)
                    vTT(v, wqt_sb[:, par * 512:(par + 1) * 512],
                        wqt_sb[:, par * 512:(par + 1) * 512], stW_sb[:],
                        ALU.subtract, inc="v")
                p4b_sched[g].append(("vector", v_tern))

                def hw_st(sy, kk=kk, a=a, b=b, par=par):
                    wt(sy, "sync", "v")
                    i = sy.dma_start(out=wtile_dst(kk, a, b),
                                     in_=wqt_sb[:, par * 512:(par + 1) * 512])
                    bump(i, "oW0" if par == 0 else "oW1", 16)
                p4b_sched[g].append(("sync", hw_st))

        for s_i in range(4):
            t0 = s_i * SLAB

            def hw_xs(sy, s_i=s_i, t0=t0):
                wtv(sy, "sync", "t", mod_t)
                if s_i > 0:
                    wt(sy, "sync", "v")
                for cc in range(8):
                    i = sy.dma_start(out=x_sl[:, cc * 512:(cc + 1) * 512],
                                     in_=xT[cc * 128:(cc + 1) * 128, t0:t0 + 512])
                    bump(i, "hX", 16)
            sec("sync", hw_xs)

            def v_cast(v, s_i=s_i):
                wt(v, "vector", "hX")
                if s_i > 0:
                    wt(v, "vector", "t")
                for cc in range(8):
                    vCP(v, xb_sb[:, cc * 512:(cc + 1) * 512], x_sl[:, cc * 512:(cc + 1) * 512])
                for cc in range(8):
                    vTT(v, xsq_sb[:, cc * 512:(cc + 1) * 512],
                        xb_sb[:, cc * 512:(cc + 1) * 512],
                        xb_sb[:, cc * 512:(cc + 1) * 512], ALU.mult)
                i = v.drain()
                bump(i, "v")
            sec("vector", v_cast)

            def t_stats(t):
                wt(t, "tensor", "v")
                for cc in range(8):
                    t.matmul(psC[:, 0:512], onesb[:], xb_sb[:, cc * 512:(cc + 1) * 512],
                             start=(cc == 0), stop=(cc == 7))
                i = None
                for cc in range(8):
                    i = t.matmul(psD[:, 0:512], onesb[:], xsq_sb[:, cc * 512:(cc + 1) * 512],
                                 start=(cc == 0), stop=(cc == 7))
                bump(i, "t")
            sec("tensor", t_stats)

            def v_stat2(v):
                wt(v, "vector", "t")
                wt(v, "vector", "oW0", "oW1")
                vTS(v, st2A[:], psC[:, 0:512], 1.0 / 1024.0, None, ALU.mult)
                vTS(v, st2B[:], psD[:, 0:512], 1.0 / 1024.0, None, ALU.mult)
                vTT(v, st2C[:], st2A[:], st2A[:], ALU.mult)
                vSTT(v, stD[:], st2C[:], -1.0, st2B[:], ALU.mult, ALU.add, inc="v")
            sec("vector", v_stat2)

            def s_stat(s):
                wt(s, "scalar", "v")
                sACT(s, st2C[:], stD[:], AF.Ln, bias=ccol(C_EPSLN), scale=ccol(C_ONE))
                sACT(s, st2B[:], st2C[:], AF.Exp, bias=ccol(C_ZERO), scale=ccol(C_MHALF), inc="s")
            sec("scalar", s_stat)

            def v_h1(v):
                wt(v, "vector", "s")
                wtv(v, "vector", "v", modfin_v)
                for cc in range(8):
                    sl = slice(cc * 512, (cc + 1) * 512)
                    vTT(v, stD[:], x_sl[:, sl], st2A[:], ALU.subtract)
                    vTT(v, st2C[:], stD[:], st2B[:], ALU.mult)
                    vTS(v, h1_sb[:, sl], st2C[:], col_ca1(cc), col_sa(cc), ALU.mult, ALU.add)
                i = v.drain()
                bump(i, "v")
            sec("vector", v_h1)
            h1_v = CNT["v"]
            if s_i == 0:
                stage("p2_h1")

            qkv_specs = ([("qf", hp) for hp in range(8)]
                         + [("kf", hp) for hp in range(8)]
                         + [("kt", tt * 2 + oh) for tt in range(4) for oh in range(2)]
                         + [("vt", tt * 2 + oh) for tt in range(4) for oh in range(2)]
                         + [("m1", hb) for hb in range(2)])

            for si, (kind, idx) in enumerate(qkv_specs):
                par = si % 2

                def t_qkv(t, kind=kind, idx=idx, par=par, si=si, h1_v=h1_v):
                    wtv(t, "tensor", "v", h1_v)
                    wt(t, "tensor", "gW")
                    if si >= 2:
                        wt(t, "tensor", "v")
                        wt(t, "tensor", "s")
                    i = None
                    if kind == "qf":
                        for ic in range(8):
                            i = t.matmul(psG[par][:, 0:512],
                                         wq_sb[:, ic * 1024 + idx * 128: ic * 1024 + (idx + 1) * 128],
                                         h1_sb[:, ic * 512:(ic + 1) * 512],
                                         start=(ic == 0), stop=(ic == 7))
                    elif kind == "kf":
                        for ic in range(8):
                            i = t.matmul(psG[par][:, 0:512],
                                         wk_sb[:, ic * 1024 + idx * 128: ic * 1024 + (idx + 1) * 128],
                                         h1_sb[:, ic * 512:(ic + 1) * 512],
                                         start=(ic == 0), stop=(ic == 7))
                    elif kind in ("kt", "vt"):
                        tt, oh = idx // 2, idx % 2
                        wsb = wk_sb if kind == "kt" else wv_sb
                        for ic in range(8):
                            i = t.matmul(psG[par][:, 0:512],
                                         h1_sb[:, ic * 512 + tt * 128: ic * 512 + (tt + 1) * 128],
                                         wsb[:, ic * 1024 + oh * 512: ic * 1024 + (oh + 1) * 512],
                                         start=(ic == 0), stop=(ic == 7))
                    else:
                        hb = idx
                        for ic in range(8):
                            i = t.matmul(psG[par][:, 0:512],
                                         wm1_sb[:, ic * 256 + hb * 128: ic * 256 + (hb + 1) * 128],
                                         h1_sb[:, ic * 512:(ic + 1) * 512],
                                         start=(ic == 0), stop=(ic == 7))
                    bump(i, "t")
                sec("tensor", t_qkv)

                if kind == "m1":
                    def s_m1(s, idx=idx, par=par):
                        wt(s, "scalar", "t")
                        sACT(s, m1b_sb[:, idx * 512:(idx + 1) * 512], psG[par][:, 0:512],
                             AF.Relu, bias=bm1_sb[:, idx:idx + 1], scale=ccol(C_ONE), inc="s")
                    sec("scalar", s_m1)
                else:
                    def v_ev(v, kind=kind, idx=idx, par=par):
                        wt(v, "vector", "t")
                        if kind == "qf":
                            dst = qf_sb[:, idx * 512:(idx + 1) * 512]
                        elif kind == "kf":
                            dst = kf_sb[:, idx * 512:(idx + 1) * 512]
                        else:
                            tt, oh = idx // 2, idx % 2
                            buf = ktok_sb if kind == "kt" else vtok_sb
                            dst = buf[:, tt * 1024 + oh * 512: tt * 1024 + (oh + 1) * 512]
                        vCP(v, dst, psG[par][:, 0:512])
                        if kind == "qf":
                            vTS(v, qfn_sb[:, idx * 512:(idx + 1) * 512],
                                psG[par][0:64, 0:512], -1.0, None, ALU.mult)
                        i = v.drain()
                        bump(i, "v")
                    sec("vector", v_ev)

            if s_i == 0:
                stage("p2_qkv")

            def t_z(t):
                wt(t, "tensor", "s")
                i = None
                for hb in range(2):
                    i = t.matmul(psC[0:16, 0:512],
                                 wm2_sb[:, hb * 16:(hb + 1) * 16],
                                 m1b_sb[:, hb * 512:(hb + 1) * 512],
                                 start=(hb == 0), stop=(hb == 1))
                bump(i, "t")
            sec("tensor", t_z)

            def s_z(s):
                wt(s, "scalar", "t", "v")
                sACT(s, e1x[:], psC[0:16, 0:512], AF.Exp, bias=bm2_sb[:], scale=c16(C_ONE))
                sACT(s, sp1[:], e1x[:], AF.Ln, bias=c16(C_ONE), scale=c16(C_ONE))
                sACT(s, e1x[:], psC[0:16, 0:512], AF.Exp, bias=nbm2_sb[:],
                     scale=consts[0:16, 4:5])
                sACT(s, sp2[:], e1x[:], AF.Ln, bias=c16(C_ONE), scale=c16(C_ONE), inc="s")
            sec("scalar", s_z)

            def v_dec(v, s_i=s_i):
                wt(v, "vector", "s")
                if s_i > 0:
                    wtv(v, "vector", "t", SN_mask[s_i * 32 - 1])
                vSCAN(v, P16[:], RST16[:], sp1[:], 0.0, ALU.mult, ALU.add)
                vTT(v, mw16[:], sp2[:], P16[:], ALU.subtract, inc="v")
            sec("vector", v_dec)
            dec_v = CNT["v"]
            if s_i == 0:
                stage("p2_z")

            for hp in range(8):
                par = hp % 2

                def t_el(t, hp=hp, par=par, dec_v=dec_v):
                    wtv(t, "tensor", "v", dec_v)
                    if hp >= 2:
                        wt(t, "tensor", "s")
                    i = t.matmul(psG[par][:, 0:512], PSel[:, hp * 128:(hp + 1) * 128],
                                 P16[:], start=True, stop=True)
                    bump(i, "t")
                sec("tensor", t_el)

                def s_el(s, hp=hp, par=par):
                    wt(s, "scalar", "t", "v")
                    sACT(s, ELp_sb[:, hp * 512:(hp + 1) * 512], psG[par][:, 0:512],
                         AF.Exp, bias=ccol(C_ZERO), scale=ccol(C_ONE))
                    sACT(s, PPall[:, hp * 4:(hp + 1) * 4], psG[par][:, 127:512:128],
                         AF.Exp, bias=ccol(C_ZERO), scale=ccol(C_ONE), inc="s")
                sec("scalar", s_el)
            elp_s = CNT["s"]
            if s_i == 0:
                stage("p2_el")

            for cin in range(4):
                gc = s_i * 4 + cin
                for hp in range(8):
                    i_g = glob
                    par = i_g % 2
                    pc = par * 256

                    def t_msk(t, i_g=i_g, hp=hp, cin=cin, pc=pc, dec_v=dec_v):
                        wtv(t, "tensor", "v", dec_v)
                        if i_g >= 2:
                            wtv(t, "tensor", "s", SN_exp[i_g - 2])
                            wtv(t, "tensor", "v", SN_ms[i_g - 2])
                        cs = slice(cin * 128, (cin + 1) * 128)
                        last = None
                        for h in range(2):
                            hm = 2 * hp + h
                            last = t.matmul(psA[:, pc + h * 128: pc + (h + 1) * 128],
                                     mw16[:, cs], NOH[:, hm * 128:(hm + 1) * 128],
                                     start=True, stop=(_MSK == 1))
                            if _MSK >= 2:
                                t.matmul(psA[:, pc + h * 128: pc + (h + 1) * 128],
                                         NOH[:, hm * 128:(hm + 1) * 128], P16[:, cs],
                                         start=False, stop=False)
                                last = t.matmul(psA[:, pc + h * 128: pc + (h + 1) * 128],
                                         IdNeg[:], U01[:], start=False, stop=True)
                        if _MSK >= 3:
                            sl = slice(hp * 512 + cin * 128, hp * 512 + (cin + 1) * 128)
                            t.matmul(psB[:, pc + 128: pc + 256],
                                     kf_sb[:, sl], qf_sb[:, sl],
                                     start=True, stop=False)
                            t.matmul(psB[:, pc + 128: pc + 256],
                                     kf_sb[0:64, sl], qfn_sb[0:64, sl],
                                     start=False, stop=True)
                            last = t.matmul(psB[:, pc: pc + 128],
                                            kf_sb[0:64, sl], qf_sb[0:64, sl],
                                            start=True, stop=True)
                        bump(last, "t")
                        SN_mask[i_g] = CNT["t"]
                    sec("tensor", t_msk)
                    if i_g == 0:
                        stage("c0_msk")

                    def s_exp(s, i_g=i_g, pc=pc):
                        wtv(s, "scalar", "t", SN_mask[i_g])
                        if i_g >= 2:
                            wtv(s, "scalar", "v", SN_ms[i_g - 2])
                        sACT(s, mask_sb[:, pc:pc + 256], psA[:, pc:pc + 256],
                             AF.Exp, bias=ccol(C_ZERO), scale=ccol(C_ONE), inc="s")
                        SN_exp[i_g] = CNT["s"]
                    sec("scalar", s_exp)
                    if i_g == 0:
                        stage("c0_exp")

                    def v_msqk(v, i_g=i_g, hp=hp, cin=cin, par=par, pc=pc, elp_s=elp_s):
                        wtv(v, "vector", "s", SN_exp[i_g])
                        wtv(v, "vector", "s", elp_s)
                        if i_g >= 2:
                            wtv(v, "vector", "t", SN_grp[i_g - 2])
                        for h in range(2):
                            vTT(v, ms_sb[:, pc + h * 128: pc + (h + 1) * 128],
                                psB[:, pc + h * 128: pc + (h + 1) * 128],
                                mask_sb[:, pc + h * 128: pc + (h + 1) * 128], ALU.mult)
                        sl = slice(hp * 512 + cin * 128, hp * 512 + (cin + 1) * 128)
                        vTT(v, qp_sb[:, par * 128:(par + 1) * 128],
                            qf_sb[:, sl], ELp_sb[:, sl], ALU.mult)
                        vTT(v, qpn_sb[:, par * 128:(par + 1) * 128],
                            qfn_sb[0:64, sl], ELp_sb[0:64, sl], ALU.mult)
                        for h in range(2):
                            hm = 2 * hp + h
                            vTS(v, kt2_sb[:, par * 128 + h * 64: par * 128 + (h + 1) * 64],
                                ktok_sb[:, cin * 1024 + hm * 64: cin * 1024 + (hm + 1) * 64],
                                mask_sb[:, pc + h * 128 + 127: pc + h * 128 + 128],
                                None, ALU.mult)
                        i = v.drain()
                        bump(i, "v")
                        SN_ms[i_g] = CNT["v"]
                    sec("vector", v_msqk)
                    if i_g == 0:
                        stage("c0_ms")

                    def t_grp(t, i_g=i_g, hp=hp, cin=cin, par=par, pc=pc):
                        wtv(t, "tensor", "v", SN_ms[i_g])
                        if i_g >= 8:
                            wtv(t, "tensor", "v", SN_new[i_g - 8])
                        if i_g >= 2:
                            wtv(t, "tensor", "v", SN_yev[i_g - 2])
                            wtv(t, "tensor", "v", SN_new[i_g - 2])
                        t.matmul(psY[0:64, par * 128:(par + 1) * 128],
                                 vtok_sb[:, cin * 1024 + 2 * hp * 64: cin * 1024 + (2 * hp + 1) * 64],
                                 ms_sb[:, pc: pc + 128],
                                 start=True, stop=False)
                        t.matmul(psY[0:64, par * 128:(par + 1) * 128],
                                 statesB[0:64, hp * 64:(hp + 1) * 64],
                                 qp_sb[0:64, par * 128:(par + 1) * 128],
                                 start=False, stop=True)
                        t.matmul(psY[64:128, par * 128:(par + 1) * 128],
                                 vtok_sb[:, cin * 1024 + (2 * hp + 1) * 64: cin * 1024 + (2 * hp + 2) * 64],
                                 ms_sb[:, pc + 128: pc + 256],
                                 start=True, stop=False, tile_position=(0, 64))
                        t.matmul(psY[64:128, par * 128:(par + 1) * 128],
                                 statesB[:, hp * 64:(hp + 1) * 64],
                                 qp_sb[:, par * 128:(par + 1) * 128],
                                 start=False, stop=False, tile_position=(0, 64))
                        t.matmul(psY[64:128, par * 128:(par + 1) * 128],
                                 statesB[0:64, hp * 64:(hp + 1) * 64],
                                 qpn_sb[0:64, par * 128:(par + 1) * 128],
                                 start=False, stop=True, tile_position=(0, 64))
                        last = None
                        for h in range(2):
                            hm = 2 * hp + h
                            tp = None if h == 0 else (0, 64)
                            last = t.matmul(
                                psS[h * 64:(h + 1) * 64, par * 128: par * 128 + 64],
                                kt2_sb[:, par * 128 + h * 64: par * 128 + (h + 1) * 64],
                                vtok_sb[:, cin * 1024 + hm * 64: cin * 1024 + (hm + 1) * 64],
                                start=True, stop=True, tile_position=tp)
                        bump(last, "t")
                        SN_grp[i_g] = CNT["t"]
                    sec("tensor", t_grp)
                    if i_g == 0:
                        stage("c0_grp")

                    def v_new(v, i_g=i_g, hp=hp, cin=cin, par=par):
                        wtv(v, "vector", "t", SN_grp[i_g])
                        vSTT(v, statesF[:, hp * 64:(hp + 1) * 64],
                             statesF[:, hp * 64:(hp + 1) * 64],
                             PPall[:, hp * 4 + cin: hp * 4 + cin + 1],
                             psS[:, par * 128: par * 128 + 64],
                             ALU.mult, ALU.add)
                        vCP(v, statesB[:, hp * 64:(hp + 1) * 64],
                            statesF[:, hp * 64:(hp + 1) * 64], inc="v")
                        SN_new[i_g] = CNT["v"]
                    sec("vector", v_new)

                    def v_yev(v, i_g=i_g, hp=hp, gc=gc, par=par):
                        wtv(v, "vector", "t", SN_grp[i_g])
                        vCP(v, y_sb[:, hp * 2048 + gc * 128: hp * 2048 + (gc + 1) * 128],
                            psY[:, par * 128:(par + 1) * 128], inc="v")
                        SN_yev[i_g] = CNT["v"]
                    sec("vector", v_yev)

                    for engx, fnx in p4b_sched[i_g]:
                        sec(engx, fnx)

                    glob += 1
                    if glob == 8:
                        stage("p2_c0")

        stage("p2")
        wq_o0 = CNT["oW0"]
        wq_o1 = CNT["oW1"]
        def g_wpj(g):
            wt(g, "gpsimd", "t")
            for hh in range(2):
                i = g.dma_start(
                    out=wpj_sb[:, hh * 4096:(hh + 1) * 4096],
                    in_=wpjT[hh * 512:(hh + 1) * 512, :].rearrange("(i p) o -> p i o", p=128))
                bump(i, "gW", 16)
        sec("gpsimd", g_wpj)

        for cc in range(8):
            for ts4 in range(4):
                si = cc * 4 + ts4
                par = si % 2

                def t_prj(t, cc=cc, ts4=ts4, par=par, si=si):
                    wt(t, "tensor", "gW")
                    wtv(t, "tensor", "v", SN_yev[NIT - 1])
                    if si >= 2:
                        wt(t, "tensor", "v")
                    i = None
                    for ic in range(8):
                        i = t.matmul(psG[par][:, 0:512],
                                     wpj_sb[:, ic * 1024 + cc * 128: ic * 1024 + (cc + 1) * 128],
                                     y_sb[:, ic * 2048 + ts4 * 512: ic * 2048 + (ts4 + 1) * 512],
                                     start=(ic == 0), stop=(ic == 7))
                    bump(i, "t")
                sec("tensor", t_prj)

                def v_prj(v, cc=cc, ts4=ts4, par=par):
                    wt(v, "vector", "t")
                    vCP(v, ob_sb[:, cc * 2048 + ts4 * 512: cc * 2048 + (ts4 + 1) * 512],
                        psG[par][:, 0:512], inc="v")
                sec("vector", v_prj)
        prj_v = CNT["v"]

        stage("p2b")
        THID_SNAP = []
        TMLP_SNAP = []
        for ts in range(4):
            def v_osq(v, ts=ts, prj_v=prj_v):
                wtv(v, "vector", "v", prj_v)
                if ts > 0:
                    wt(v, "vector", "t")
                for cc in range(8):
                    sl = slice(cc * 512, (cc + 1) * 512)
                    ol = slice(cc * 2048 + ts * 512, cc * 2048 + (ts + 1) * 512)
                    vTT(v, yn_sb[:, sl], ob_sb[:, ol], ob_sb[:, ol], ALU.mult)
                i = v.drain()
                bump(i, "v")
            sec("vector", v_osq)

            def t_rstat(t):
                wt(t, "tensor", "v")
                i = None
                for cc in range(8):
                    i = t.matmul(psC[:, 0:512], onesb[:], yn_sb[:, cc * 512:(cc + 1) * 512],
                                 start=(cc == 0), stop=(cc == 7))
                bump(i, "t")
            sec("tensor", t_rstat)

            def v_rst2(v):
                wt(v, "vector", "t")
                vTS(v, st3A[:], psC[:, 0:512], 1.0 / 1024.0, None, ALU.mult, inc="v")
            sec("vector", v_rst2)

            def s_rst(s):
                wt(s, "scalar", "v")
                sACT(s, st3C[:], st3A[:], AF.Ln, bias=ccol(C_EPSRMS), scale=ccol(C_ONE))
                sACT(s, st3B[:], st3C[:], AF.Exp, bias=ccol(C_ZERO), scale=ccol(C_MHALF), inc="s")
            sec("scalar", s_rst)

            def v_yn(v, ts=ts):
                wt(v, "vector", "s")
                for cc in range(8):
                    sl = slice(cc * 512, (cc + 1) * 512)
                    ol = slice(cc * 2048 + ts * 512, cc * 2048 + (ts + 1) * 512)
                    vTT(v, yn_sb[:, sl], ob_sb[:, ol], st3B[:], ALU.mult)
                i = v.drain()
                bump(i, "v")
            sec("vector", v_yn)
            yn_v = CNT["v"]

            for fc in range(32):
                par = fc % 2

                def g_wfc(eng, ts=ts, fc=fc, par=par):
                    q = "gpsimd" if par == 0 else "scalar"
                    if ts == 0 and fc < 2:
                        wtv(eng, q, "cv", CV_TOT)
                    gi = ts * 32 + fc
                    if gi >= 2:
                        wtv(eng, q, "t", THID_SNAP[gi - 2])
                    i = eng.dma_start(
                        out=wfc_t[:, par * 1024:(par + 1) * 1024],
                        in_=wfc_scr[:, fc * 1024:(fc + 1) * 1024])
                    bump(i, "gF0" if par == 0 else "gF1", 16)
                sec("gpsimd" if par == 0 else "scalar", g_wfc)

                def t_hid(t, fc=fc, par=par, yn_v=yn_v):
                    wt(t, "tensor", "gF0" if par == 0 else "gF1")
                    wtv(t, "tensor", "v", yn_v)
                    if fc >= 2:
                        wt(t, "tensor", "v", "s")
                    i = None
                    for ic in range(8):
                        i = t.matmul(psG[par][:, 0:512],
                                     wfc_t[:, par * 1024 + ic * 128: par * 1024 + (ic + 1) * 128],
                                     yn_sb[:, ic * 512:(ic + 1) * 512],
                                     start=(ic == 0), stop=(ic == 7))
                    bump(i, "t")
                    THID_SNAP.append(CNT["t"])
                sec("tensor", t_hid)

                def s_r2(s, fc=fc, par=par):
                    wt(s, "scalar", "t")
                    sACT(s, t1_sc[:, par * 512:(par + 1) * 512], psG[par][:, 0:512],
                         AF.Relu, bias=ccol(C_ZERO), scale=ccol(C_ONE), inc="s")
                sec("scalar", s_r2)

                def v_r2(v, fc=fc, par=par):
                    wt(v, "vector", "s")
                    vTT(v, relu2_sb[:, fc * 512:(fc + 1) * 512],
                        t1_sc[:, par * 512:(par + 1) * 512],
                        t1_sc[:, par * 512:(par + 1) * 512], ALU.mult, inc="v")
                sec("vector", v_r2)
            r2_v = CNT["v"]

            for ch in range(2):
                for fc in range(32):
                    parw = fc % 2

                    def g_wcp(eng, ts=ts, ch=ch, fc=fc, parw=parw):
                        q = "sync" if parw == 0 else "gpsimd"
                        if ts == 0 and ch == 0 and fc < 2:
                            wtv(eng, q, "cv", CV_TOT)
                        gi = (ts * 2 + ch) * 32 + fc
                        if gi >= 2:
                            wtv(eng, q, "t", TMLP_SNAP[gi - 2])
                        i = eng.dma_start(out=wcpt[:, parw * 512:(parw + 1) * 512],
                                          in_=wcp_scr[:, fc * 1024 + ch * 512:
                                                      fc * 1024 + (ch + 1) * 512])
                        bump(i, "gP0" if parw == 0 else "gP1", 16)
                    sec("sync" if parw == 0 else "gpsimd", g_wcp)

                    def t_mlp(t, ch=ch, fc=fc, parw=parw, r2_v=r2_v):
                        wtv(t, "tensor", "v", r2_v)
                        wt(t, "tensor", "gP0" if parw == 0 else "gP1")
                        if fc == 0:
                            wt(t, "tensor", "v")
                        i = None
                        for j in range(4):
                            i = t.matmul(psBANKS[j][:, 0:512],
                                         wcpt[:, parw * 512 + j * 128: parw * 512 + (j + 1) * 128],
                                         relu2_sb[:, fc * 512:(fc + 1) * 512],
                                         start=(fc == 0), stop=(fc == 31))
                        bump(i, "t")
                        TMLP_SNAP.append(CNT["t"])
                    sec("tensor", t_mlp)

                for j in range(4):
                    cc = ch * 4 + j
                    parx = (ts * 8 + cc) % 2

                    def hw_x3(sy, cc=cc, ts=ts, parx=parx):
                        wt(sy, "sync", "v")
                        i = sy.dma_start(out=x_sl3[:, parx * 512:(parx + 1) * 512],
                                         in_=xT[cc * 128:(cc + 1) * 128, ts * 512:(ts + 1) * 512])
                        bump(i, "h3", 16)
                    sec("sync", hw_x3)

                    def v_x2(v, j=j, cc=cc, ts=ts, parx=parx):
                        wt(v, "vector", "t", "h3")
                        ol = slice(cc * 2048 + ts * 512, cc * 2048 + (ts + 1) * 512)
                        vTT(v, t1_sc[:, parx * 512:(parx + 1) * 512],
                            psBANKS[j][:, 0:512], ob_sb[:, ol], ALU.add)
                        vSTT(v, x2_sb[:, ol],
                             t1_sc[:, parx * 512:(parx + 1) * 512], col_ga(cc),
                             x_sl3[:, parx * 512:(parx + 1) * 512],
                             ALU.mult, ALU.add, inc="v")
                    sec("vector", v_x2)
        x2_v = CNT["v"]

        stage("p3")
        for ts in range(4):
            def v_x2c(v, ts=ts, x2_v=x2_v):
                wtv(v, "vector", "v", x2_v)
                if ts > 0:
                    wt(v, "vector", "t")
                for cc in range(8):
                    sl = slice(cc * 512, (cc + 1) * 512)
                    ol = slice(cc * 2048 + ts * 512, cc * 2048 + (ts + 1) * 512)
                    vCP(v, xb4_sb[:, sl], x2_sb[:, ol])
                for cc in range(8):
                    sl = slice(cc * 512, (cc + 1) * 512)
                    vTT(v, xsq4_sb[:, sl], xb4_sb[:, sl], xb4_sb[:, sl], ALU.mult)
                i = v.drain()
                bump(i, "v")
            sec("vector", v_x2c)

            def t_st4(t):
                wt(t, "tensor", "v")
                for cc in range(8):
                    t.matmul(psC[:, 0:512], onesb[:], xb4_sb[:, cc * 512:(cc + 1) * 512],
                             start=(cc == 0), stop=(cc == 7))
                i = None
                for cc in range(8):
                    i = t.matmul(psD[:, 0:512], onesb[:], xsq4_sb[:, cc * 512:(cc + 1) * 512],
                                 start=(cc == 0), stop=(cc == 7))
                bump(i, "t")
            sec("tensor", t_st4)

            def v_st4(v):
                wt(v, "vector", "t")
                vTS(v, st3A[:], psC[:, 0:512], 1.0 / 1024.0, None, ALU.mult)
                vTS(v, st3B[:], psD[:, 0:512], 1.0 / 1024.0, None, ALU.mult)
                vTT(v, st3C[:], st3A[:], st3A[:], ALU.mult)
                vSTT(v, st3D[:], st3C[:], -1.0, st3B[:], ALU.mult, ALU.add, inc="v")
            sec("vector", v_st4)

            def s_st4(s):
                wt(s, "scalar", "v")
                sACT(s, st3C[:], st3D[:], AF.Ln, bias=ccol(C_EPSLN), scale=ccol(C_ONE))
                sACT(s, st3B[:], st3C[:], AF.Exp, bias=ccol(C_ZERO), scale=ccol(C_MHALF), inc="s")
            sec("scalar", s_st4)

            def v_h2(v, ts=ts):
                wt(v, "vector", "s")
                wt(v, "vector", "oW0", "oW1")
                for cc in range(8):
                    ol = slice(cc * 2048 + ts * 512, cc * 2048 + (ts + 1) * 512)
                    vTT(v, st3D[:], x2_sb[:, ol], st3A[:], ALU.subtract)
                    vTT(v, st3C[:], st3D[:], st3B[:], ALU.mult)
                    vTS(v, h2_sb[:, ol], st3C[:], col_cb1(cc), col_sb(cc), ALU.mult, ALU.add)
                o0 = ts * 512
                vMS(v, st3D[:], 0.0)
                for cc in range(8):
                    hsl = slice(cc * 2048 + o0, cc * 2048 + o0 + 512)
                    vTT(v, st3D[:], st3D[:], h2_sb[:, hsl], ALU.max)
                    vSTT(v, st3D[:], h2_sb[:, hsl], -1.0, st3D[:], ALU.mult, ALU.max)
                i = v.drain()
                bump(i, "v")
            sec("vector", v_h2)

            def t_tp4(t):
                wt(t, "tensor", "v")
                i = None
                for tt in range(4):
                    i = t.transpose(psD[:, tt * 128:(tt + 1) * 128],
                                    st3D[:, tt * 128:(tt + 1) * 128], diagf[:])
                bump(i, "t")
            sec("tensor", t_tp4)

            def v_rm1(v, ts=ts):
                wt(v, "vector", "t")
                for tt in range(4):
                    vRED(v, rmax1[:, ts * 4 + tt: ts * 4 + tt + 1],
                         psD[:, tt * 128:(tt + 1) * 128], ALU.max)
                i = v.drain()
                bump(i, "v")
            sec("vector", v_rm1)
        p4a_v = CNT["v"]

        stage("p4a")
        def v_s1(v, p4a_v=p4a_v):
            wtv(v, "vector", "v", p4a_v)
            wtv(v, "vector", "oW0", wq_o0)
            wtv(v, "vector", "oW1", wq_o1)
            vTS(v, sv_a[:], rmax1[:], 1.0e-5, None, ALU.max)
            vRCP(v, sv_b[:], sv_a[:])
            vTS(v, s1v[:], sv_b[:], 127.0, None, ALU.mult)
            vTS(v, sc1v[:], sv_a[:], msc_sc[:, 1:2], 1.0 / 127.0, ALU.mult, ALU.mult,
                inc="v")
        sec("vector", v_s1)

        def t_tps1(t):
            wt(t, "tensor", "v")
            t.transpose(psD[0:16, 0:128], s1v[:], diagf[:])
            i = t.transpose(psD[0:16, 128:256], sc1v[:], diagf[:])
            bump(i, "t")
        sec("tensor", t_tps1)

        def v_tps1(v):
            wt(v, "vector", "t")
            vCP(v, tposed[:], psD[0:16, 0:256], inc="v")
        sec("vector", v_tps1)

        def g_oht1(g):
            wt(g, "gpsimd", "v")
            i = g.iota(OHT[:], [[-1, 16], [0, 128]], base=0, channel_multiplier=1,
                       allow_small_or_imprecise_dtypes=True)
            bump(i, "g")
            g.drain()
        sec("gpsimd", g_oht1)

        def v_oht1(v):
            wt(v, "vector", "g")
            vTS(v, OHT[:], OHT[:], 0.0, None, ALU.is_equal, inc="v")
        sec("vector", v_oht1)

        for grp in range(4):
            def t_rp1(t, grp=grp):
                wt(t, "tensor", "v")
                for k in range(4):
                    tt = grp * 4 + k
                    t.matmul(psC[:, k * 128:(k + 1) * 128],
                             OHT[:, tt * 128:(tt + 1) * 128], tposed[:, 0:128],
                             start=True, stop=True)
                i = None
                for k in range(4):
                    tt = grp * 4 + k
                    i = t.matmul(psD[:, k * 128:(k + 1) * 128],
                                 OHT[:, tt * 128:(tt + 1) * 128], tposed[:, 128:256],
                                 start=True, stop=True)
                bump(i, "t")
            sec("tensor", t_rp1)

            def v_rp1(v, grp=grp):
                wt(v, "vector", "t")
                vCP(v, s1rep[:, grp * 512:(grp + 1) * 512], psC[:, 0:512])
                vCP(v, sc1rep[:, grp * 512:(grp + 1) * 512], psD[:, 0:512], inc="v")
            sec("vector", v_rp1)
        rep1_v = CNT["v"]

        def v_pad(v):
            wt(v, "vector", "t")
            for ic in range(8):
                vMS(v, xq1_sb[:, ic * 2050:ic * 2050 + 1], 0.0)
                vMS(v, xq1_sb[:, ic * 2050 + 2049:ic * 2050 + 2050], 0.0)
            i = v.drain()
            bump(i, "v")
        sec("vector", v_pad)

        for cc in range(8):

            def v_xq1(v, cc=cc, rep1_v=rep1_v):
                wtv(v, "vector", "v", rep1_v)
                ol = slice(cc * 2048, (cc + 1) * 2048)
                ladder_round(v, pq4, aq4, rq4, uq4,
                             h2_sb[:, ol], s1rep[:],
                             xq1_sb[:, cc * 2050 + 1: cc * 2050 + 2049],
                             sc1rep[:], inc="v")
            sec("vector", v_xq1)
        xq1_v = CNT["v"]

        stage("p4c")
        TU_SNAP = []
        UEV_SNAP = []
        TCV_SNAP = []
        for hf in range(2):
            def v_rz(v, hf=hf, xq1_v=xq1_v):
                wtv(v, "vector", "v", xq1_v)
                if hf > 0:
                    wt(v, "vector", "t")
                vMS(v, run2q[:], 0.0, inc="v")
            sec("vector", v_rz)

            for fc in range(32):
                par = fc % 2

                def hw_w1s(sy, hf=hf, fc=fc, par=par):
                    gi = hf * 32 + fc
                    wtv(sy, "sync", "oW0", wq_o0)
                    wtv(sy, "sync", "oW1", wq_o1)
                    if gi >= 2:
                        wtv(sy, "sync", "t", TU_SNAP[gi - 2])
                    else:
                        wt(sy, "sync", "t", "v")
                    i = sy.dma_start(out=w1t_sb[:, par * 1024:(par + 1) * 1024],
                                     in_=w1q_scr[:, fc * 1024:(fc + 1) * 1024])
                    bump(i, "h50" if par == 0 else "h51", 16)
                sec("sync", hw_w1s)

                def g_dia(g, hf=hf, fc=fc, par=par):
                    gi = hf * 32 + fc
                    if fc < 2:
                        if hf == 0:
                            wtv(g, "gpsimd", "v", xq1_v)
                        else:
                            wt(g, "gpsimd", "v")
                    if gi >= 2:
                        wtv(g, "gpsimd", "t", TCV_SNAP[gi - 2])
                    for j in range(3):
                        pTS(g, dia_sb[:, par * 384 + j * 128: par * 384 + (j + 1) * 128],
                            diagf[:], cwc_sb[:, fc * 3 + j: fc * 3 + j + 1], None, ALU.mult)
                    i = g.nop()
                    bump(i, "g")
                sec("gpsimd", g_dia)

                uG0 = psG0 if par == 0 else psY
                uG1 = psG1 if par == 0 else psS
                uC = psC if par == 0 else psD

                def t_u(t, hf=hf, fc=fc, par=par, uG0=uG0, uG1=uG1, uC=uC):
                    gi = hf * 32 + fc
                    wt(t, "tensor", "h50" if par == 0 else "h51")
                    if gi >= 2:
                        wtv(t, "tensor", "v", UEV_SNAP[gi - 2])
                    else:
                        wtv(t, "tensor", "v", xq1_v)
                        wt(t, "tensor", "v")
                    for ic in range(8):
                        base = ic * 2050 + hf * 1024
                        t.matmul(uG0[:, 0:512],
                                 w1t_sb[:, par * 1024 + ic * 128: par * 1024 + (ic + 1) * 128],
                                 xq1_sb[:, base: base + 512],
                                 start=(ic == 0), stop=(ic == 7))
                        t.matmul(uG1[:, 0:512],
                                 w1t_sb[:, par * 1024 + ic * 128: par * 1024 + (ic + 1) * 128],
                                 xq1_sb[:, base + 512: base + 1024],
                                 start=(ic == 0), stop=(ic == 7))
                    i = None
                    for ic in range(8):
                        base = ic * 2050 + hf * 1024
                        i = t.matmul(uC[:, 0:2],
                                     w1t_sb[:, par * 1024 + ic * 128: par * 1024 + (ic + 1) * 128],
                                     xq1_sb[:, base + 1024: base + 1026],
                                     start=(ic == 0), stop=(ic == 7))
                    bump(i, "t")
                    TU_SNAP.append(CNT["t"])
                sec("tensor", t_u)

                def v_uev(v, par=par, uG0=uG0, uG1=uG1, uC=uC):
                    wt(v, "vector", "t")
                    wtv(v, "vector", "oW0", wq_o0)
                    wtv(v, "vector", "oW1", wq_o1)
                    vCP(v, u_sb[:, par * 1026 + 0: par * 1026 + 512], uG0[:, 0:512])
                    vCP(v, u_sb[:, par * 1026 + 512: par * 1026 + 1024], uG1[:, 0:512])
                    vCP(v, u_sb[:, par * 1026 + 1024: par * 1026 + 1026], uC[:, 0:2],
                        inc="v")
                    UEV_SNAP.append(CNT["v"])
                sec("vector", v_uev)

                def t_cv(t, fc=fc, par=par):
                    wt(t, "tensor", "v", "s", "g")
                    i = None
                    for oh in range(2):
                        pb = psA if oh == 0 else psB
                        for j in range(3):
                            i = t.matmul(pb[:, 0:512],
                                         dia_sb[:, par * 384 + j * 128: par * 384 + (j + 1) * 128],
                                         u_sb[:, par * 1026 + oh * 512 + j:
                                              par * 1026 + oh * 512 + j + 512],
                                         start=(j == 0), stop=(j == 2))
                    bump(i, "t")
                    TCV_SNAP.append(CNT["t"])
                sec("tensor", t_cv)

                def s_si(s, fc=fc):
                    wt(s, "scalar", "t")
                    sACT(s, silu_sb[:, fc * 512: (fc + 1) * 512], psA[:, 0:512],
                         AF.Silu, bias=cbc_sb[:, fc:fc + 1], scale=ccol(C_ONE))
                    sACT(s, silu_sb[:, 16384 + fc * 512: 16384 + (fc + 1) * 512],
                         psB[:, 0:512],
                         AF.Silu, bias=cbc_sb[:, fc:fc + 1], scale=ccol(C_ONE), inc="s")
                sec("scalar", s_si)

                if fc % 2 == 1:
                    def v_cm(v, fc=fc):
                        wt(v, "vector", "s")
                        for qq in range(2):
                            seg = slice(qq * 16384 + (fc - 1) * 512,
                                        qq * 16384 + (fc + 1) * 512)
                            acc = slice(qq * 1024, (qq + 1) * 1024)
                            vTT(v, run2q[:, acc], run2q[:, acc], silu_sb[:, seg],
                                ALU.max)
                            vSTT(v, run2q[:, acc], silu_sb[:, seg], -1.0,
                                 run2q[:, acc], ALU.mult, ALU.max)
                        i = v.drain()
                        bump(i, "v")
                    sec("vector", v_cm)
            cm_v = CNT["v"]

            def v_r2f(v, cm_v=cm_v):
                wtv(v, "vector", "v", cm_v)
                for qq in range(2):
                    vTT(v, run2f[:, qq * 512:(qq + 1) * 512],
                        run2q[:, qq * 1024: qq * 1024 + 512],
                        run2q[:, qq * 1024 + 512: (qq + 1) * 1024], ALU.max)
                i = v.drain()
                bump(i, "v")
            sec("vector", v_r2f)

            for wv_i in range(2):
                def t_tp5(t, wv_i=wv_i):
                    wt(t, "tensor", "v")
                    i = None
                    for k in range(4):
                        i = t.transpose(psD[:, k * 128:(k + 1) * 128],
                                        run2f[:, (wv_i * 4 + k) * 128:(wv_i * 4 + k + 1) * 128],
                                        diagf[:])
                    bump(i, "t")
                sec("tensor", t_tp5)

                def v_rm2(v, wv_i=wv_i):
                    wt(v, "vector", "t")
                    for k in range(4):
                        vRED(v, rmax2[:, wv_i * 4 + k: wv_i * 4 + k + 1],
                             psD[:, k * 128:(k + 1) * 128], ALU.max)
                    i = v.drain()
                    bump(i, "v")
                sec("vector", v_rm2)

            def v_s2(v):
                wt(v, "vector", "t")
                vTS(v, sv_a[:, 0:8], rmax2[:], 1.0e-5, None, ALU.max)
                vRCP(v, sv_b[:, 0:8], sv_a[:, 0:8])
                vTS(v, s1v[:, 0:8], sv_b[:, 0:8], 127.0, None, ALU.mult)
                vTS(v, sc1v[:, 0:8], sv_a[:, 0:8], msc_sc[:, 3:4], 1.0 / 127.0,
                    ALU.mult, ALU.mult, inc="v")
            sec("vector", v_s2)

            def t_tps2(t):
                wt(t, "tensor", "v")
                t.transpose(psD[0:8, 0:128], s1v[:, 0:8], diagf[:])
                i = t.transpose(psD[0:8, 128:256], sc1v[:, 0:8], diagf[:])
                bump(i, "t")
            sec("tensor", t_tps2)

            def v_tps2(v):
                wt(v, "vector", "t")
                vCP(v, tposed[0:8, 0:256], psD[0:8, 0:256], inc="v")
            sec("vector", v_tps2)

            def g_oht2(g):
                wt(g, "gpsimd", "v", "t")
                i = g.iota(OHT[:], [[-1, 16], [0, 128]], base=0, channel_multiplier=1,
                           allow_small_or_imprecise_dtypes=True)
                bump(i, "g")
                g.drain()
            sec("gpsimd", g_oht2)

            def v_oht2(v):
                wt(v, "vector", "g")
                vTS(v, OHT[:], OHT[:], 0.0, None, ALU.is_equal, inc="v")
            sec("vector", v_oht2)

            for grp in range(2):
                def t_rp2(t, grp=grp):
                    wt(t, "tensor", "v")
                    for k in range(4):
                        tt = grp * 4 + k
                        t.matmul(psC[:, k * 128:(k + 1) * 128],
                                 OHT[:, tt * 128:(tt + 1) * 128], tposed[0:16, 0:128],
                                 start=True, stop=True)
                    i = None
                    for k in range(4):
                        tt = grp * 4 + k
                        i = t.matmul(psD[:, k * 128:(k + 1) * 128],
                                     OHT[:, tt * 128:(tt + 1) * 128], tposed[0:16, 128:256],
                                     start=True, stop=True)
                    bump(i, "t")
                sec("tensor", t_rp2)

                def v_rp2(v, grp=grp):
                    wt(v, "vector", "t")
                    vCP(v, s2rep[:, grp * 512:(grp + 1) * 512], psC[:, 0:512])
                    vCP(v, sc2rep[:, grp * 512:(grp + 1) * 512], psD[:, 0:512], inc="v")
                sec("vector", v_rp2)
            rep2_v = CNT["v"]

            for q in range(2):
                ts_g = hf * 2 + q

                def v_s2d(v, q=q, rep2_v=rep2_v):
                    wtv(v, "vector", "v", rep2_v)
                    qsl = slice(q * 512, (q + 1) * 512)
                    vCP(v, s2d[:, 0:512], s2rep[:, qsl])
                    vCP(v, s2d[:, 512:1024], s2rep[:, qsl])
                    vCP(v, sc2d[:, 0:512], sc2rep[:, qsl])
                    vCP(v, sc2d[:, 512:1024], sc2rep[:, qsl], inc="v")
                sec("vector", v_s2d)

                TSNAP = []
                for fcp in range(16):

                    def v_xq2(v, q=q, fcp=fcp, TSNAP=TSNAP):
                        if fcp < 2:
                            wt(v, "vector", "t")
                        else:
                            wtv(v, "vector", "t", TSNAP[2 * fcp - 3])
                        base = (fcp % 2) * 1024
                        ladder_round(v, pq5, aq5, rq5, uq5,
                                     silu_sb[:, q * 16384 + fcp * 1024:
                                             q * 16384 + (fcp + 1) * 1024],
                                     s2d[:], xq2_sc[:, base: base + 1024],
                                     sc2d[:], inc="v")
                    sec("vector", v_xq2)

                    for hx in range(2):
                        fc = fcp * 2 + hx
                        par5 = fc % 2

                        def hw_w2s(sy, fc=fc, par5=par5, TSNAP=TSNAP):
                            if fc >= 2:
                                wtv(sy, "sync", "t", TSNAP[fc - 2])
                            else:
                                wt(sy, "sync", "t")
                            i = sy.dma_start(out=w2t_sb[:, par5 * 1024:(par5 + 1) * 1024],
                                             in_=w2q_scr[:, fc * 1024:(fc + 1) * 1024])
                            bump(i, "h50" if par5 == 0 else "h51", 16)
                        sec("sync", hw_w2s)

                        def t_w2(t, fc=fc, fcp=fcp, par5=par5, TSNAP=TSNAP):
                            wt(t, "tensor", "v", "h50" if par5 == 0 else "h51")
                            xbase = (fcp % 2) * 1024 + (fc % 2) * 512
                            i = None
                            for ccx in range(8):
                                i = t.matmul(psBANKS[ccx][:, 0:512],
                                             w2t_sb[:, par5 * 1024 + ccx * 128:
                                                    par5 * 1024 + (ccx + 1) * 128],
                                             xq2_sc[:, xbase: xbase + 512],
                                             start=(fc == 0), stop=(fc == 31))
                            bump(i, "t")
                            TSNAP.append(CNT["t"])
                        sec("tensor", t_w2)

                for ccx in range(8):
                    parO = ccx % 2

                    def v_out(v, ccx=ccx, ts_g=ts_g, parO=parO):
                        wt(v, "vector", "t")
                        if ccx >= 2 or ts_g > 0:
                            wt(v, "vector", "oO0" if parO == 0 else "oO1")
                        ol = slice(ccx * 2048 + ts_g * 512, ccx * 2048 + (ts_g + 1) * 512)
                        vSTT(v, outb[:, parO * 512:(parO + 1) * 512],
                             psBANKS[ccx][:, 0:512], col_gb(ccx), x2_sb[:, ol],
                             ALU.mult, ALU.add, inc="v")
                    sec("vector", v_out)

                    def hw_out(sy, ccx=ccx, ts_g=ts_g, parO=parO):
                        wt(sy, "sync", "v")
                        i = sy.dma_start(out=out[ccx * 128:(ccx + 1) * 128,
                                                 ts_g * 512:(ts_g + 1) * 512],
                                         in_=outb[:, parO * 512:(parO + 1) * 512])
                        bump(i, "oO0" if parO == 0 else "oO1", 16)
                    sec("sync", hw_out)

        def hw_fin(sy):
            sy.wait_ge(SEM["oO0"], CNT["oO0"])
            sy.wait_ge(SEM["oO1"], CNT["oO1"])
        sec("sync", hw_fin)

    return nc


_CACHE = {}


def _build_quiet():
    import os, sys
    sys.stdout.flush(); sys.stderr.flush()
    saved = (os.dup(1), os.dup(2))
    devnull = os.open(os.devnull, os.O_WRONLY)
    try:
        os.dup2(devnull, 1)
        os.dup2(devnull, 2)
        return build_nc()
    finally:
        os.dup2(saved[0], 1)
        os.dup2(saved[1], 2)
        os.close(saved[0]); os.close(saved[1]); os.close(devnull)


def kernel(**inputs):
    key = "nc"
    if key not in _CACHE:
        _CACHE[key] = _build_quiet()
    nc = _CACHE[key]

    f32 = np.float32

    def ct(a):
        return np.ascontiguousarray(a, dtype=f32)

    x = ct(inputs["L0_attn_x"])
    Wq = ct(inputs["L0_attn_Wq"])
    Wk = ct(inputs["L0_attn_Wk"])
    Wv = ct(inputs["L0_attn_Wv"])
    Wm1 = ct(inputs["L0_attn_Wm1"])
    bm1 = ct(inputs["L0_attn_bm1"])
    Wm2 = ct(inputs["L0_attn_Wm2"])
    bm2 = ct(inputs["L0_attn_bm2"])
    Wproj = ct(inputs["L0_attn_Wproj"])
    Wfc = ct(inputs["L0_attn_Wfc"])
    Wcp = ct(inputs["L0_attn_Wcp"])
    w1 = ct(inputs["L0_ffn_w1"])
    conv_w = ct(inputs["L0_ffn_conv_w"])
    conv_b = ct(inputs["L0_ffn_conv_b"])
    w2 = ct(inputs["L0_ffn_w2"])
    cvec = ct(inputs["L0_c"])
    wmod = ct(inputs["L0_wmod"])

    com = {
        "wqT": np.ascontiguousarray(Wq.T),
        "wkT": np.ascontiguousarray(Wk.T),
        "wvT": np.ascontiguousarray(Wv.T),
        "wpjT": np.ascontiguousarray(Wproj.T),
        "wm1T": np.ascontiguousarray(Wm1.T),
        "wm2T": np.ascontiguousarray(Wm2.T),
        "bm1c": np.ascontiguousarray(bm1.reshape(2, 128).T),
        "bm2c": np.ascontiguousarray(bm2.reshape(16, 1)),
        "wfcT": np.ascontiguousarray(Wfc.T),
        "wcpT": np.ascontiguousarray(Wcp.T),
        "w1T": np.ascontiguousarray(w1.T),
        "w2T": np.ascontiguousarray(w2.T),
        "cwc": np.ascontiguousarray(
            conv_w.reshape(32, 128, 3).transpose(1, 0, 2).reshape(128, 96)),
        "cbc": np.ascontiguousarray(conv_b.reshape(32, 128).T),
        "wmod": wmod,
    }
    in_maps = []
    for b in range(8):
        m = dict(com)
        m["xT"] = np.ascontiguousarray(x[b].T)
        m["c_col"] = np.ascontiguousarray(cvec[b].reshape(8, 128).T)
        in_maps.append(m)

    res = run_bass_kernel_spmd(nc, in_maps, core_ids=list(range(8)))
    outs = [np.ascontiguousarray(np.asarray(r["out"]).T) for r in res.results]
    return np.stack(outs, axis=0)


# version 14
# speedup vs baseline: 1.1673x; 1.0056x over previous
import numpy as np
import ml_dtypes
import concourse.bass as bass
import concourse.mybir as mybir
from concourse.bass_utils import run_bass_kernel_spmd
from contextlib import ExitStack

F32 = mybir.dt.float32
BF16 = mybir.dt.bfloat16
AF = mybir.ActivationFunctionType
ALU = mybir.AluOpType
AX = mybir.AxisListType

T = 2048
C = 1024
H = 16
HID = 256
FF = 4096
SLAB = 512
EPS_LN = 1.0e-5
EPS_RMS = float(np.finfo(np.float32).eps)
KB = 1024

_STOP = None
_MSK = 3


def build_nc():
    nc = bass.Bass("TRN2", target_bir_lowering=False)

    def din(name, shape, dt=F32):
        return nc.dram_tensor(name, shape, dt, kind="ExternalInput").ap()

    xT = din("xT", [C, T])
    wqT = din("wqT", [C, C])
    wkT = din("wkT", [C, C])
    wvT = din("wvT", [C, C])
    wpjT = din("wpjT", [C, C])
    wm1T = din("wm1T", [C, HID])
    wm2T = din("wm2T", [HID, H])
    bm1c = din("bm1c", [128, 2])
    bm2c = din("bm2c", [16, 1])
    wfcT = din("wfcT", [C, FF])
    wcpT = din("wcpT", [FF, C])
    w1T = din("w1T", [C, FF])
    w2T = din("w2T", [FF, C])
    cwc = din("cwc", [128, 96])
    cbc = din("cbc", [128, 32])
    c_col = din("c_col", [128, 8])
    wmod = din("wmod", [C, 6 * C])

    out = nc.dram_tensor("out", [C, T], F32, kind="ExternalOutput").ap()
    w1q_scr = nc.dram_tensor("w1q_scr", [128, 8 * 4096], BF16, kind="Internal").ap()
    w2q_scr = nc.dram_tensor("w2q_scr", [128, 8 * 4096], BF16, kind="Internal").ap()
    wfc_scr = nc.dram_tensor("wfc_scr", [128, 32 * 1024], BF16, kind="Internal").ap()
    wcp_scr = nc.dram_tensor("wcp_scr", [128, 64 * 512], BF16, kind="Internal").ap()

    es = ExitStack()
    with es:
        R1 = 0
        R2 = 64 * KB
        R3 = 128 * KB
        W0 = R3 + 33 * KB
        ARENA = W0 + 32 * KB

        back = es.enter_context(nc.sbuf_tensor("ARENA", [128, ARENA], mybir.dt.uint8))
        base_addr = nc.lookup_mloc(back).addr

        def place(name, shape, dtype, off, base_partition=0):
            h = nc.virtual_sbuf_tensor(name, shape, dtype)
            if base_partition:
                h.base_partition = base_partition
            nc._tensor(h.name, shape, dtype, type="SB")
            mloc = nc.lookup_mloc(h)
            mloc.allocated = True
            mloc.addr = base_addr + off
            return h

        wq_sb = place("wq_sb", [128, 8 * C], BF16, R1 + 0)
        wk_sb = place("wk_sb", [128, 8 * C], BF16, R1 + 16 * KB)
        wv_sb = place("wv_sb", [128, 8 * C], BF16, R1 + 32 * KB)
        wm1_sb = place("wm1_sb", [128, 2048], BF16, R1 + 48 * KB)
        wm2_sb = place("wm2_sb", [128, 32], BF16, R1 + 52 * KB)
        ELp_sb = place("ELp_sb", [128, 8 * 512], BF16, R1 + 53 * KB)
        wpj_sb = place("wpj_sb", [128, 8 * C], BF16, R1 + 0)
        x2_sb = place("x2_sb", [128, 8 * T], F32, R1 + 0)

        wmod_sb = place("wmod_sb", [128, 2 * 6144], BF16, R2 + 0)
        x_sl = place("x_sl", [128, 8 * 512], F32, R2 + 0)
        h1_sb = place("h1_sb", [128, 8 * 512], BF16, R2 + 16 * KB)
        qf_sb = place("qf_sb", [128, 8 * 512], BF16, R2 + 24 * KB)
        xb_sb = place("xb_sb", [128, 8 * 512], BF16, R2 + 24 * KB)
        kf_sb = place("kf_sb", [128, 8 * 512], BF16, R2 + 32 * KB)
        xsq_sb = place("xsq_sb", [128, 8 * 512], BF16, R2 + 32 * KB)
        ktok_sb = place("ktok_sb", [128, 4 * C], BF16, R2 + 40 * KB)
        vtok_sb = place("vtok_sb", [128, 4 * C], BF16, R2 + 48 * KB)
        m1b_sb = place("m1b_sb", [128, 1024], BF16, R2 + 56 * KB)
        st2A = place("st2A", [128, 512], F32, R2 + 58 * KB)
        st2B = place("st2B", [128, 512], F32, R2 + 60 * KB)
        st2C = place("st2C", [128, 512], F32, R2 + 62 * KB)
        ob_sb = place("ob_sb", [128, 8 * T], BF16, R2 + 0)
        h2_sb = place("h2_sb", [128, 8 * T], F32, R2 + 0)
        silu_sb = place("silu_sb", [128, 32 * 1024], BF16, R2 + 0)

        y_sb = place("y_sb", [128, 8 * T], BF16, R3 + 0)
        relu2_sb = place("relu2_sb", [128, 32 * 512], BF16, R3 + 0)
        xq1_sb = place("xq1_sb", [128, 8 * 2050], BF16, R3 + 0)

        qfn_sb = place("qfn_sb", [64, 8 * 512], BF16, W0 + 0)
        NOH = place("NOH", [16, 16 * 128], F32, W0 + 8 * KB)
        PSel = place("PSel", [16, 8 * 128], F32, W0 + 16 * KB)
        e1x = place("e1x", [16, 512], F32, W0 + 20 * KB)
        sp1 = place("sp1", [16, 512], F32, W0 + 22 * KB)
        sp2 = place("sp2", [16, 512], F32, W0 + 24 * KB)
        P16 = place("P16", [16, 512], F32, W0 + 26 * KB)
        mw16 = place("mw16", [16, 512], F32, W0 + 28 * KB)
        RST16 = place("RST16", [16, 512], F32, W0 + 30 * KB)
        yn_sb = place("yn_sb", [128, 8 * 512], BF16, W0 + 0)
        wfc_t = place("wfc_t", [128, 2 * 1024], BF16, W0 + 8 * KB)
        x_sl3 = place("x_sl3", [128, 2 * 512], F32, W0 + 12 * KB)
        t1_sc = place("t1_sc", [128, 2 * 512], F32, W0 + 16 * KB)
        wcpt = place("wcpt", [128, 2 * 512], BF16, W0 + 20 * KB)
        st3A = place("st3A", [128, 512], F32, W0 + 22 * KB)
        st3B = place("st3B", [128, 512], F32, W0 + 24 * KB)
        st3C = place("st3C", [128, 512], F32, W0 + 26 * KB)
        st3D = place("st3D", [128, 512], F32, W0 + 28 * KB)
        xb4_sb = place("xb4_sb", [128, 8 * 512], BF16, W0 + 0)
        xsq4_sb = place("xsq4_sb", [128, 8 * 512], BF16, W0 + 8 * KB)
        wfT_sb = place("wfT_sb", [128, 512], F32, R2 + 58 * KB)
        wqt_sb = place("wqt_sb", [128, 2 * 512], BF16, R2 + 60 * KB)
        stW_sb = place("stW_sb", [128, 512], BF16, R2 + 62 * KB)
        pq4 = place("pq4", [128, 2048], BF16, W0 + 0)
        aq4 = place("aq4", [128, 2048], BF16, W0 + 4 * KB)
        rq4 = place("rq4", [128, 2048], BF16, W0 + 8 * KB)
        uq4 = place("uq4", [128, 2048], BF16, W0 + 12 * KB)
        OHT = place("OHT", [16, 16 * 128], F32, W0 + 10 * KB, 0)
        s1rep = place("s1rep", [128, T], F32, W0 + 18 * KB)
        sc1rep = place("sc1rep", [128, T], BF16, W0 + 26 * KB)
        outb = place("outb", [128, 2 * 512], F32, W0 + 0)
        u_sb = place("u_sb", [128, 2 * 1026], BF16, W0 + 4 * KB + 256)
        w1t_sb = place("w1t_sb", [128, 2 * 1024], BF16, W0 + 8 * KB + 512)
        dia_sb = place("dia_sb", [128, 2 * 384], BF16, W0 + 12 * KB + 768)
        run2q = place("run2q", [128, 2048], BF16, W0 + 19 * KB)
        run2f = place("run2f", [128, 1024], F32, W0 + 10 * KB)
        s2rep = place("s2rep", [128, 1024], BF16, W0 + 23 * KB)
        sc2rep = place("sc2rep", [128, 1024], BF16, W0 + 25 * KB)
        pq5 = place("pq5", [128, 1024], BF16, W0 + 10 * KB)
        aq5 = place("aq5", [128, 1024], BF16, W0 + 12 * KB)
        rq5 = place("rq5", [128, 1024], BF16, W0 + 14 * KB)
        uq5 = place("uq5", [128, 1024], BF16, W0 + 16 * KB)
        s2d = place("s2d", [128, 1024], BF16, W0 + 27 * KB)
        sc2d = place("sc2d", [128, 1024], BF16, W0 + 29 * KB)
        xq2_sc = place("xq2_sc", [128, 4 * 512], BF16, W0 + 19 * KB)
        w2t_sb = place("w2t_sb", [128, 2 * 1024], BF16, W0 + 4 * KB + 256)

        def sbt(name, shape, dtype):
            return es.enter_context(nc.sbuf_tensor(name, shape, dtype))

        consts = sbt("consts", [128, 8], F32)
        onesb = sbt("onesb", [128, 128], BF16)
        onesf = sbt("onesf", [1, 128], F32)
        diagf = sbt("diagf", [128, 128], F32)
        U01 = sbt("U01", [128, 128], F32)
        IdNeg = sbt("IdNeg", [128, 128], F32)
        statesF = sbt("statesF", [128, 8 * 64], F32)
        statesB = sbt("statesB", [128, 8 * 64], BF16)
        PPall = sbt("PPall", [128, 32], F32)
        modtab = sbt("modtab", [128, 48], F32)
        modfin = sbt("modfin", [128, 48], F32)
        modsc = sbt("modsc", [128, 2], F32)
        ccol_sb = sbt("ccol_sb", [128, 8], F32)
        csb = sbt("csb", [128, 8], BF16)
        bm1_sb = sbt("bm1_sb", [128, 2], F32)
        bm2_sb = sbt("bm2_sb", [16, 1], F32)
        nbm2_sb = sbt("nbm2_sb", [16, 1], F32)
        cwc_sb = sbt("cwc_sb", [128, 96], F32)
        cbc_sb = sbt("cbc_sb", [128, 32], F32)
        stD = sbt("stD", [128, 512], F32)
        mask_sb = sbt("mask_sb", [128, 512], F32)
        ms_sb = sbt("ms_sb", [128, 512], BF16)
        qp_sb = sbt("qp_sb", [128, 256], BF16)
        qpn_sb = sbt("qpn_sb", [64, 256], BF16)
        kt2_sb = sbt("kt2_sb", [128, 256], BF16)
        rmax1 = sbt("rmax1", [128, 16], F32)
        rmax2 = sbt("rmax2", [128, 8], F32)
        sv_a = sbt("sv_a", [128, 16], F32)
        sv_b = sbt("sv_b", [128, 16], F32)
        s1v = sbt("s1v", [128, 16], F32)
        sc1v = sbt("sc1v", [128, 16], F32)
        tposed = sbt("tposed", [16, 256], F32)
        acc_w = sbt("acc_w", [128, 1], F32)
        tmpc = sbt("tmpc", [128, 1], F32)
        onecf = sbt("onecf", [128, 1], F32)
        msc_sc = sbt("msc_sc", [128, 4], F32)
        thn_sc = sbt("thn_sc", [128, 2], F32)
        sc11 = sbt("sc11", [1, 2], F32)

        psA = es.enter_context(nc.psum_tensor("psA", [128, 512], F32))
        psB = es.enter_context(nc.psum_tensor("psB", [128, 512], F32))
        psY = es.enter_context(nc.psum_tensor("psY", [128, 512], F32))
        psS = es.enter_context(nc.psum_tensor("psS", [128, 512], F32))
        psG0 = es.enter_context(nc.psum_tensor("psG0", [128, 512], F32))
        psG1 = es.enter_context(nc.psum_tensor("psG1", [128, 512], F32))
        psC = es.enter_context(nc.psum_tensor("psC", [128, 512], F32))
        psD = es.enter_context(nc.psum_tensor("psD", [128, 512], F32))
        psG = [psG0, psG1]
        psBANKS = [psA, psB, psY, psS, psG0, psG1, psC, psD]

        sem_names = ["hS", "hX", "h3", "hW0", "hW1", "oW0", "oW1",
                     "h50", "h51", "oO0", "oO1", "gW", "gM0", "gM1", "gF0", "gF1",
                     "gP0", "gP1", "cv", "t", "v", "s", "g"]
        SEM = {n: es.enter_context(nc.semaphore("sem_" + n)) for n in sem_names}
        CNT = {n: 0 for n in sem_names}
        WAITED = {}

        block = es.enter_context(nc.Block())

        def wt(eng, ename, *keys):
            for k in keys:
                tgt = CNT[k]
                if tgt and WAITED.get((ename, k), 0) < tgt:
                    eng.wait_ge(SEM[k], tgt)
                    WAITED[(ename, k)] = tgt

        def wtv(eng, ename, k, tgt):
            if tgt and WAITED.get((ename, k), 0) < tgt:
                eng.wait_ge(SEM[k], tgt)
                WAITED[(ename, k)] = tgt

        def bump(instr, key, n=1):
            instr.then_inc(SEM[key], n)
            CNT[key] += n

        stopped = [False]

        def sec(ename, fn):
            if stopped[0]:
                return
            getattr(block, ename)(fn)

        def vTT(v, o, a, b, op, inc=None):
            i = v.tensor_tensor(o, a, b, op)
            if inc: bump(i, inc)
            v.drain()

        def vTS(v, o, a, s1, s2, op0, op1=None, inc=None):
            if op1 is None:
                i = v.tensor_scalar(o, a, s1, None, op0)
            else:
                i = v.tensor_scalar(o, a, s1, s2, op0, op1)
            if inc: bump(i, inc)
            v.drain()

        def vSTT(v, o, a, s, b, op0, op1, inc=None):
            i = v.scalar_tensor_tensor(o, a, s, b, op0, op1)
            if inc: bump(i, inc)
            v.drain()

        def vCP(v, o, a, inc=None):
            i = v.tensor_copy(o, a)
            if inc: bump(i, inc)
            v.drain()

        def vMS(v, o, val, inc=None):
            i = v.memset(o, val)
            if inc: bump(i, inc)
            v.drain()

        def vRED(v, o, a, op, absval=None, inc=None):
            i = v.tensor_reduce(o, a, axis=AX.X, op=op, apply_absolute_value=absval)
            if inc: bump(i, inc)
            v.drain()

        def vRCP(v, o, a, inc=None):
            i = v.reciprocal(o, a)
            if inc: bump(i, inc)
            v.drain()

        def vSCAN(v, o, d0, d1, init, op0, op1, inc=None):
            i = v.tensor_tensor_scan(o, d0, d1, init, op0, op1)
            if inc: bump(i, inc)
            v.drain()

        def sACT(s, o, a, func, bias, scale, inc=None):
            i = s.activation(o, a, func, bias=bias, scale=scale)
            if inc: bump(i, inc)
            s.drain()

        def pTS(g, o, a, s1, s2, op0, op1=None, inc=None):
            if op1 is None:
                i = g.tensor_scalar(o, a, s1, None, op0)
            else:
                i = g.tensor_scalar(o, a, s1, s2, op0, op1)
            if inc: bump(i, inc)
            g.drain()

        def ladder_round(v, pq, aq, rq, uq, src_a, src_b, dest, scAP, inc=None):
            vTT(v, pq[:], src_a, src_b, ALU.mult)
            vTS(v, rq[:], pq[:], -1.0, None, ALU.mult)
            vTT(v, aq[:], rq[:], pq[:], ALU.max)
            first = True
            for kbit in (64, 32, 16, 8, 4, 2, 1):
                src = aq if first else rq
                vTS(v, uq[:], src[:], kbit - 0.5, float(-kbit), ALU.is_ge, ALU.mult)
                vTT(v, rq[:], src[:], uq[:], ALU.add)
                first = False
            vTT(v, aq[:], aq[:], rq[:], ALU.subtract)
            vTS(v, uq[:], pq[:], 0.0, 2.0, ALU.is_ge, ALU.mult)
            vSTT(v, aq[:], uq[:], -1.0, aq[:], ALU.add, ALU.mult)
            vTT(v, dest, aq[:], scAP, ALU.mult, inc=inc)

        def ccol(idx):
            return consts[:, idx:idx + 1]
        C_ONE, C_EPSLN, C_EPSRMS, C_MHALF, C_MONE, C_INV1024, C_ZERO = 0, 1, 2, 3, 4, 5, 6

        def c16(idx):
            return consts[0:16, idx:idx + 1]

        def g_p0(g):
            g.iota(U01[:], [[-1, 128]], base=0, channel_multiplier=1,
                   allow_small_or_imprecise_dtypes=True)
            g.iota(NOH[:], [[-1, 16], [0, 128]], base=0, channel_multiplier=1,
                   allow_small_or_imprecise_dtypes=True)
            i = g.iota(PSel[:], [[-1, 16], [0, 64]], base=0, channel_multiplier=1,
                       allow_small_or_imprecise_dtypes=True)
            bump(i, "g")
            for (dst, src) in [(wq_sb, wqT), (wk_sb, wkT), (wv_sb, wvT)]:
                for hh in range(2):
                    i = g.dma_start(
                        out=dst[:, hh * 4096:(hh + 1) * 4096],
                        in_=src[hh * 512:(hh + 1) * 512, :].rearrange(
                            "(i p) o -> p i o", p=128))
                    bump(i, "gW", 16)
            i = g.dma_start(out=wm1_sb[:], in_=wm1T.rearrange("(i p) o -> p i o", p=128))
            bump(i, "gW", 16)
            i = g.dma_start(out=wm2_sb[:], in_=wm2T.rearrange("(i p) o -> p i o", p=128))
            bump(i, "gW", 16)
            i = g.dma_start(out=wmod_sb[:, 0:6144], in_=wmod[0:128, :])
            bump(i, "gM0", 16)
        sec("gpsimd", g_p0)

        def hw_p0(sy):
            for (dst, src) in [(ccol_sb, c_col), (bm1_sb, bm1c), (bm2_sb, bm2c),
                               (cwc_sb, cwc), (cbc_sb, cbc)]:
                i = sy.dma_start(out=dst[:], in_=src)
                bump(i, "hS", 16)
        sec("sync", hw_p0)

        def v_p0(v):
            vMS(v, consts[:, 0:1], 1.0)
            vMS(v, consts[:, 1:2], EPS_LN)
            vMS(v, consts[:, 7:8], 3.0)
            vRCP(v, consts[:, 2:3], consts[:, 7:8])
            vTS(v, consts[:, 2:3], consts[:, 2:3], 4.0, -1.0, ALU.mult, ALU.add)
            vTS(v, consts[:, 2:3], consts[:, 2:3], 3.0, -1.0, ALU.mult, ALU.add)
            vTS(v, consts[:, 7:8], consts[:, 2:3], -1.0, None, ALU.mult)
            vTT(v, consts[:, 2:3], consts[:, 2:3], consts[:, 7:8], ALU.max)
            vMS(v, consts[:, 3:4], -0.5)
            vMS(v, consts[:, 4:5], -1.0)
            vMS(v, consts[:, 5:6], 1.0 / 1024.0)
            vMS(v, consts[:, 6:7], 0.0)
            vMS(v, onesb[:], 1.0)
            vMS(v, onesf[:], 1.0)
            vMS(v, onecf[:], 1.0)
            vMS(v, statesF[:], 0.0)
            vMS(v, statesB[:], 0.0)
            vMS(v, RST16[:], 1.0)
            for cst in range(0, 512, 128):
                vMS(v, RST16[:, cst:cst + 1], 0.0)
            wt(v, "vector", "g", "hS")
            vTS(v, NOH[:], NOH[:], 0.0, -1.0, ALU.is_equal, ALU.mult)
            vTS(v, PSel[:], PSel[:], 0.0, -1.0, ALU.is_equal, ALU.mult)
            vTS(v, nbm2_sb[:], bm2_sb[:], -1.0, None, ALU.mult)
            vTS(v, diagf[:], U01[:], 0.0, None, ALU.is_equal)
            vTS(v, U01[:], U01[:], 0.0, None, ALU.is_gt)
            vTS(v, IdNeg[:], diagf[:], -1.0e9, None, ALU.mult, inc="v")
        sec("vector", v_p0)
        p0_v = CNT["v"]

        def s_p0(s):
            wt(s, "scalar", "hS")
            wtv(s, "scalar", "v", p0_v)
            sACT(s, csb[:], ccol_sb[:], AF.Silu, bias=ccol(C_ZERO), scale=ccol(C_ONE), inc="s")
        sec("scalar", s_p0)

        for ic in range(8):
            par = ic % 2
            if ic + 1 < 8:
                def g_ld(g, ic=ic):
                    par2 = (ic + 1) % 2
                    if ic + 1 >= 2:
                        wt(g, "gpsimd", "t")
                    i = g.dma_start(out=wmod_sb[:, par2 * 6144:(par2 + 1) * 6144],
                                    in_=wmod[(ic + 1) * 128:(ic + 2) * 128, :])
                    bump(i, "gM0" if par2 == 0 else "gM1", 16)
                sec("gpsimd", g_ld)

            def t_mod(t, ic=ic, par=par):
                wt(t, "tensor", "gM0" if par == 0 else "gM1", "s")
                last = None
                for jc in range(48):
                    last = t.matmul(
                        psC[:, jc * 8 + ic: jc * 8 + ic + 1],
                        wmod_sb[:, par * 6144 + jc * 128: par * 6144 + (jc + 1) * 128],
                        csb[:, ic:ic + 1], start=True, stop=True)
                bump(last, "t")
            sec("tensor", t_mod)

        def v_mod(v):
            wt(v, "vector", "t")
            i = v.tensor_reduce(modtab[:],
                                psC[:, 0:384].rearrange("p (j i) -> p j i", i=8),
                                axis=AX.X, op=ALU.add)
            v.drain()
            vCP(v, modfin[:, 0:8], modtab[:, 0:8])
            vTS(v, modfin[:, 8:16], modtab[:, 8:16], 1.0, None, ALU.add)
            vCP(v, modfin[:, 24:32], modtab[:, 24:32])
            vTS(v, modfin[:, 32:40], modtab[:, 32:40], 1.0, None, ALU.add)
            for g0 in (16, 40):
                for kk in range(8):
                    vTS(v, modsc[:, 0:1], modtab[:, g0 + kk:g0 + kk + 1], 0.0, None, ALU.is_ge)
                    vTS(v, modsc[:, 0:1], modsc[:, 0:1], 2.0, -1.0, ALU.mult, ALU.add)
                    vTS(v, modsc[:, 1:2], modtab[:, g0 + kk:g0 + kk + 1], -1.0, None, ALU.mult)
                    vTT(v, modsc[:, 1:2], modsc[:, 1:2], modtab[:, g0 + kk:g0 + kk + 1], ALU.max)
                    vTS(v, modsc[:, 1:2], modsc[:, 1:2], 0.25, None, ALU.max)
                    vTT(v, modfin[:, g0 + kk:g0 + kk + 1], modsc[:, 0:1], modsc[:, 1:2], ALU.mult)
            i = v.drain()
            bump(i, "v")
        sec("vector", v_mod)
        modfin_v = CNT["v"]
        mod_t = CNT["t"]

        def col_sa(cc): return modfin[:, 0 + cc:1 + cc]
        def col_ca1(cc): return modfin[:, 8 + cc:9 + cc]
        def col_ga(cc): return modfin[:, 16 + cc:17 + cc]
        def col_sb(cc): return modfin[:, 24 + cc:25 + cc]
        def col_cb1(cc): return modfin[:, 32 + cc:33 + cc]
        def col_gb(cc): return modfin[:, 40 + cc:41 + cc]

        def stage(name):
            if _STOP == name:
                def hw_stop(sy):
                    for k in ("t", "v", "s", "g"):
                        wt(sy, "sync", k)
                    i = sy.dma_start(out=out[0:128, 0:8], in_=consts[:, 0:8])
                    bump(i, "oO0", 16)
                    sy.wait_ge(SEM["oO0"], CNT["oO0"])
                sec("sync", hw_stop)
                stopped[0] = True

        stage("p0")

        def g_conv(g):
            wfi = wfcT.rearrange("(i p) (c f) -> p c i f", p=128, f=128)
            wci = wcpT.rearrange("(c p) (h x) -> p c h x", p=128, x=512)
            for c in range(32):
                i = g.dma_start(out=wfc_scr[:, c * 1024:(c + 1) * 1024]
                                .rearrange("p (i f) -> p i f", f=128),
                                in_=wfi[:, c])
                bump(i, "cv", 16)
                i = g.dma_start(out=wcp_scr[:, c * 1024:(c + 1) * 1024]
                                .rearrange("p (h x) -> p h x", x=512),
                                in_=wci[:, c])
                bump(i, "cv", 16)
        sec("gpsimd", g_conv)
        CV_TOT = CNT["cv"]

        glob = 0
        NIT = 16 * 8
        SN_mask = [0] * NIT
        SN_exp = [0] * NIT
        SN_ms = [0] * NIT
        SN_grp = [0] * NIT
        SN_new = [0] * NIT
        SN_yev = [0] * NIT

        w1_tiles = [("w1", a, b) for a in range(8) for b in range(8)]
        w2_tiles = [("w2", a, b) for a in range(32) for b in range(2)]

        def wtile_src(kind, a, b):
            if kind == "w1":
                return w1T[a * 128:(a + 1) * 128, b * 512:(b + 1) * 512]
            return w2T[a * 128:(a + 1) * 128, b * 512:(b + 1) * 512]

        def wtile_dst(kind, a, b):
            if kind == "w1":
                rr = w1q_scr.rearrange("p (fc i f) -> p fc i f", i=8, f=128)
                return rr[:, b * 4:(b + 1) * 4, a, :]
            return w2q_scr[:, a * 1024 + b * 512: a * 1024 + (b + 1) * 512]

        p4b_sched = [[] for _ in range(NIT)]
        mth_cell = [0, 0]
        for wi, tiles in enumerate((w1_tiles, w2_tiles)):
            g0 = wi * 64

            def v_acc0(v):
                vMS(v, acc_w[:], 0.0, inc="v")
            p4b_sched[g0].append(("vector", v_acc0))

            for ti, (kk, a, b) in enumerate(tiles):
                g = g0 + ti // 2

                def hw_w(sy, kk=kk, a=a, b=b):
                    wt(sy, "sync", "v")
                    i = sy.dma_start(out=wfT_sb[:], in_=wtile_src(kk, a, b))
                    bump(i, "hW0", 16)
                p4b_sched[g].append(("sync", hw_w))

                def v_abs(v):
                    wt(v, "vector", "hW0")
                    vRED(v, tmpc[:], wfT_sb[:], ALU.add, absval=True)
                    vTT(v, acc_w[:], acc_w[:], tmpc[:], ALU.add, inc="v")
                p4b_sched[g].append(("vector", v_abs))

            gm = g0 + 32

            def t_tot(t):
                wt(t, "tensor", "v")
                i = t.matmul(psD[0:1, 0:1], acc_w[:], onecf[:], start=True, stop=True)
                bump(i, "t")
            p4b_sched[gm].append(("tensor", t_tot))

            def v_mth(v):
                wt(v, "vector", "t")
                vTS(v, sc11[0:1, 0:1], psD[0:1, 0:1], 1.0 / float(FF * C), None, ALU.mult)
                vTS(v, sc11[0:1, 1:2], sc11[0:1, 0:1], 1.0e-5, None, ALU.max)
                vTS(v, sc11[0:1, 0:1], sc11[0:1, 1:2], 0.5, None, ALU.mult, inc="v")
            p4b_sched[gm].append(("vector", v_mth))

            def t_rep(t):
                wt(t, "tensor", "v")
                i = t.matmul(psD[:, 0:2], onesf[:], sc11[0:1, 0:2], start=True, stop=True)
                bump(i, "t")
            p4b_sched[gm].append(("tensor", t_rep))

            def v_rep(v, wi=wi):
                wt(v, "vector", "t")
                vCP(v, msc_sc[:, wi * 2:(wi + 1) * 2], psD[:, 0:2])
                vTS(v, thn_sc[:, wi:wi + 1], msc_sc[:, wi * 2:wi * 2 + 1], -1.0, None,
                    ALU.mult, inc="v")
                mth_cell[wi] = CNT["v"]
            p4b_sched[gm].append(("vector", v_rep))

            for ti, (kk, a, b) in enumerate(tiles):
                g = g0 + 32 + ti // 2
                par = ti % 2

                def hw_w2(sy, kk=kk, a=a, b=b):
                    wt(sy, "sync", "v")
                    i = sy.dma_start(out=wfT_sb[:], in_=wtile_src(kk, a, b))
                    bump(i, "hW0", 16)
                p4b_sched[g].append(("sync", hw_w2))

                def v_tern(v, par=par, wi=wi):
                    wt(v, "vector", "hW0")
                    wtv(v, "vector", "v", mth_cell[wi])
                    wt(v, "vector", "oW0" if par == 0 else "oW1")
                    vTS(v, wqt_sb[:, par * 512:(par + 1) * 512], wfT_sb[:],
                        msc_sc[:, wi * 2:wi * 2 + 1], None, ALU.is_ge)
                    vTS(v, stW_sb[:], wfT_sb[:], thn_sc[:, wi:wi + 1], None, ALU.is_le)
                    vTT(v, wqt_sb[:, par * 512:(par + 1) * 512],
                        wqt_sb[:, par * 512:(par + 1) * 512], stW_sb[:],
                        ALU.subtract, inc="v")
                p4b_sched[g].append(("vector", v_tern))

                def hw_st(sy, kk=kk, a=a, b=b, par=par):
                    wt(sy, "sync", "v")
                    i = sy.dma_start(out=wtile_dst(kk, a, b),
                                     in_=wqt_sb[:, par * 512:(par + 1) * 512])
                    bump(i, "oW0" if par == 0 else "oW1", 16)
                p4b_sched[g].append(("sync", hw_st))

        for s_i in range(4):
            t0 = s_i * SLAB

            def hw_xs(sy, s_i=s_i, t0=t0):
                wtv(sy, "sync", "t", mod_t)
                if s_i > 0:
                    wt(sy, "sync", "v")
                for cc in range(8):
                    i = sy.dma_start(out=x_sl[:, cc * 512:(cc + 1) * 512],
                                     in_=xT[cc * 128:(cc + 1) * 128, t0:t0 + 512])
                    bump(i, "hX", 16)
            sec("sync", hw_xs)

            def v_cast(v, s_i=s_i):
                wt(v, "vector", "hX")
                if s_i > 0:
                    wt(v, "vector", "t")
                for cc in range(8):
                    vCP(v, xb_sb[:, cc * 512:(cc + 1) * 512], x_sl[:, cc * 512:(cc + 1) * 512])
                for cc in range(8):
                    vTT(v, xsq_sb[:, cc * 512:(cc + 1) * 512],
                        xb_sb[:, cc * 512:(cc + 1) * 512],
                        xb_sb[:, cc * 512:(cc + 1) * 512], ALU.mult)
                i = v.drain()
                bump(i, "v")
            sec("vector", v_cast)

            def t_stats(t):
                wt(t, "tensor", "v")
                for cc in range(8):
                    t.matmul(psC[:, 0:512], onesb[:], xb_sb[:, cc * 512:(cc + 1) * 512],
                             start=(cc == 0), stop=(cc == 7))
                i = None
                for cc in range(8):
                    i = t.matmul(psD[:, 0:512], onesb[:], xsq_sb[:, cc * 512:(cc + 1) * 512],
                                 start=(cc == 0), stop=(cc == 7))
                bump(i, "t")
            sec("tensor", t_stats)

            def v_stat2(v):
                wt(v, "vector", "t")
                wt(v, "vector", "oW0", "oW1")
                vTS(v, st2A[:], psC[:, 0:512], 1.0 / 1024.0, None, ALU.mult)
                vTS(v, st2B[:], psD[:, 0:512], 1.0 / 1024.0, None, ALU.mult)
                vTT(v, st2C[:], st2A[:], st2A[:], ALU.mult)
                vSTT(v, stD[:], st2C[:], -1.0, st2B[:], ALU.mult, ALU.add, inc="v")
            sec("vector", v_stat2)

            def s_stat(s):
                wt(s, "scalar", "v")
                sACT(s, st2C[:], stD[:], AF.Ln, bias=ccol(C_EPSLN), scale=ccol(C_ONE))
                sACT(s, st2B[:], st2C[:], AF.Exp, bias=ccol(C_ZERO), scale=ccol(C_MHALF), inc="s")
            sec("scalar", s_stat)

            def v_h1(v):
                wt(v, "vector", "s")
                wtv(v, "vector", "v", modfin_v)
                for cc in range(8):
                    sl = slice(cc * 512, (cc + 1) * 512)
                    vTT(v, stD[:], x_sl[:, sl], st2A[:], ALU.subtract)
                    vTT(v, st2C[:], stD[:], st2B[:], ALU.mult)
                    vTS(v, h1_sb[:, sl], st2C[:], col_ca1(cc), col_sa(cc), ALU.mult, ALU.add)
                i = v.drain()
                bump(i, "v")
            sec("vector", v_h1)
            h1_v = CNT["v"]
            if s_i == 0:
                stage("p2_h1")

            qkv_specs = ([("qf", hp) for hp in range(8)]
                         + [("kf", hp) for hp in range(8)]
                         + [("kt", tt * 2 + oh) for tt in range(4) for oh in range(2)]
                         + [("vt", tt * 2 + oh) for tt in range(4) for oh in range(2)]
                         + [("m1", hb) for hb in range(2)])

            for si, (kind, idx) in enumerate(qkv_specs):
                par = si % 2

                def t_qkv(t, kind=kind, idx=idx, par=par, si=si, h1_v=h1_v):
                    wtv(t, "tensor", "v", h1_v)
                    wt(t, "tensor", "gW")
                    if si >= 2:
                        wt(t, "tensor", "v")
                        wt(t, "tensor", "s")
                    i = None
                    if kind == "qf":
                        for ic in range(8):
                            i = t.matmul(psG[par][:, 0:512],
                                         wq_sb[:, ic * 1024 + idx * 128: ic * 1024 + (idx + 1) * 128],
                                         h1_sb[:, ic * 512:(ic + 1) * 512],
                                         start=(ic == 0), stop=(ic == 7))
                    elif kind == "kf":
                        for ic in range(8):
                            i = t.matmul(psG[par][:, 0:512],
                                         wk_sb[:, ic * 1024 + idx * 128: ic * 1024 + (idx + 1) * 128],
                                         h1_sb[:, ic * 512:(ic + 1) * 512],
                                         start=(ic == 0), stop=(ic == 7))
                    elif kind in ("kt", "vt"):
                        tt, oh = idx // 2, idx % 2
                        wsb = wk_sb if kind == "kt" else wv_sb
                        for ic in range(8):
                            i = t.matmul(psG[par][:, 0:512],
                                         h1_sb[:, ic * 512 + tt * 128: ic * 512 + (tt + 1) * 128],
                                         wsb[:, ic * 1024 + oh * 512: ic * 1024 + (oh + 1) * 512],
                                         start=(ic == 0), stop=(ic == 7))
                    else:
                        hb = idx
                        for ic in range(8):
                            i = t.matmul(psG[par][:, 0:512],
                                         wm1_sb[:, ic * 256 + hb * 128: ic * 256 + (hb + 1) * 128],
                                         h1_sb[:, ic * 512:(ic + 1) * 512],
                                         start=(ic == 0), stop=(ic == 7))
                    bump(i, "t")
                sec("tensor", t_qkv)

                if kind == "m1":
                    def s_m1(s, idx=idx, par=par):
                        wt(s, "scalar", "t")
                        sACT(s, m1b_sb[:, idx * 512:(idx + 1) * 512], psG[par][:, 0:512],
                             AF.Relu, bias=bm1_sb[:, idx:idx + 1], scale=ccol(C_ONE), inc="s")
                    sec("scalar", s_m1)
                else:
                    def v_ev(v, kind=kind, idx=idx, par=par):
                        wt(v, "vector", "t")
                        if kind == "qf":
                            dst = qf_sb[:, idx * 512:(idx + 1) * 512]
                        elif kind == "kf":
                            dst = kf_sb[:, idx * 512:(idx + 1) * 512]
                        else:
                            tt, oh = idx // 2, idx % 2
                            buf = ktok_sb if kind == "kt" else vtok_sb
                            dst = buf[:, tt * 1024 + oh * 512: tt * 1024 + (oh + 1) * 512]
                        vCP(v, dst, psG[par][:, 0:512])
                        if kind == "qf":
                            vTS(v, qfn_sb[:, idx * 512:(idx + 1) * 512],
                                psG[par][0:64, 0:512], -1.0, None, ALU.mult)
                        i = v.drain()
                        bump(i, "v")
                    sec("vector", v_ev)

            if s_i == 0:
                stage("p2_qkv")

            def t_z(t):
                wt(t, "tensor", "s")
                i = None
                for hb in range(2):
                    i = t.matmul(psC[0:16, 0:512],
                                 wm2_sb[:, hb * 16:(hb + 1) * 16],
                                 m1b_sb[:, hb * 512:(hb + 1) * 512],
                                 start=(hb == 0), stop=(hb == 1))
                bump(i, "t")
            sec("tensor", t_z)

            def s_z(s):
                wt(s, "scalar", "t", "v")
                sACT(s, e1x[:], psC[0:16, 0:512], AF.Exp, bias=bm2_sb[:], scale=c16(C_ONE))
                sACT(s, sp1[:], e1x[:], AF.Ln, bias=c16(C_ONE), scale=c16(C_ONE))
                sACT(s, e1x[:], psC[0:16, 0:512], AF.Exp, bias=nbm2_sb[:],
                     scale=consts[0:16, 4:5])
                sACT(s, sp2[:], e1x[:], AF.Ln, bias=c16(C_ONE), scale=c16(C_ONE), inc="s")
            sec("scalar", s_z)

            def v_dec(v, s_i=s_i):
                wt(v, "vector", "s")
                if s_i > 0:
                    wtv(v, "vector", "t", SN_mask[s_i * 32 - 1])
                vSCAN(v, P16[:], RST16[:], sp1[:], 0.0, ALU.mult, ALU.add)
                vTT(v, mw16[:], sp2[:], P16[:], ALU.subtract, inc="v")
            sec("vector", v_dec)
            dec_v = CNT["v"]
            if s_i == 0:
                stage("p2_z")

            for hp in range(8):
                par = hp % 2

                def t_el(t, hp=hp, par=par, dec_v=dec_v):
                    wtv(t, "tensor", "v", dec_v)
                    if hp >= 2:
                        wt(t, "tensor", "s")
                    i = t.matmul(psG[par][:, 0:512], PSel[:, hp * 128:(hp + 1) * 128],
                                 P16[:], start=True, stop=True)
                    bump(i, "t")
                sec("tensor", t_el)

                def s_el(s, hp=hp, par=par):
                    wt(s, "scalar", "t", "v")
                    sACT(s, ELp_sb[:, hp * 512:(hp + 1) * 512], psG[par][:, 0:512],
                         AF.Exp, bias=ccol(C_ZERO), scale=ccol(C_ONE))
                    sACT(s, PPall[:, hp * 4:(hp + 1) * 4], psG[par][:, 127:512:128],
                         AF.Exp, bias=ccol(C_ZERO), scale=ccol(C_ONE), inc="s")
                sec("scalar", s_el)
            elp_s = CNT["s"]
            if s_i == 0:
                stage("p2_el")

            for cin in range(4):
                gc = s_i * 4 + cin
                for hp in range(8):
                    i_g = glob
                    par = i_g % 2
                    pc = par * 256

                    def t_msk(t, i_g=i_g, hp=hp, cin=cin, pc=pc, dec_v=dec_v):
                        wtv(t, "tensor", "v", dec_v)
                        if i_g >= 2:
                            wtv(t, "tensor", "s", SN_exp[i_g - 2])
                            wtv(t, "tensor", "v", SN_ms[i_g - 2])
                        cs = slice(cin * 128, (cin + 1) * 128)
                        last = None
                        for h in range(2):
                            hm = 2 * hp + h
                            last = t.matmul(psA[:, pc + h * 128: pc + (h + 1) * 128],
                                     mw16[:, cs], NOH[:, hm * 128:(hm + 1) * 128],
                                     start=True, stop=(_MSK == 1))
                            if _MSK >= 2:
                                t.matmul(psA[:, pc + h * 128: pc + (h + 1) * 128],
                                         NOH[:, hm * 128:(hm + 1) * 128], P16[:, cs],
                                         start=False, stop=False)
                                last = t.matmul(psA[:, pc + h * 128: pc + (h + 1) * 128],
                                         IdNeg[:], U01[:], start=False, stop=True)
                        if _MSK >= 3:
                            sl = slice(hp * 512 + cin * 128, hp * 512 + (cin + 1) * 128)
                            t.matmul(psB[:, pc + 128: pc + 256],
                                     kf_sb[:, sl], qf_sb[:, sl],
                                     start=True, stop=False)
                            t.matmul(psB[:, pc + 128: pc + 256],
                                     kf_sb[0:64, sl], qfn_sb[0:64, sl],
                                     start=False, stop=True)
                            last = t.matmul(psB[:, pc: pc + 128],
                                            kf_sb[0:64, sl], qf_sb[0:64, sl],
                                            start=True, stop=True)
                        bump(last, "t")
                        SN_mask[i_g] = CNT["t"]
                    sec("tensor", t_msk)
                    if i_g == 0:
                        stage("c0_msk")

                    def s_exp(s, i_g=i_g, pc=pc):
                        wtv(s, "scalar", "t", SN_mask[i_g])
                        if i_g >= 2:
                            wtv(s, "scalar", "v", SN_ms[i_g - 2])
                        sACT(s, mask_sb[:, pc:pc + 256], psA[:, pc:pc + 256],
                             AF.Exp, bias=ccol(C_ZERO), scale=ccol(C_ONE), inc="s")
                        SN_exp[i_g] = CNT["s"]
                    sec("scalar", s_exp)
                    if i_g == 0:
                        stage("c0_exp")

                    def v_msqk(v, i_g=i_g, hp=hp, cin=cin, par=par, pc=pc, elp_s=elp_s):
                        wtv(v, "vector", "s", SN_exp[i_g])
                        wtv(v, "vector", "s", elp_s)
                        if i_g >= 2:
                            wtv(v, "vector", "t", SN_grp[i_g - 2])
                        for h in range(2):
                            vTT(v, ms_sb[:, pc + h * 128: pc + (h + 1) * 128],
                                psB[:, pc + h * 128: pc + (h + 1) * 128],
                                mask_sb[:, pc + h * 128: pc + (h + 1) * 128], ALU.mult)
                        sl = slice(hp * 512 + cin * 128, hp * 512 + (cin + 1) * 128)
                        vTT(v, qp_sb[:, par * 128:(par + 1) * 128],
                            qf_sb[:, sl], ELp_sb[:, sl], ALU.mult)
                        vTT(v, qpn_sb[:, par * 128:(par + 1) * 128],
                            qfn_sb[0:64, sl], ELp_sb[0:64, sl], ALU.mult)
                        for h in range(2):
                            hm = 2 * hp + h
                            vTS(v, kt2_sb[:, par * 128 + h * 64: par * 128 + (h + 1) * 64],
                                ktok_sb[:, cin * 1024 + hm * 64: cin * 1024 + (hm + 1) * 64],
                                mask_sb[:, pc + h * 128 + 127: pc + h * 128 + 128],
                                None, ALU.mult)
                        i = v.drain()
                        bump(i, "v")
                        SN_ms[i_g] = CNT["v"]
                    sec("vector", v_msqk)
                    if i_g == 0:
                        stage("c0_ms")

                    def t_grp(t, i_g=i_g, hp=hp, cin=cin, par=par, pc=pc):
                        wtv(t, "tensor", "v", SN_ms[i_g])
                        if i_g >= 8:
                            wtv(t, "tensor", "v", SN_new[i_g - 8])
                        if i_g >= 2:
                            wtv(t, "tensor", "s", SN_yev[i_g - 2])
                            wtv(t, "tensor", "v", SN_new[i_g - 2])
                        t.matmul(psY[0:64, par * 128:(par + 1) * 128],
                                 vtok_sb[:, cin * 1024 + 2 * hp * 64: cin * 1024 + (2 * hp + 1) * 64],
                                 ms_sb[:, pc: pc + 128],
                                 start=True, stop=False)
                        t.matmul(psY[0:64, par * 128:(par + 1) * 128],
                                 statesB[0:64, hp * 64:(hp + 1) * 64],
                                 qp_sb[0:64, par * 128:(par + 1) * 128],
                                 start=False, stop=True)
                        t.matmul(psY[64:128, par * 128:(par + 1) * 128],
                                 vtok_sb[:, cin * 1024 + (2 * hp + 1) * 64: cin * 1024 + (2 * hp + 2) * 64],
                                 ms_sb[:, pc + 128: pc + 256],
                                 start=True, stop=False, tile_position=(0, 64))
                        t.matmul(psY[64:128, par * 128:(par + 1) * 128],
                                 statesB[:, hp * 64:(hp + 1) * 64],
                                 qp_sb[:, par * 128:(par + 1) * 128],
                                 start=False, stop=False, tile_position=(0, 64))
                        t.matmul(psY[64:128, par * 128:(par + 1) * 128],
                                 statesB[0:64, hp * 64:(hp + 1) * 64],
                                 qpn_sb[0:64, par * 128:(par + 1) * 128],
                                 start=False, stop=True, tile_position=(0, 64))
                        last = None
                        for h in range(2):
                            hm = 2 * hp + h
                            tp = None if h == 0 else (0, 64)
                            last = t.matmul(
                                psS[h * 64:(h + 1) * 64, par * 128: par * 128 + 64],
                                kt2_sb[:, par * 128 + h * 64: par * 128 + (h + 1) * 64],
                                vtok_sb[:, cin * 1024 + hm * 64: cin * 1024 + (hm + 1) * 64],
                                start=True, stop=True, tile_position=tp)
                        bump(last, "t")
                        SN_grp[i_g] = CNT["t"]
                    sec("tensor", t_grp)
                    if i_g == 0:
                        stage("c0_grp")

                    def v_new(v, i_g=i_g, hp=hp, cin=cin, par=par):
                        wtv(v, "vector", "t", SN_grp[i_g])
                        vSTT(v, statesF[:, hp * 64:(hp + 1) * 64],
                             statesF[:, hp * 64:(hp + 1) * 64],
                             PPall[:, hp * 4 + cin: hp * 4 + cin + 1],
                             psS[:, par * 128: par * 128 + 64],
                             ALU.mult, ALU.add)
                        vCP(v, statesB[:, hp * 64:(hp + 1) * 64],
                            statesF[:, hp * 64:(hp + 1) * 64], inc="v")
                        SN_new[i_g] = CNT["v"]
                    sec("vector", v_new)

                    def s_yev(s, i_g=i_g, hp=hp, gc=gc, par=par):
                        wtv(s, "scalar", "t", SN_grp[i_g])
                        sACT(s, y_sb[:, hp * 2048 + gc * 128: hp * 2048 + (gc + 1) * 128],
                             psY[:, par * 128:(par + 1) * 128],
                             AF.Identity, bias=ccol(C_ZERO), scale=ccol(C_ONE), inc="s")
                        SN_yev[i_g] = CNT["s"]
                    sec("scalar", s_yev)

                    for engx, fnx in p4b_sched[i_g]:
                        sec(engx, fnx)

                    glob += 1
                    if glob == 8:
                        stage("p2_c0")

        stage("p2")
        wq_o0 = CNT["oW0"]
        wq_o1 = CNT["oW1"]
        def g_wpj(g):
            wt(g, "gpsimd", "t")
            for hh in range(2):
                i = g.dma_start(
                    out=wpj_sb[:, hh * 4096:(hh + 1) * 4096],
                    in_=wpjT[hh * 512:(hh + 1) * 512, :].rearrange("(i p) o -> p i o", p=128))
                bump(i, "gW", 16)
        sec("gpsimd", g_wpj)

        for cc in range(8):
            for ts4 in range(4):
                si = cc * 4 + ts4
                par = si % 2

                def t_prj(t, cc=cc, ts4=ts4, par=par, si=si):
                    wt(t, "tensor", "gW")
                    wtv(t, "tensor", "s", SN_yev[NIT - 1])
                    if si >= 2:
                        wt(t, "tensor", "v")
                    i = None
                    for ic in range(8):
                        i = t.matmul(psG[par][:, 0:512],
                                     wpj_sb[:, ic * 1024 + cc * 128: ic * 1024 + (cc + 1) * 128],
                                     y_sb[:, ic * 2048 + ts4 * 512: ic * 2048 + (ts4 + 1) * 512],
                                     start=(ic == 0), stop=(ic == 7))
                    bump(i, "t")
                sec("tensor", t_prj)

                def v_prj(v, cc=cc, ts4=ts4, par=par):
                    wt(v, "vector", "t")
                    vCP(v, ob_sb[:, cc * 2048 + ts4 * 512: cc * 2048 + (ts4 + 1) * 512],
                        psG[par][:, 0:512], inc="v")
                sec("vector", v_prj)
        prj_v = CNT["v"]

        stage("p2b")
        THID_SNAP = []
        TMLP_SNAP = []
        for ts in range(4):
            def v_osq(v, ts=ts, prj_v=prj_v):
                wtv(v, "vector", "v", prj_v)
                if ts > 0:
                    wt(v, "vector", "t")
                for cc in range(8):
                    sl = slice(cc * 512, (cc + 1) * 512)
                    ol = slice(cc * 2048 + ts * 512, cc * 2048 + (ts + 1) * 512)
                    vTT(v, yn_sb[:, sl], ob_sb[:, ol], ob_sb[:, ol], ALU.mult)
                i = v.drain()
                bump(i, "v")
            sec("vector", v_osq)

            def t_rstat(t):
                wt(t, "tensor", "v")
                i = None
                for cc in range(8):
                    i = t.matmul(psC[:, 0:512], onesb[:], yn_sb[:, cc * 512:(cc + 1) * 512],
                                 start=(cc == 0), stop=(cc == 7))
                bump(i, "t")
            sec("tensor", t_rstat)

            def v_rst2(v):
                wt(v, "vector", "t")
                vTS(v, st3A[:], psC[:, 0:512], 1.0 / 1024.0, None, ALU.mult, inc="v")
            sec("vector", v_rst2)

            def s_rst(s):
                wt(s, "scalar", "v")
                sACT(s, st3C[:], st3A[:], AF.Ln, bias=ccol(C_EPSRMS), scale=ccol(C_ONE))
                sACT(s, st3B[:], st3C[:], AF.Exp, bias=ccol(C_ZERO), scale=ccol(C_MHALF), inc="s")
            sec("scalar", s_rst)

            def v_yn(v, ts=ts):
                wt(v, "vector", "s")
                for cc in range(8):
                    sl = slice(cc * 512, (cc + 1) * 512)
                    ol = slice(cc * 2048 + ts * 512, cc * 2048 + (ts + 1) * 512)
                    vTT(v, yn_sb[:, sl], ob_sb[:, ol], st3B[:], ALU.mult)
                i = v.drain()
                bump(i, "v")
            sec("vector", v_yn)
            yn_v = CNT["v"]

            for fc in range(32):
                par = fc % 2

                def g_wfc(eng, ts=ts, fc=fc, par=par):
                    q = "gpsimd" if par == 0 else "scalar"
                    if ts == 0 and fc < 2:
                        wtv(eng, q, "cv", CV_TOT)
                    gi = ts * 32 + fc
                    if gi >= 2:
                        wtv(eng, q, "t", THID_SNAP[gi - 2])
                    i = eng.dma_start(
                        out=wfc_t[:, par * 1024:(par + 1) * 1024],
                        in_=wfc_scr[:, fc * 1024:(fc + 1) * 1024])
                    bump(i, "gF0" if par == 0 else "gF1", 16)
                sec("gpsimd" if par == 0 else "scalar", g_wfc)

                def t_hid(t, fc=fc, par=par, yn_v=yn_v):
                    wt(t, "tensor", "gF0" if par == 0 else "gF1")
                    wtv(t, "tensor", "v", yn_v)
                    if fc >= 2:
                        wt(t, "tensor", "v", "s")
                    i = None
                    for ic in range(8):
                        i = t.matmul(psG[par][:, 0:512],
                                     wfc_t[:, par * 1024 + ic * 128: par * 1024 + (ic + 1) * 128],
                                     yn_sb[:, ic * 512:(ic + 1) * 512],
                                     start=(ic == 0), stop=(ic == 7))
                    bump(i, "t")
                    THID_SNAP.append(CNT["t"])
                sec("tensor", t_hid)

                def s_r2(s, fc=fc, par=par):
                    wt(s, "scalar", "t")
                    sACT(s, t1_sc[:, par * 512:(par + 1) * 512], psG[par][:, 0:512],
                         AF.Relu, bias=ccol(C_ZERO), scale=ccol(C_ONE), inc="s")
                sec("scalar", s_r2)

                def v_r2(v, fc=fc, par=par):
                    wt(v, "vector", "s")
                    vTT(v, relu2_sb[:, fc * 512:(fc + 1) * 512],
                        t1_sc[:, par * 512:(par + 1) * 512],
                        t1_sc[:, par * 512:(par + 1) * 512], ALU.mult, inc="v")
                sec("vector", v_r2)
            r2_v = CNT["v"]

            for ch in range(2):
                for fc in range(32):
                    parw = fc % 2

                    def g_wcp(eng, ts=ts, ch=ch, fc=fc, parw=parw):
                        q = "sync" if parw == 0 else "gpsimd"
                        if ts == 0 and ch == 0 and fc < 2:
                            wtv(eng, q, "cv", CV_TOT)
                        gi = (ts * 2 + ch) * 32 + fc
                        if gi >= 2:
                            wtv(eng, q, "t", TMLP_SNAP[gi - 2])
                        i = eng.dma_start(out=wcpt[:, parw * 512:(parw + 1) * 512],
                                          in_=wcp_scr[:, fc * 1024 + ch * 512:
                                                      fc * 1024 + (ch + 1) * 512])
                        bump(i, "gP0" if parw == 0 else "gP1", 16)
                    sec("sync" if parw == 0 else "gpsimd", g_wcp)

                    def t_mlp(t, ch=ch, fc=fc, parw=parw, r2_v=r2_v):
                        wtv(t, "tensor", "v", r2_v)
                        wt(t, "tensor", "gP0" if parw == 0 else "gP1")
                        if fc == 0:
                            wt(t, "tensor", "v")
                        i = None
                        for j in range(4):
                            i = t.matmul(psBANKS[j][:, 0:512],
                                         wcpt[:, parw * 512 + j * 128: parw * 512 + (j + 1) * 128],
                                         relu2_sb[:, fc * 512:(fc + 1) * 512],
                                         start=(fc == 0), stop=(fc == 31))
                        bump(i, "t")
                        TMLP_SNAP.append(CNT["t"])
                    sec("tensor", t_mlp)

                for j in range(4):
                    cc = ch * 4 + j
                    parx = (ts * 8 + cc) % 2

                    def hw_x3(sy, cc=cc, ts=ts, parx=parx):
                        wt(sy, "sync", "v")
                        i = sy.dma_start(out=x_sl3[:, parx * 512:(parx + 1) * 512],
                                         in_=xT[cc * 128:(cc + 1) * 128, ts * 512:(ts + 1) * 512])
                        bump(i, "h3", 16)
                    sec("sync", hw_x3)

                    def v_x2(v, j=j, cc=cc, ts=ts, parx=parx):
                        wt(v, "vector", "t", "h3")
                        ol = slice(cc * 2048 + ts * 512, cc * 2048 + (ts + 1) * 512)
                        vTT(v, t1_sc[:, parx * 512:(parx + 1) * 512],
                            psBANKS[j][:, 0:512], ob_sb[:, ol], ALU.add)
                        vSTT(v, x2_sb[:, ol],
                             t1_sc[:, parx * 512:(parx + 1) * 512], col_ga(cc),
                             x_sl3[:, parx * 512:(parx + 1) * 512],
                             ALU.mult, ALU.add, inc="v")
                    sec("vector", v_x2)
        x2_v = CNT["v"]

        stage("p3")
        for ts in range(4):
            def v_x2c(v, ts=ts, x2_v=x2_v):
                wtv(v, "vector", "v", x2_v)
                if ts > 0:
                    wt(v, "vector", "t")
                for cc in range(8):
                    sl = slice(cc * 512, (cc + 1) * 512)
                    ol = slice(cc * 2048 + ts * 512, cc * 2048 + (ts + 1) * 512)
                    vCP(v, xb4_sb[:, sl], x2_sb[:, ol])
                for cc in range(8):
                    sl = slice(cc * 512, (cc + 1) * 512)
                    vTT(v, xsq4_sb[:, sl], xb4_sb[:, sl], xb4_sb[:, sl], ALU.mult)
                i = v.drain()
                bump(i, "v")
            sec("vector", v_x2c)

            def t_st4(t):
                wt(t, "tensor", "v")
                for cc in range(8):
                    t.matmul(psC[:, 0:512], onesb[:], xb4_sb[:, cc * 512:(cc + 1) * 512],
                             start=(cc == 0), stop=(cc == 7))
                i = None
                for cc in range(8):
                    i = t.matmul(psD[:, 0:512], onesb[:], xsq4_sb[:, cc * 512:(cc + 1) * 512],
                                 start=(cc == 0), stop=(cc == 7))
                bump(i, "t")
            sec("tensor", t_st4)

            def v_st4(v):
                wt(v, "vector", "t")
                vTS(v, st3A[:], psC[:, 0:512], 1.0 / 1024.0, None, ALU.mult)
                vTS(v, st3B[:], psD[:, 0:512], 1.0 / 1024.0, None, ALU.mult)
                vTT(v, st3C[:], st3A[:], st3A[:], ALU.mult)
                vSTT(v, st3D[:], st3C[:], -1.0, st3B[:], ALU.mult, ALU.add, inc="v")
            sec("vector", v_st4)

            def s_st4(s):
                wt(s, "scalar", "v")
                sACT(s, st3C[:], st3D[:], AF.Ln, bias=ccol(C_EPSLN), scale=ccol(C_ONE))
                sACT(s, st3B[:], st3C[:], AF.Exp, bias=ccol(C_ZERO), scale=ccol(C_MHALF), inc="s")
            sec("scalar", s_st4)

            def v_h2(v, ts=ts):
                wt(v, "vector", "s")
                wt(v, "vector", "oW0", "oW1")
                for cc in range(8):
                    ol = slice(cc * 2048 + ts * 512, cc * 2048 + (ts + 1) * 512)
                    vTT(v, st3D[:], x2_sb[:, ol], st3A[:], ALU.subtract)
                    vTT(v, st3C[:], st3D[:], st3B[:], ALU.mult)
                    vTS(v, h2_sb[:, ol], st3C[:], col_cb1(cc), col_sb(cc), ALU.mult, ALU.add)
                o0 = ts * 512
                vMS(v, st3D[:], 0.0)
                for cc in range(8):
                    hsl = slice(cc * 2048 + o0, cc * 2048 + o0 + 512)
                    vTT(v, st3D[:], st3D[:], h2_sb[:, hsl], ALU.max)
                    vSTT(v, st3D[:], h2_sb[:, hsl], -1.0, st3D[:], ALU.mult, ALU.max)
                i = v.drain()
                bump(i, "v")
            sec("vector", v_h2)

            def t_tp4(t):
                wt(t, "tensor", "v")
                i = None
                for tt in range(4):
                    i = t.transpose(psD[:, tt * 128:(tt + 1) * 128],
                                    st3D[:, tt * 128:(tt + 1) * 128], diagf[:])
                bump(i, "t")
            sec("tensor", t_tp4)

            def v_rm1(v, ts=ts):
                wt(v, "vector", "t")
                for tt in range(4):
                    vRED(v, rmax1[:, ts * 4 + tt: ts * 4 + tt + 1],
                         psD[:, tt * 128:(tt + 1) * 128], ALU.max)
                i = v.drain()
                bump(i, "v")
            sec("vector", v_rm1)
        p4a_v = CNT["v"]

        stage("p4a")
        def v_s1(v, p4a_v=p4a_v):
            wtv(v, "vector", "v", p4a_v)
            wtv(v, "vector", "oW0", wq_o0)
            wtv(v, "vector", "oW1", wq_o1)
            vTS(v, sv_a[:], rmax1[:], 1.0e-5, None, ALU.max)
            vRCP(v, sv_b[:], sv_a[:])
            vTS(v, s1v[:], sv_b[:], 127.0, None, ALU.mult)
            vTS(v, sc1v[:], sv_a[:], msc_sc[:, 1:2], 1.0 / 127.0, ALU.mult, ALU.mult,
                inc="v")
        sec("vector", v_s1)

        def t_tps1(t):
            wt(t, "tensor", "v")
            t.transpose(psD[0:16, 0:128], s1v[:], diagf[:])
            i = t.transpose(psD[0:16, 128:256], sc1v[:], diagf[:])
            bump(i, "t")
        sec("tensor", t_tps1)

        def v_tps1(v):
            wt(v, "vector", "t")
            vCP(v, tposed[:], psD[0:16, 0:256], inc="v")
        sec("vector", v_tps1)

        def g_oht1(g):
            wt(g, "gpsimd", "v")
            i = g.iota(OHT[:], [[-1, 16], [0, 128]], base=0, channel_multiplier=1,
                       allow_small_or_imprecise_dtypes=True)
            bump(i, "g")
            g.drain()
        sec("gpsimd", g_oht1)

        def v_oht1(v):
            wt(v, "vector", "g")
            vTS(v, OHT[:], OHT[:], 0.0, None, ALU.is_equal, inc="v")
        sec("vector", v_oht1)

        for grp in range(4):
            def t_rp1(t, grp=grp):
                wt(t, "tensor", "v")
                for k in range(4):
                    tt = grp * 4 + k
                    t.matmul(psC[:, k * 128:(k + 1) * 128],
                             OHT[:, tt * 128:(tt + 1) * 128], tposed[:, 0:128],
                             start=True, stop=True)
                i = None
                for k in range(4):
                    tt = grp * 4 + k
                    i = t.matmul(psD[:, k * 128:(k + 1) * 128],
                                 OHT[:, tt * 128:(tt + 1) * 128], tposed[:, 128:256],
                                 start=True, stop=True)
                bump(i, "t")
            sec("tensor", t_rp1)

            def v_rp1(v, grp=grp):
                wt(v, "vector", "t")
                vCP(v, s1rep[:, grp * 512:(grp + 1) * 512], psC[:, 0:512])
                vCP(v, sc1rep[:, grp * 512:(grp + 1) * 512], psD[:, 0:512], inc="v")
            sec("vector", v_rp1)
        rep1_v = CNT["v"]

        def v_pad(v):
            wt(v, "vector", "t")
            for ic in range(8):
                vMS(v, xq1_sb[:, ic * 2050:ic * 2050 + 1], 0.0)
                vMS(v, xq1_sb[:, ic * 2050 + 2049:ic * 2050 + 2050], 0.0)
            i = v.drain()
            bump(i, "v")
        sec("vector", v_pad)

        for cc in range(8):

            def v_xq1(v, cc=cc, rep1_v=rep1_v):
                wtv(v, "vector", "v", rep1_v)
                ol = slice(cc * 2048, (cc + 1) * 2048)
                ladder_round(v, pq4, aq4, rq4, uq4,
                             h2_sb[:, ol], s1rep[:],
                             xq1_sb[:, cc * 2050 + 1: cc * 2050 + 2049],
                             sc1rep[:], inc="v")
            sec("vector", v_xq1)
        xq1_v = CNT["v"]

        stage("p4c")
        TU_SNAP = []
        UEV_SNAP = []
        TCV_SNAP = []
        for hf in range(2):
            def v_rz(v, hf=hf, xq1_v=xq1_v):
                wtv(v, "vector", "v", xq1_v)
                if hf > 0:
                    wt(v, "vector", "t")
                vMS(v, run2q[:], 0.0, inc="v")
            sec("vector", v_rz)

            for fc in range(32):
                par = fc % 2

                def hw_w1s(sy, hf=hf, fc=fc, par=par):
                    gi = hf * 32 + fc
                    wtv(sy, "sync", "oW0", wq_o0)
                    wtv(sy, "sync", "oW1", wq_o1)
                    if gi >= 2:
                        wtv(sy, "sync", "t", TU_SNAP[gi - 2])
                    else:
                        wt(sy, "sync", "t", "v")
                    i = sy.dma_start(out=w1t_sb[:, par * 1024:(par + 1) * 1024],
                                     in_=w1q_scr[:, fc * 1024:(fc + 1) * 1024])
                    bump(i, "h50" if par == 0 else "h51", 16)
                sec("sync", hw_w1s)

                def g_dia(g, hf=hf, fc=fc, par=par):
                    gi = hf * 32 + fc
                    if fc < 2:
                        if hf == 0:
                            wtv(g, "gpsimd", "v", xq1_v)
                        else:
                            wt(g, "gpsimd", "v")
                    if gi >= 2:
                        wtv(g, "gpsimd", "t", TCV_SNAP[gi - 2])
                    for j in range(3):
                        pTS(g, dia_sb[:, par * 384 + j * 128: par * 384 + (j + 1) * 128],
                            diagf[:], cwc_sb[:, fc * 3 + j: fc * 3 + j + 1], None, ALU.mult)
                    i = g.nop()
                    bump(i, "g")
                sec("gpsimd", g_dia)

                uG0 = psG0 if par == 0 else psY
                uG1 = psG1 if par == 0 else psS
                uC = psC if par == 0 else psD

                def t_u(t, hf=hf, fc=fc, par=par, uG0=uG0, uG1=uG1, uC=uC):
                    gi = hf * 32 + fc
                    wt(t, "tensor", "h50" if par == 0 else "h51")
                    if gi >= 2:
                        wtv(t, "tensor", "v", UEV_SNAP[gi - 2])
                    else:
                        wtv(t, "tensor", "v", xq1_v)
                        wt(t, "tensor", "v")
                    for ic in range(8):
                        base = ic * 2050 + hf * 1024
                        t.matmul(uG0[:, 0:512],
                                 w1t_sb[:, par * 1024 + ic * 128: par * 1024 + (ic + 1) * 128],
                                 xq1_sb[:, base: base + 512],
                                 start=(ic == 0), stop=(ic == 7))
                        t.matmul(uG1[:, 0:512],
                                 w1t_sb[:, par * 1024 + ic * 128: par * 1024 + (ic + 1) * 128],
                                 xq1_sb[:, base + 512: base + 1024],
                                 start=(ic == 0), stop=(ic == 7))
                    i = None
                    for ic in range(8):
                        base = ic * 2050 + hf * 1024
                        i = t.matmul(uC[:, 0:2],
                                     w1t_sb[:, par * 1024 + ic * 128: par * 1024 + (ic + 1) * 128],
                                     xq1_sb[:, base + 1024: base + 1026],
                                     start=(ic == 0), stop=(ic == 7))
                    bump(i, "t")
                    TU_SNAP.append(CNT["t"])
                sec("tensor", t_u)

                def v_uev(v, par=par, uG0=uG0, uG1=uG1, uC=uC):
                    wt(v, "vector", "t")
                    wtv(v, "vector", "oW0", wq_o0)
                    wtv(v, "vector", "oW1", wq_o1)
                    vCP(v, u_sb[:, par * 1026 + 0: par * 1026 + 512], uG0[:, 0:512])
                    vCP(v, u_sb[:, par * 1026 + 512: par * 1026 + 1024], uG1[:, 0:512])
                    vCP(v, u_sb[:, par * 1026 + 1024: par * 1026 + 1026], uC[:, 0:2],
                        inc="v")
                    UEV_SNAP.append(CNT["v"])
                sec("vector", v_uev)

                def t_cv(t, fc=fc, par=par):
                    wt(t, "tensor", "v", "s", "g")
                    i = None
                    for oh in range(2):
                        pb = psA if oh == 0 else psB
                        for j in range(3):
                            i = t.matmul(pb[:, 0:512],
                                         dia_sb[:, par * 384 + j * 128: par * 384 + (j + 1) * 128],
                                         u_sb[:, par * 1026 + oh * 512 + j:
                                              par * 1026 + oh * 512 + j + 512],
                                         start=(j == 0), stop=(j == 2))
                    bump(i, "t")
                    TCV_SNAP.append(CNT["t"])
                sec("tensor", t_cv)

                def s_si(s, fc=fc):
                    wt(s, "scalar", "t")
                    sACT(s, silu_sb[:, fc * 512: (fc + 1) * 512], psA[:, 0:512],
                         AF.Silu, bias=cbc_sb[:, fc:fc + 1], scale=ccol(C_ONE))
                    sACT(s, silu_sb[:, 16384 + fc * 512: 16384 + (fc + 1) * 512],
                         psB[:, 0:512],
                         AF.Silu, bias=cbc_sb[:, fc:fc + 1], scale=ccol(C_ONE), inc="s")
                sec("scalar", s_si)

                if fc % 2 == 1:
                    def v_cm(v, fc=fc):
                        wt(v, "vector", "s")
                        for qq in range(2):
                            seg = slice(qq * 16384 + (fc - 1) * 512,
                                        qq * 16384 + (fc + 1) * 512)
                            acc = slice(qq * 1024, (qq + 1) * 1024)
                            vTT(v, run2q[:, acc], run2q[:, acc], silu_sb[:, seg],
                                ALU.max)
                            vSTT(v, run2q[:, acc], silu_sb[:, seg], -1.0,
                                 run2q[:, acc], ALU.mult, ALU.max)
                        i = v.drain()
                        bump(i, "v")
                    sec("vector", v_cm)
            cm_v = CNT["v"]

            def v_r2f(v, cm_v=cm_v):
                wtv(v, "vector", "v", cm_v)
                for qq in range(2):
                    vTT(v, run2f[:, qq * 512:(qq + 1) * 512],
                        run2q[:, qq * 1024: qq * 1024 + 512],
                        run2q[:, qq * 1024 + 512: (qq + 1) * 1024], ALU.max)
                i = v.drain()
                bump(i, "v")
            sec("vector", v_r2f)

            for wv_i in range(2):
                def t_tp5(t, wv_i=wv_i):
                    wt(t, "tensor", "v")
                    i = None
                    for k in range(4):
                        i = t.transpose(psD[:, k * 128:(k + 1) * 128],
                                        run2f[:, (wv_i * 4 + k) * 128:(wv_i * 4 + k + 1) * 128],
                                        diagf[:])
                    bump(i, "t")
                sec("tensor", t_tp5)

                def v_rm2(v, wv_i=wv_i):
                    wt(v, "vector", "t")
                    for k in range(4):
                        vRED(v, rmax2[:, wv_i * 4 + k: wv_i * 4 + k + 1],
                             psD[:, k * 128:(k + 1) * 128], ALU.max)
                    i = v.drain()
                    bump(i, "v")
                sec("vector", v_rm2)

            def v_s2(v):
                wt(v, "vector", "t")
                vTS(v, sv_a[:, 0:8], rmax2[:], 1.0e-5, None, ALU.max)
                vRCP(v, sv_b[:, 0:8], sv_a[:, 0:8])
                vTS(v, s1v[:, 0:8], sv_b[:, 0:8], 127.0, None, ALU.mult)
                vTS(v, sc1v[:, 0:8], sv_a[:, 0:8], msc_sc[:, 3:4], 1.0 / 127.0,
                    ALU.mult, ALU.mult, inc="v")
            sec("vector", v_s2)

            def t_tps2(t):
                wt(t, "tensor", "v")
                t.transpose(psD[0:8, 0:128], s1v[:, 0:8], diagf[:])
                i = t.transpose(psD[0:8, 128:256], sc1v[:, 0:8], diagf[:])
                bump(i, "t")
            sec("tensor", t_tps2)

            def v_tps2(v):
                wt(v, "vector", "t")
                vCP(v, tposed[0:8, 0:256], psD[0:8, 0:256], inc="v")
            sec("vector", v_tps2)

            def g_oht2(g):
                wt(g, "gpsimd", "v", "t")
                i = g.iota(OHT[:], [[-1, 16], [0, 128]], base=0, channel_multiplier=1,
                           allow_small_or_imprecise_dtypes=True)
                bump(i, "g")
                g.drain()
            sec("gpsimd", g_oht2)

            def v_oht2(v):
                wt(v, "vector", "g")
                vTS(v, OHT[:], OHT[:], 0.0, None, ALU.is_equal, inc="v")
            sec("vector", v_oht2)

            for grp in range(2):
                def t_rp2(t, grp=grp):
                    wt(t, "tensor", "v")
                    for k in range(4):
                        tt = grp * 4 + k
                        t.matmul(psC[:, k * 128:(k + 1) * 128],
                                 OHT[:, tt * 128:(tt + 1) * 128], tposed[0:16, 0:128],
                                 start=True, stop=True)
                    i = None
                    for k in range(4):
                        tt = grp * 4 + k
                        i = t.matmul(psD[:, k * 128:(k + 1) * 128],
                                     OHT[:, tt * 128:(tt + 1) * 128], tposed[0:16, 128:256],
                                     start=True, stop=True)
                    bump(i, "t")
                sec("tensor", t_rp2)

                def v_rp2(v, grp=grp):
                    wt(v, "vector", "t")
                    vCP(v, s2rep[:, grp * 512:(grp + 1) * 512], psC[:, 0:512])
                    vCP(v, sc2rep[:, grp * 512:(grp + 1) * 512], psD[:, 0:512], inc="v")
                sec("vector", v_rp2)
            rep2_v = CNT["v"]

            for q in range(2):
                ts_g = hf * 2 + q

                def v_s2d(v, q=q, rep2_v=rep2_v):
                    wtv(v, "vector", "v", rep2_v)
                    qsl = slice(q * 512, (q + 1) * 512)
                    vCP(v, s2d[:, 0:512], s2rep[:, qsl])
                    vCP(v, s2d[:, 512:1024], s2rep[:, qsl])
                    vCP(v, sc2d[:, 0:512], sc2rep[:, qsl])
                    vCP(v, sc2d[:, 512:1024], sc2rep[:, qsl], inc="v")
                sec("vector", v_s2d)

                TSNAP = []
                for fcp in range(16):

                    def v_xq2(v, q=q, fcp=fcp, TSNAP=TSNAP):
                        if fcp < 2:
                            wt(v, "vector", "t")
                        else:
                            wtv(v, "vector", "t", TSNAP[2 * fcp - 3])
                        base = (fcp % 2) * 1024
                        ladder_round(v, pq5, aq5, rq5, uq5,
                                     silu_sb[:, q * 16384 + fcp * 1024:
                                             q * 16384 + (fcp + 1) * 1024],
                                     s2d[:], xq2_sc[:, base: base + 1024],
                                     sc2d[:], inc="v")
                    sec("vector", v_xq2)

                    for hx in range(2):
                        fc = fcp * 2 + hx
                        par5 = fc % 2

                        def hw_w2s(sy, fc=fc, par5=par5, TSNAP=TSNAP):
                            if fc >= 2:
                                wtv(sy, "sync", "t", TSNAP[fc - 2])
                            else:
                                wt(sy, "sync", "t")
                            i = sy.dma_start(out=w2t_sb[:, par5 * 1024:(par5 + 1) * 1024],
                                             in_=w2q_scr[:, fc * 1024:(fc + 1) * 1024])
                            bump(i, "h50" if par5 == 0 else "h51", 16)
                        sec("sync", hw_w2s)

                        def t_w2(t, fc=fc, fcp=fcp, par5=par5, TSNAP=TSNAP):
                            wt(t, "tensor", "v", "h50" if par5 == 0 else "h51")
                            xbase = (fcp % 2) * 1024 + (fc % 2) * 512
                            i = None
                            for ccx in range(8):
                                i = t.matmul(psBANKS[ccx][:, 0:512],
                                             w2t_sb[:, par5 * 1024 + ccx * 128:
                                                    par5 * 1024 + (ccx + 1) * 128],
                                             xq2_sc[:, xbase: xbase + 512],
                                             start=(fc == 0), stop=(fc == 31))
                            bump(i, "t")
                            TSNAP.append(CNT["t"])
                        sec("tensor", t_w2)

                for ccx in range(8):
                    parO = ccx % 2

                    def v_out(v, ccx=ccx, ts_g=ts_g, parO=parO):
                        wt(v, "vector", "t")
                        if ccx >= 2 or ts_g > 0:
                            wt(v, "vector", "oO0" if parO == 0 else "oO1")
                        ol = slice(ccx * 2048 + ts_g * 512, ccx * 2048 + (ts_g + 1) * 512)
                        vSTT(v, outb[:, parO * 512:(parO + 1) * 512],
                             psBANKS[ccx][:, 0:512], col_gb(ccx), x2_sb[:, ol],
                             ALU.mult, ALU.add, inc="v")
                    sec("vector", v_out)

                    def hw_out(sy, ccx=ccx, ts_g=ts_g, parO=parO):
                        wt(sy, "sync", "v")
                        i = sy.dma_start(out=out[ccx * 128:(ccx + 1) * 128,
                                                 ts_g * 512:(ts_g + 1) * 512],
                                         in_=outb[:, parO * 512:(parO + 1) * 512])
                        bump(i, "oO0" if parO == 0 else "oO1", 16)
                    sec("sync", hw_out)

        def hw_fin(sy):
            sy.wait_ge(SEM["oO0"], CNT["oO0"])
            sy.wait_ge(SEM["oO1"], CNT["oO1"])
        sec("sync", hw_fin)

    return nc


_CACHE = {}


def _build_quiet():
    import os, sys
    sys.stdout.flush(); sys.stderr.flush()
    saved = (os.dup(1), os.dup(2))
    devnull = os.open(os.devnull, os.O_WRONLY)
    try:
        os.dup2(devnull, 1)
        os.dup2(devnull, 2)
        return build_nc()
    finally:
        os.dup2(saved[0], 1)
        os.dup2(saved[1], 2)
        os.close(saved[0]); os.close(saved[1]); os.close(devnull)


def kernel(**inputs):
    key = "nc"
    if key not in _CACHE:
        _CACHE[key] = _build_quiet()
    nc = _CACHE[key]

    f32 = np.float32

    def ct(a):
        return np.ascontiguousarray(a, dtype=f32)

    x = ct(inputs["L0_attn_x"])
    Wq = ct(inputs["L0_attn_Wq"])
    Wk = ct(inputs["L0_attn_Wk"])
    Wv = ct(inputs["L0_attn_Wv"])
    Wm1 = ct(inputs["L0_attn_Wm1"])
    bm1 = ct(inputs["L0_attn_bm1"])
    Wm2 = ct(inputs["L0_attn_Wm2"])
    bm2 = ct(inputs["L0_attn_bm2"])
    Wproj = ct(inputs["L0_attn_Wproj"])
    Wfc = ct(inputs["L0_attn_Wfc"])
    Wcp = ct(inputs["L0_attn_Wcp"])
    w1 = ct(inputs["L0_ffn_w1"])
    conv_w = ct(inputs["L0_ffn_conv_w"])
    conv_b = ct(inputs["L0_ffn_conv_b"])
    w2 = ct(inputs["L0_ffn_w2"])
    cvec = ct(inputs["L0_c"])
    wmod = ct(inputs["L0_wmod"])

    com = {
        "wqT": np.ascontiguousarray(Wq.T),
        "wkT": np.ascontiguousarray(Wk.T),
        "wvT": np.ascontiguousarray(Wv.T),
        "wpjT": np.ascontiguousarray(Wproj.T),
        "wm1T": np.ascontiguousarray(Wm1.T),
        "wm2T": np.ascontiguousarray(Wm2.T),
        "bm1c": np.ascontiguousarray(bm1.reshape(2, 128).T),
        "bm2c": np.ascontiguousarray(bm2.reshape(16, 1)),
        "wfcT": np.ascontiguousarray(Wfc.T),
        "wcpT": np.ascontiguousarray(Wcp.T),
        "w1T": np.ascontiguousarray(w1.T),
        "w2T": np.ascontiguousarray(w2.T),
        "cwc": np.ascontiguousarray(
            conv_w.reshape(32, 128, 3).transpose(1, 0, 2).reshape(128, 96)),
        "cbc": np.ascontiguousarray(conv_b.reshape(32, 128).T),
        "wmod": wmod,
    }
    in_maps = []
    for b in range(8):
        m = dict(com)
        m["xT"] = np.ascontiguousarray(x[b].T)
        m["c_col"] = np.ascontiguousarray(cvec[b].reshape(8, 128).T)
        in_maps.append(m)

    res = run_bass_kernel_spmd(nc, in_maps, core_ids=list(range(8)))
    outs = [np.ascontiguousarray(np.asarray(r["out"]).T) for r in res.results]
    return np.stack(outs, axis=0)
